# version 03
# baseline (speedup 1.0000x reference)
import math
import numpy as np
import concourse.bass as bass
import concourse.mybir as mybir
from concourse.bass_utils import run_bass_kernel_spmd

F32 = mybir.dt.float32
BF16 = mybir.dt.bfloat16
I32 = mybir.dt.int32
U8 = mybir.dt.uint8
AF = mybir.ActivationFunctionType
ALU = mybir.AluOpType
AX = mybir.AxisListType.X

T = 2048
D = 1024
NT = T // 128
ND = D // 128
TWO_PI = float(2.0 * math.pi)
PI = float(math.pi)

N_LAYERS = 4
DEBUG_TAPS = ()

ENGINES = ("sync", "scalar", "vector", "tensor", "gpsimd")



class Task:
    __slots__ = ("eng", "kind", "emit", "waits", "tag", "count_after")


class BK:
    def __init__(self, nc):
        self.nc = nc
        self.tasks = {e: [] for e in ENGINES}
        self.count = {e: 0 for e in ENGINES}
        self.writer = {}
        self.readers = {}
        self.waited = {e: {} for e in ENGINES}
        self._names = {}
        self.taps = {}

    def name(self, base):
        i = self._names.get(base, 0)
        self._names[base] = i + 1
        return f"{base}_{i}" if i else base

    def sb(self, base, shape, dtype):
        return self.nc.alloc_sbuf_tensor(self.name(base), list(shape), dtype).ap()

    def psum(self, base, shape, dtype=F32):
        return self.nc.alloc_psum_tensor(self.name(base), list(shape), dtype).ap()

    def t(self, eng, emit, reads=(), writes=(), kind="c", tag=""):
        waits = {}

        def need(e2, c2, dep_is_dma=False):
            if (e2 != eng or dep_is_dma) and c2 > self.waited[eng].get(e2, 0):
                waits[e2] = max(waits.get(e2, 0), c2)

        for r in reads:
            w = self.writer.get(r)
            if w is not None:
                need(*w)
        for r in writes:
            w = self.writer.get(r)
            if w is not None:
                need(*w)
            for (e2, d2), c2 in self.readers.get(r, {}).items():
                need(e2, c2, d2)
        for e2, c2 in waits.items():
            self.waited[eng][e2] = c2
        tk = Task()
        tk.eng, tk.emit, tk.waits, tk.tag = eng, emit, waits, tag
        if kind == "c":
            tk.kind = "c"
            self.count[eng] += 1
        else:
            n = int(kind.split(":")[1]) if ":" in kind else 1
            tk.kind = "d"
            self.count[eng] += 16 * n
        tk.count_after = self.count[eng]
        self.tasks[eng].append(tk)
        cnt = self.count[eng]
        is_dma = tk.kind == "d"
        for r in writes:
            self.writer[r] = (eng, cnt, is_dma)
            self.readers.pop(r, None)
        for r in reads:
            self.readers.setdefault(r, {})[(eng, is_dma)] = cnt
        return tk

    def dma(self, eng, pairs, reads=(), writes=(), tag=""):
        pairs = list(pairs)

        def emit(e, pairs=pairs):
            return [e.dma_start(out=o, in_=i) for o, i in pairs]

        return self.t(eng, emit, reads=reads, writes=writes, kind=f"d:{len(pairs)}", tag=tag)

    def tap(self, nm, ap, reads):
        if nm not in DEBUG_TAPS or nm in self.taps:
            return
        out = self.nc.dram_tensor("tap_" + nm, list(ap.shape), ap.dtype, kind="ExternalOutput").ap()
        self.taps[nm] = out
        self.dma("gpsimd", [(out, ap)], reads=reads, tag=f"tap_{nm}")

    def check_deadlock(self):
        idx = {e: 0 for e in ENGINES}
        done = {e: 0 for e in ENGINES}
        total = sum(len(v) for v in self.tasks.values())
        ndone = 0
        while ndone < total:
            progress = False
            for e in ENGINES:
                while idx[e] < len(self.tasks[e]):
                    tk = self.tasks[e][idx[e]]
                    if all(done[e2] >= c2 for e2, c2 in tk.waits.items()):
                        done[e] = tk.count_after
                        idx[e] += 1
                        ndone += 1
                        progress = True
                    else:
                        break
            if not progress:
                lines = []
                for e in ENGINES:
                    if idx[e] < len(self.tasks[e]):
                        tk = self.tasks[e][idx[e]]
                        miss = {e2: (c2, done[e2]) for e2, c2 in tk.waits.items() if done[e2] < c2}
                        lines.append(f"{e}[{idx[e]}] tag={tk.tag} needs={miss}")
                raise RuntimeError("task-graph deadlock:\n" + "\n".join(lines))

    def finalize(self):
        self.check_deadlock()
        nc = self.nc
        sems = {e: nc.alloc_semaphore(f"sem_{e}") for e in ENGINES}

        _PASS = {"dma_start", "wait_ge", "drain", "nop", "engine_nop",
                 "dma_start_transpose", "memzero"}

        class Shim:

            def __init__(self, e):
                self._e = e

            def __getattr__(self, nm):
                fn = getattr(self._e, nm)
                if nm.startswith("_") or nm in _PASS or not callable(fn):
                    return fn

                def w(*a, **kw):
                    inst = fn(*a, **kw)
                    small = bool(kw.get("accum_out") is not None) if "accum_out" in kw else False
                    if not small:
                        try:
                            small = a[0].free_size() < 256
                        except Exception:
                            small = True
                    if small:
                        self._e.drain()
                    return inst
                return w

        with nc.Block() as block:
            def mk(eng_name):
                def body(eng):
                    if eng_name in ("vector", "scalar"):
                        eng = Shim(eng)
                    for tk in self.tasks[eng_name]:
                        for e2, c2 in sorted(tk.waits.items()):
                            eng.wait_ge(sems[e2], c2)
                        try:
                            insts = tk.emit(eng)
                        except Exception as ex:
                            raise RuntimeError(f"emit failed in task {tk.tag}: {ex}") from ex
                        if tk.kind == "d":
                            for inst in insts:
                                inst.then_inc(sems[eng_name], 16)
                        else:
                            if isinstance(insts, (list, tuple)):
                                insts = insts[-1]
                            assert insts is not None, f"task {tk.tag} returned None"
                            insts.then_inc(sems[eng_name], 1)
                return body
            for e in ENGINES:
                getattr(block, e)(mk(e))


class Arena:
    def __init__(self, bk, name, bytes_per_part, block=8192):
        self.bk = bk
        self.nm = name
        self.block = block
        self.tsr = bk.nc.alloc_sbuf_tensor(name, [128, bytes_per_part], U8)
        self.nbytes = bytes_per_part

    def view(self, off, shape, dtype):
        esz = mybir.dt.size(dtype)
        n = 1
        for s_ in shape[1:]:
            n *= s_
        nb = n * esz
        assert off % 32 == 0 and off + nb <= self.nbytes, (off, nb, self.nbytes)
        ap = self.tsr.ap()[:, off:off + nb].bitcast(dtype)
        if len(shape) > 2:
            dims = " ".join(f"s{i}" for i in range(1, len(shape)))
            ap = ap.rearrange(f"p ({dims}) -> p {dims}",
                              **{f"s{i}": shape[i] for i in range(1, len(shape))})
        keys = tuple(f"{self.nm}:{b}" for b in range(off // self.block,
                                                     (off + nb + self.block - 1) // self.block))
        return ap, keys



class Model:
    def __init__(self, bk: BK):
        self.bk = bk
        self.nc = bk.nc
        self.din = {}
        self.dscr = {}

    def dram_in(self, nm, shape, dtype=F32):
        ap = self.nc.dram_tensor(nm, list(shape), dtype, kind="ExternalInput").ap()
        self.din[nm] = ap
        return ap

    def dram_scratch(self, nm, shape, dtype=F32):
        kind = "ExternalOutput" if nm in DEBUG_TAPS else "Internal"
        ap = self.nc.dram_tensor(nm, list(shape), dtype, kind=kind).ap()
        self.dscr[nm] = ap
        return ap

    def declare_inputs(self):
        d = self.dram_in
        d("x", [T, D])
        d("pos_phases", [16384, D])
        for w in ("Wpv", "Wk", "Wkv", "Wo0"):
            d(w, [D, D])
        d("Wg1", [2 * D, D])
        d("Wb1", [D, 512])
        d("Wg2T", [1, D])
        d("Wb2T", [2, 512])
        for b, n in (("bpv", D), ("bk0", D), ("bkv", D), ("bg1", D), ("bg2", 1),
                     ("bb1", 512), ("bb2", 2), ("ln_g", D), ("ln_b", D), ("bo0", D),
                     ("f0b1", 4096), ("f0b2", D)):
            d(b, [1, n])
        d("f0w1T", [D, 4096])
        d("f0w2T", [4096, D])
        for w in ("Wq1T", "Wk1T", "Wv1T", "Wo1T"):
            d(w, [D, D])
        for b in ("bq1C", "bk1C"):
            d(b, [128, ND])
        d("bv1R", [1, D])
        d("bo1", [1, D])
        d("f1Win", [D, 64])
        d("f1Wout", [64, D])
        for b, n in (("f1bin", 64), ("f1W1", 1024), ("f1b1", 1024), ("f1W2", 1024),
                     ("f1b2", 64), ("f1bout", D)):
            d(b, [1, n])
        d("W2inT", [D, 5888])
        d("convw", [128, 16, 4])
        d("convb", [128, 16])
        d("W2xT", [2048, 96])
        d("dtlnw", [1, 64])
        d("Blnw", [1, 16])
        d("Clnw", [1, 16])
        d("W2dtT", [64, 2048])
        d("bdt", [128, 16])
        d("AlogC", [128, 16, 16])
        d("DparC", [128, 16])
        d("ln1wC", [128, 16])
        d("ln2wC", [128, 16])
        d("W2outT", [2048, D])
        d("w2gate", [2816, D])
        d("w2up", [2816, D])
        d("w2down", [D, 2816])
        d("cmask", [4, 128, 512])
        d("w3kvd", [D, 128])
        d("w3kvu", [128, 2048])
        d("w3q", [D, D])
        d("w3out", [D, D])
        d("w3g", [D, 8])
        d("w3W1", [8, D, 2048])
        d("w3W2", [8, 2048, D])
        d("w3b1C", [128, 8, 16])
        d("w3b2", [8, D])
        d("cmaskT", [4, 128, 512])
        d("ident", [128, 128])
        d("tri", [128, 128])
        d("onesrow", [1, 128])

    def v(self, emit, reads=(), writes=(), tag=""):
        return self.bk.t("vector", emit, reads, writes, tag=tag)

    def s(self, emit, reads=(), writes=(), tag=""):
        return self.bk.t("scalar", emit, reads, writes, tag=tag)

    def mm(self, emit, reads=(), writes=(), tag=""):
        return self.bk.t("tensor", emit, reads, writes, tag=tag)

    def rep_at(self, off, nm, width, tag=None):
        ap, keys = self.arena.view(off, [128, width], F32)
        self.bk.dma("gpsimd", [(ap[:, :], self.din[nm].partition_broadcast(128).squeeze(1))],
                    writes=keys, tag=tag or f"rep_{nm}")
        return ap, keys

    def build(self, n_layers):
        bk = self.bk
        self.ident = bk.sb("identc", [128, 128], F32)
        self.tri = bk.sb("tric", [128, 128], F32)
        self.ones1 = bk.sb("ones1c", [1, 128], F32)
        bk.dma("sync", [(self.ident[:, :], self.din["ident"]),
                        (self.tri[:, :], self.din["tri"]),
                        (self.ones1[:, :], self.din["onesrow"])],
               writes=("consts",), tag="consts")
        self.onescol = bk.sb("onescol", [128, 1], F32)
        self.one1 = bk.sb("one1", [1, 1], F32)
        self.halfpi = bk.sb("halfpi", [128, 1], F32)

        def emit_c2(e):
            e.memset(self.onescol[:, :], 1.0)
            e.memset(self.one1[:, :], 1.0)
            return e.memset(self.halfpi[:, :], PI / 2)
        self.v(emit_c2, writes=("consts2",), tag="onescol")

        self.xTM = bk.sb("xTM", [128, NT, D], F32)
        x_r = self.din["x"].rearrange("(tc p) d -> p tc d", p=128)
        for h in range(4):
            bk.dma("sync", [(self.xTM[:, 4 * h:4 * h + 4, :], x_r[:, 4 * h:4 * h + 4, :])],
                   writes=tuple(f"xTM:{tc}" for tc in range(4 * h, 4 * h + 4)), tag=f"loadx{h}")

        self.negmu = bk.sb("negmu", [128, NT], F32)
        self.lninv = bk.sb("lninv", [128, NT], F32)
        self.st1 = bk.sb("st1", [128, NT], F32)
        self.st2 = bk.sb("st2", [128, NT], F32)
        self.sm = bk.sb("smallpool", [128, 64], F32)
        self.sqdump = bk.sb("sqdump", [128, D], F32)
        self.carries = [bk.sb(f"carry{i}", [1, D], F32) for i in range(4)]
        self.gcar = bk.sb("gcar", [1, 1], F32)
        self.pb = [bk.psum(f"pb{i}", [128, 512], F32) for i in range(8)]
        self.arena = Arena(bk, "arena", 122880)
        self.xnFM, self.xnFM_keys = self.arena.view(0, [128, ND, 2052], F32)

        if n_layers >= 1:
            self.layer0()
        if n_layers >= 2:
            self.layer1()
        if n_layers >= 3:
            self.layer2()
        if n_layers >= 4:
            self.layer3()

        out = self.nc.dram_tensor("out", [T, D], F32, kind="ExternalOutput").ap()
        out_r = out.rearrange("(tc p) d -> p tc d", p=128)
        bk.dma("gpsimd", [(out_r[:, 4 * h:4 * h + 4, :], self.xTM[:, 4 * h:4 * h + 4, :])
                          for h in range(4)],
               reads=tuple(f"xTM:{tc}" for tc in range(NT)), writes=("out_stored",),
               tag="store_out")
        self.v(lambda e: e.memset(self.sm[:, 63:64], 0.0), reads=("out_stored",),
               writes=("sm63",), tag="final_wait")

    def ln_stats(self, tag, eps=1e-5):
        for tc in range(NT):
            self.v(lambda e, tc=tc: e.reduce_sum(self.st1[:, tc:tc + 1], self.xTM[:, tc, :], axis=AX),
                   reads=(f"xTM:{tc}",), writes=(f"st1:{tc}",), tag=f"{tag}s1_{tc}")
            self.s(lambda e, tc=tc: e.activation(self.sqdump[:, :], self.xTM[:, tc, :], AF.Square,
                                                 accum_out=self.st2[:, tc:tc + 1]),
                   reads=(f"xTM:{tc}",), writes=(f"st2:{tc}", "sqdump"), tag=f"{tag}s2_{tc}")
        allr = tuple(f"st1:{tc}" for tc in range(NT)) + tuple(f"st2:{tc}" for tc in range(NT))

        def emit_small(e):
            e.tensor_scalar(self.negmu[:, :], self.st1[:, :], -1.0 / D, None, ALU.mult)
            e.tensor_tensor(self.st1[:, :], self.negmu[:, :], self.negmu[:, :], ALU.mult)
            e.tensor_scalar(self.st2[:, :], self.st2[:, :], 1.0 / D, None, ALU.mult)
            e.tensor_tensor(self.st2[:, :], self.st2[:, :], self.st1[:, :], ALU.subtract)
            e.tensor_scalar(self.st2[:, :], self.st2[:, :], float(eps), None, ALU.add)
            return e.reciprocal(self.st2[:, :], self.st2[:, :])
        self.v(emit_small, reads=allr, writes=("negmu",) + allr, tag=f"{tag}small")
        self.s(lambda e: e.activation(self.lninv[:, :], self.st2[:, :], AF.Sqrt),
               reads=allr, writes=("lninv",), tag=f"{tag}inv")

    def xn_tile(self, dst, dst_keys, tc, tag):
        self.v(lambda e: e.tensor_scalar(dst, self.xTM[:, tc, :],
                                         self.negmu[:, tc:tc + 1], self.lninv[:, tc:tc + 1],
                                         ALU.add, ALU.mult),
               reads=(f"xTM:{tc}", "negmu", "lninv"), writes=tuple(dst_keys), tag=tag)

    def transpose128(self, src, src_keys, dst, dst_keys, n_blocks, tag, cast=None):
        for g0 in range(0, n_blocks, 4):
            nb = min(4, n_blocks - g0)
            pbank = self.pb[4 + (g0 // 4) % 2]
            pk = f"pb{4 + (g0 // 4) % 2}"

            def emit_tr(e, g0=g0, nb=nb, pbank=pbank, src=src):
                return [e.transpose(pbank[:, j * 128:(j + 1) * 128],
                                    src[:, (g0 + j) * 128:(g0 + j + 1) * 128],
                                    self.ident[:, :]) for j in range(nb)]
            self.mm(emit_tr, reads=tuple(src_keys) + ("consts",), writes=(pk,), tag=f"{tag}tr{g0}")

            def emit_ev(e, g0=g0, nb=nb, pbank=pbank, dst=dst):
                return e.activation(dst[:, g0:g0 + nb, :],
                                    pbank[:, 0:nb * 128].rearrange("p (j c) -> p j c", c=128),
                                    AF.Copy)
            self.s(emit_ev, reads=(pk,), writes=tuple(dst_keys), tag=f"{tag}ev{g0}")

    def build_xnFM(self, tag):
        self.v(lambda e: e.memset(self.xnFM[:, :, 0:1], 0.0), writes=self.xnFM_keys,
               tag=f"{tag}pad")
        av, ak = self.arena.view(65664, [128, D], F32)
        for tc in range(NT):
            self.xn_tile(av[:, :], ak, tc, f"{tag}xn{tc}")
            fmv = self.xnFM[:, :, 1 + tc * 128:1 + tc * 128 + 128]
            self.transpose128(av, ak, fmv, self.xnFM_keys, ND, f"{tag}t{tc}")

    def layer0(self):
        bk = self.bk
        self.ln_stats("l0ln")
        self.build_xnFM("l0fm")
        bk.tap("l0_xnfm", self.xnFM[:, :, :], reads=self.xnFM_keys)

        pvD = self.dram_scratch("pvD", [T, D])
        kvD = self.dram_scratch("kvD", [T, D])
        thD = self.dram_scratch("thD", [T, D])
        lnD = self.dram_scratch("lnD", [T, D])
        sD = self.dram_scratch("sD", [T, D])
        hD = self.dram_scratch("hD", [T, 4096])
        w1qD = self.dram_scratch("w1qD", [D, 4096], BF16)
        w2qD = self.dram_scratch("w2qD", [4096, D], BF16)

        self.quant_ternary("f0w1T", w1qD, D, 4096, "dqw1")
        self.quant_ternary("f0w2T", w2qD, 4096, D, "dqw2")

        self.proj_spill("Wpv", D, pvD, "bpv", 110592, None, "l0pv")
        self.proj_spill("Wkv", D, kvD, "bkv", 114688, None, "l0kv")
        self.proj_spill("Wk", D, thD, "bk0", 118784, AF.Tanh, "l0th")

        self.vg = bk.sb("vg_l0", [128, NT], F32)
        self.bw0p = bk.sb("bw0p", [128, NT], F32)
        self.bw1p = bk.sb("bw1p", [128, NT], F32)
        self.l0_gates()
        self.l0_blendw()
        self.l0_gatecum()
        self.l0_chains(pvD, kvD, thD, lnD)
        self.l0_wo(lnD, sD)
        self.l0_ffn(w1qD, w2qD, hD, sD)

    def quant_ternary(self, wnm, wqD, K, N, dqkey):
        bk = self.bk
        nk = K // 128
        base = 69632
        wv, wk = self.arena.view(base, [128, N], F32)
        qi, qik = self.arena.view(base + 16384, [128, N], I32)
        qb, qbk = self.arena.view(base + 32768, [128, N], BF16)
        parts = self.sm[:, 32:33]
        self.v(lambda e: e.memset(parts, 0.0), writes=("sm32",), tag=f"{dqkey}z")
        wsrc = self.din[wnm].rearrange("(c p) n -> c p n", p=128)
        for c in range(nk):
            bk.dma("sync", [(wv[:, :], wsrc[c, :, :])], writes=wk, tag=f"{dqkey}ld{c}")

            def emit_abs(e):
                e.tensor_reduce(self.sm[:, 33:34], wv[:, :], AX, ALU.add, apply_absolute_value=True)
                return e.tensor_tensor(parts, parts, self.sm[:, 33:34], ALU.add)
            self.v(emit_abs, reads=wk, writes=("sm32", "sm33"), tag=f"{dqkey}abs{c}")
        pbank = self.pb[4]
        self.mm(lambda e: e.matmul(pbank[0:1, 0:1], parts, self.onescol[:, :], start=True, stop=True),
                reads=("sm32", "consts2"), writes=("pb4",), tag=f"{dqkey}tot")
        dqrow = self.sm[0:1, 34:35]

        def emit_mean(e):
            e.tensor_scalar(dqrow, pbank[0:1, 0:1], 1.0 / (K * N), None, ALU.mult)
            return e.tensor_scalar(dqrow, dqrow, 1e-5, None, ALU.max)
        self.v(emit_mean, reads=("pb4",), writes=("sm34",), tag=f"{dqkey}mean")
        dq = bk.sb(f"dq_{dqkey}", [128, 2], F32)
        dqd = self.dram_scratch(f"dqd_{dqkey}", [1, 1])
        bk.dma("gpsimd", [(dqd, dqrow)], reads=("sm34",), writes=(f"dqd_{dqkey}",),
               tag=f"{dqkey}spill")
        bk.dma("gpsimd", [(dq[:, 0:1], dqd.partition_broadcast(128).squeeze(1))],
               reads=(f"dqd_{dqkey}",), writes=(f"dq_{dqkey}",), tag=f"{dqkey}rep")
        self.v(lambda e: e.reciprocal(dq[:, 1:2], dq[:, 0:1]),
               reads=(f"dq_{dqkey}",), writes=(f"dq_{dqkey}",), tag=f"{dqkey}rec")
        setattr(self, dqkey, dq)
        wqdst = wqD.rearrange("(c p) n -> c p n", p=128)
        for c in range(nk):
            bk.dma("sync", [(wv[:, :], wsrc[c, :, :])], writes=wk, tag=f"{dqkey}ld2_{c}")

            def emit_q(e):
                e.tensor_scalar(wv[:, :], wv[:, :], dq[:, 1:2], None, ALU.mult)
                e.tensor_copy(qi[:, :], wv[:, :])
                e.tensor_copy(wv[:, :], qi[:, :])
                e.tensor_scalar(wv[:, :], wv[:, :], -1.0, 1.0, ALU.max, ALU.min)
                return e.tensor_copy(qb[:, :], wv[:, :])
            self.v(emit_q, reads=wk + (f"dq_{dqkey}",), writes=wk + qik + qbk, tag=f"{dqkey}q{c}")
            bk.dma("gpsimd", [(wqdst[c, :, :], qb[:, :])], reads=qbk,
                   writes=(f"dram_{dqkey}:{c}",), tag=f"{dqkey}st{c}")

    def proj_spill(self, wnm, outw, outD, bias_nm, bias_off, act, tag):
        bk = self.bk
        wbuf, wkeys = self.arena.view(69632, [128, 2, ND, 512], F32)
        h1 = len(wkeys) // 2
        wbk = [wkeys[:h1], wkeys[h1:]]
        stg = [self.arena.view(102400, [128, outw], F32),
               self.arena.view(106496, [128, outw], F32)]
        bias, bias_k = self.rep_at(bias_off, bias_nm, outw, tag=f"{tag}bias")
        wsrc = self.din[wnm]
        outr = outD.rearrange("(tc p) n -> tc p n", p=128)
        nE = outw // 512
        for eb in range(nE):
            src = wsrc[:, eb * 512:eb * 512 + 512].rearrange("(c p) n -> p c n", p=128)
            bk.dma("sync", [(wbuf[:, eb, :, :], src)], writes=wbk[eb], tag=f"{tag}w{eb}")
        for tc in range(NT):
            sb_, sbk = stg[tc % 2]
            for eb in range(nE):
                pbank = self.pb[(2 * (tc % 2) + eb) % 4]
                pk = f"pb{(2 * (tc % 2) + eb) % 4}"

                def emit_mm(e, tc=tc, eb=eb, pbank=pbank):
                    return [e.matmul(pbank[:, :], self.xnFM[:, dc, 1 + tc * 128:1 + tc * 128 + 128],
                                     wbuf[:, eb, dc, :], start=(dc == 0), stop=(dc == ND - 1))
                            for dc in range(ND)]
                self.mm(emit_mm, reads=wbk[eb] + self.xnFM_keys, writes=(pk,),
                        tag=f"{tag}mm{eb}_{tc}")

                def emit_ev(e, eb=eb, pbank=pbank, sb_=sb_):
                    return e.scalar_tensor_tensor(sb_[:, eb * 512:eb * 512 + 512], pbank[:, :], 1.0,
                                                  bias[:, eb * 512:eb * 512 + 512],
                                                  ALU.mult, ALU.add)
                self.v(emit_ev, reads=(pk,) + bias_k, writes=sbk, tag=f"{tag}ev{eb}_{tc}")
            if act is not None:
                def emit_act(e, sb_=sb_):
                    return e.activation(sb_[:, :], sb_[:, :], act)
                self.s(emit_act, reads=sbk, writes=sbk, tag=f"{tag}act{tc}")
            bk.dma("gpsimd", [(outr[tc, :, :], sb_[:, :])], reads=sbk,
                   writes=(f"{tag}D:{tc}",), tag=f"{tag}st{tc}")

    def l0_gates(self):
        bk = self.bk
        wbuf, wkeys = self.arena.view(69632, [128, 2 * ND, 512], F32)
        gh, ghk = self.arena.view(102400, [128, 512], F32)
        gh2, gh2k = self.arena.view(104448, [128, 512], F32)
        tmp, tmpk = self.arena.view(106496, [128, 512], F32)
        bg1, bg1k = self.rep_at(110592, "bg1", D, tag="repbg1")
        wg2, wg2k = self.rep_at(114688, "Wg2T", D, tag="repwg2")
        bg2, bg2k = self.rep_at(118784, "bg2", 1, tag="repbg2")
        wsrc = self.din["Wg1"]
        for eo in range(2):
            src = wsrc[:, eo * 512:eo * 512 + 512].rearrange("(c p) n -> p c n", p=128)
            bk.dma("sync", [(wbuf[:, :, :], src)], writes=wkeys, tag=f"l0g_w{eo}")
            for tc in range(NT):
                pbank = self.pb[tc % 4]
                pk = f"pb{tc % 4}"

                def emit_mm(e, tc=tc, pbank=pbank):
                    insts = []
                    col = 1 + tc * 128
                    for dc in range(ND):
                        insts.append(e.matmul(pbank[:, :], self.xnFM[:, dc, col:col + 128],
                                              wbuf[:, dc, :], start=(dc == 0), stop=False))
                    for dc in range(ND):
                        insts.append(e.matmul(pbank[:, :], self.xnFM[:, dc, col - 1:col + 127],
                                              wbuf[:, ND + dc, :], start=False, stop=(dc == ND - 1)))
                    return insts
                self.mm(emit_mm, reads=wkeys + self.xnFM_keys, writes=(pk,), tag=f"l0g_mm{eo}_{tc}")

                def emit_ev(e, eo=eo, pbank=pbank):
                    return e.scalar_tensor_tensor(gh[:, :], pbank[:, :], 1.0,
                                                  bg1[:, eo * 512:eo * 512 + 512], ALU.mult, ALU.add)
                self.v(emit_ev, reads=(pk,) + bg1k, writes=ghk, tag=f"l0g_ev{eo}_{tc}")
                self.s(lambda e: e.activation(gh2[:, :], gh[:, :], AF.Gelu),
                       reads=ghk, writes=gh2k, tag=f"l0g_gelu{eo}_{tc}")

                def emit_dot(e, tc=tc, eo=eo):
                    e.tensor_tensor(tmp[:, :], gh2[:, :], wg2[:, eo * 512:eo * 512 + 512], ALU.mult)
                    e.reduce_sum(self.sm[:, 35:36], tmp[:, :], axis=AX)
                    if eo == 0:
                        return e.tensor_copy(self.vg[:, tc:tc + 1], self.sm[:, 35:36])
                    return e.tensor_tensor(self.vg[:, tc:tc + 1], self.vg[:, tc:tc + 1],
                                           self.sm[:, 35:36], ALU.add)
                self.v(emit_dot, reads=gh2k + wg2k, writes=("vg", "sm35") + tmpk,
                       tag=f"l0g_dot{eo}_{tc}")

        def emit_fin(e):
            return e.tensor_scalar(self.vg[:, :], self.vg[:, :], bg2[:, 0:1], None, ALU.add)
        self.v(emit_fin, reads=("vg",) + bg2k, writes=("vg",), tag="l0g_bias")
        self.s(lambda e: e.activation(self.vg[:, :], self.vg[:, :], AF.Sigmoid),
               reads=("vg",), writes=("vg",), tag="l0g_sig")

    def l0_blendw(self):
        bk = self.bk
        wbuf, wkeys = self.arena.view(69632, [128, ND, 512], F32)
        bh, bhk = self.arena.view(86016, [128, 512], F32)
        bh2, bh2k = self.arena.view(88064, [128, 512], F32)
        tmp, tmpk = self.arena.view(90112, [128, 512], F32)
        bb1, bb1k = self.rep_at(92160, "bb1", 512, tag="repbb1")
        bb2, bb2k = self.rep_at(94208, "bb2", 2, tag="repbb2")
        wb2, wb2k = self.arena.view(96256, [128, 2, 512], F32)
        bk.dma("gpsimd",
               [(wb2[:, 0, :], self.din["Wb2T"][0:1, :].partition_broadcast(128).squeeze(1)),
                (wb2[:, 1, :], self.din["Wb2T"][1:2, :].partition_broadcast(128).squeeze(1))],
               writes=wb2k, tag="repwb2")
        src = self.din["Wb1"].rearrange("(c p) n -> p c n", p=128)
        bk.dma("sync", [(wbuf[:, :, :], src)], writes=wkeys, tag="l0b_w")
        z0 = self.sm[:, 36:37]
        z1 = self.sm[:, 37:38]
        for tc in range(NT):
            pbank = self.pb[tc % 4]
            pk = f"pb{tc % 4}"

            def emit_mm(e, tc=tc, pbank=pbank):
                col = 1 + tc * 128
                return [e.matmul(pbank[:, :], self.xnFM[:, dc, col:col + 128],
                                 wbuf[:, dc, :], start=(dc == 0), stop=(dc == ND - 1))
                        for dc in range(ND)]
            self.mm(emit_mm, reads=wkeys + self.xnFM_keys, writes=(pk,), tag=f"l0b_mm{tc}")

            def emit_ev(e, pbank=pbank):
                return e.scalar_tensor_tensor(bh[:, :], pbank[:, :], 1.0, bb1[:, :],
                                              ALU.mult, ALU.add)
            self.v(emit_ev, reads=(pk,) + bb1k, writes=bhk, tag=f"l0b_ev{tc}")
            self.s(lambda e: e.activation(bh2[:, :], bh[:, :], AF.Gelu),
                   reads=bhk, writes=bh2k, tag=f"l0b_gelu{tc}")

            def emit_dots(e, tc=tc):
                e.tensor_tensor(tmp[:, :], bh2[:, :], wb2[:, 0, :], ALU.mult)
                e.reduce_sum(z0, tmp[:, :], axis=AX)
                e.tensor_tensor(tmp[:, :], bh2[:, :], wb2[:, 1, :], ALU.mult)
                e.reduce_sum(z1, tmp[:, :], axis=AX)
                e.tensor_tensor(z0, z0, z1, ALU.subtract)
                e.tensor_tensor(z1, bb2[:, 0:1], bb2[:, 1:2], ALU.subtract)
                return e.tensor_tensor(self.bw0p[:, tc:tc + 1], z0, z1, ALU.add)
            self.v(emit_dots, reads=bh2k + wb2k + bb2k,
                   writes=("bw0p", "sm36", "sm37") + tmpk, tag=f"l0b_dot{tc}")
        self.s(lambda e: e.activation(self.bw1p[:, :], self.bw0p[:, :], AF.Sigmoid, scale=-1.0),
               reads=("bw0p",), writes=("bw1p",), tag="l0b_sig1")
        self.s(lambda e: e.activation(self.bw0p[:, :], self.bw0p[:, :], AF.Sigmoid),
               reads=("bw0p",), writes=("bw0p",), tag="l0b_sig0")
        self.v(lambda e: e.tensor_scalar(self.bw0p[:, :], self.bw0p[:, :], 1.0 / 32.0, None, ALU.mult),
               reads=("bw0p",), writes=("bw0p",), tag="l0b_sc0")

    def l0_gatecum(self):
        self.v(lambda e: e.memset(self.gcar[:, :], 0.0), writes=("gcar",), tag="l0gc_z")
        pbank = self.pb[6]
        pcar = self.pb[7]
        for tc in range(NT):
            def emit_mm(e, tc=tc):
                e.matmul(pbank[:, 0:1], self.tri[:, :], self.vg[:, tc:tc + 1], start=True, stop=False)
                e.matmul(pbank[:, 0:1], self.ones1[:, :], self.gcar[:, :],
                         start=False, stop=True)
                e.matmul(pcar[0:1, 0:1], self.onescol[:, :], self.vg[:, tc:tc + 1],
                         start=True, stop=False)
                return e.matmul(pcar[0:1, 0:1], self.one1[:, :], self.gcar[:, :],
                                start=False, stop=True)
            self.mm(emit_mm, reads=("vg", "gcar", "consts", "consts2"),
                    writes=("pb6", "pb7"), tag=f"l0gc_mm{tc}")

            def emit_ex(e, tc=tc):
                e.tensor_copy(self.gcar[:, :], pcar[0:1, 0:1])
                return e.tensor_scalar(self.sm[:, 39 + tc:40 + tc], pbank[:, 0:1], 1.0, None, ALU.max)
            self.v(emit_ex, reads=("pb6", "pb7"), writes=("gcar", f"gcm:{tc}"), tag=f"l0gc_ex{tc}")
        allg = tuple(f"gcm:{tc}" for tc in range(NT))
        self.s(lambda e: e.activation(self.sm[:, 39:39 + NT], self.sm[:, 39:39 + NT], AF.Sqrt),
               reads=allg, writes=allg, tag="l0gc_sqrt")
        self.v(lambda e: e.reciprocal(self.sm[:, 39:39 + NT], self.sm[:, 39:39 + NT]),
               reads=allg, writes=allg, tag="l0gc_rec")

        def emit_bw1(e):
            e.tensor_tensor(self.bw1p[:, :], self.bw1p[:, :], self.sm[:, 39:39 + NT], ALU.mult)
            return e.tensor_scalar(self.bw1p[:, :], self.bw1p[:, :], 1.0 / 32.0, None, ALU.mult)
        self.v(emit_bw1, reads=allg + ("bw1p",), writes=("bw1p",), tag="l0gc_bw1")

    def carry_ap(self, ci):
        return self.carries[ci][0:1, :]

    def cumsum_pair(self, za, zak, zb, zbk, ci_a, ci_b, tag):
        def emit_mm(e, ci_a=ci_a, ci_b=ci_b):
            insts = []
            for h in range(2):
                ca = self.carry_ap(ci_a)[:, h * 512:(h + 1) * 512]
                cb = self.carry_ap(ci_b)[:, h * 512:(h + 1) * 512]
                insts.append(e.matmul(self.pb[4 + h][:, :], self.tri[:, :],
                                      za[:, h * 512:(h + 1) * 512], start=True, stop=False))
                insts.append(e.matmul(self.pb[4 + h][:, :], self.ones1[:, :], ca,
                                      start=False, stop=True))
                insts.append(e.matmul(self.pb[6 + h][:, :], self.tri[:, :],
                                      zb[:, h * 512:(h + 1) * 512], start=True, stop=False))
                insts.append(e.matmul(self.pb[6 + h][:, :], self.ones1[:, :], cb,
                                      start=False, stop=True))
                insts.append(e.matmul(self.pb[0 + h][0:1, :], self.onescol[:, :],
                                      za[:, h * 512:(h + 1) * 512], start=True, stop=False))
                insts.append(e.matmul(self.pb[0 + h][0:1, :], self.one1[:, :], ca,
                                      start=False, stop=True))
                insts.append(e.matmul(self.pb[2 + h][0:1, :], self.onescol[:, :],
                                      zb[:, h * 512:(h + 1) * 512], start=True, stop=False))
                insts.append(e.matmul(self.pb[2 + h][0:1, :], self.one1[:, :], cb,
                                      start=False, stop=True))
            return insts
        self.mm(emit_mm, reads=tuple(zak) + tuple(zbk) + ("carry", "consts", "consts2"),
                writes=("pb0", "pb1", "pb2", "pb3", "pb4", "pb5", "pb6", "pb7"), tag=f"{tag}mm")

        def emit_car(e, ci_a=ci_a, ci_b=ci_b):
            for h in range(2):
                e.tensor_copy(self.carry_ap(ci_a)[:, h * 512:(h + 1) * 512],
                              self.pb[0 + h][0:1, :])
            e.tensor_copy(self.carry_ap(ci_b)[:, 0:512], self.pb[2][0:1, :])
            return e.tensor_copy(self.carry_ap(ci_b)[:, 512:1024], self.pb[3][0:1, :])
        self.v(emit_car, reads=("pb0", "pb1", "pb2", "pb3"), writes=("carry",), tag=f"{tag}car")

    def l0_chains(self, pvD, kvD, thD, lnD):
        bk = self.bk
        A = self.arena
        pv, pvk = A.view(0, [128, D], F32)
        kv, kvk = A.view(4096, [128, D], F32)
        th, thk = A.view(8192, [128, D], F32)
        pp, ppk = A.view(12288, [128, D], F32)
        sa, sak = A.view(16384, [128, D], F32)
        sb_, sbk = A.view(20480, [128, D], F32)
        sc_, sck = A.view(24576, [128, D], F32)
        sd_, sdk = A.view(28672, [128, D], F32)
        ii, iik = A.view(32768, [128, D], I32)
        kc = [A.view(36864, [128, D], F32), A.view(40960, [128, D], F32)]
        ks = [A.view(45056, [128, D], F32), A.view(49152, [128, D], F32)]
        kcs, kcsk = A.view(53248, [128, D], F32)
        kss, kssk = A.view(57344, [128, D], F32)
        pr, prk = A.view(61440, [128, D], F32)
        kr, krk = A.view(65536, [128, D], F32)
        lnt, lntk = A.view(69632, [128, D], F32)
        pst, pstk = A.view(73728, [128, D], F32)
        pct, pctk = A.view(77824, [128, D], F32)
        lng, lngk = self.rep_at(81920, "ln_g", D, tag="replng")
        lnb, lnbk = self.rep_at(86016, "ln_b", D, tag="replnb")
        def emit_cz(e):
            for i in range(4):
                e.memset(self.carries[i][:, :], 0.0)
            return e.memset(self.carries[3][:, :], 0.0)
        self.v(emit_cz, writes=("carry",), tag="l0c_z")

        pvr = pvD.rearrange("(tc p) d -> tc p d", p=128)
        kvr = kvD.rearrange("(tc p) d -> tc p d", p=128)
        thr = thD.rearrange("(tc p) d -> tc p d", p=128)
        lnr = lnD.rearrange("(tc p) d -> tc p d", p=128)
        ppsrc = self.din["pos_phases"]

        for tc in range(NT):
            p = tc % 2
            q = 1 - p
            bk.dma("sync", [(pp[:, :], ppsrc[tc * 128:(tc + 1) * 128, :]),
                            (pv[:, :], pvr[tc, :, :]),
                            (kv[:, :], kvr[tc, :, :]),
                            (th[:, :], thr[tc, :, :])],
                   reads=(f"l0pvD:{tc}", f"l0kvD:{tc}", f"l0thD:{tc}"),
                   writes=ppk + pvk + kvk + thk, tag=f"l0c_ld{tc}")

            def emit_rr(e):
                e.tensor_scalar(sa[:, :], pp[:, :], 1.0 / TWO_PI, None, ALU.mult)
                e.tensor_copy(ii[:, :], sa[:, :])
                e.tensor_copy(sa[:, :], ii[:, :])
                return e.scalar_tensor_tensor(sa[:, :], sa[:, :], -TWO_PI, pp[:, :],
                                              ALU.mult, ALU.add)
            self.v(emit_rr, reads=ppk, writes=sak + iik, tag=f"l0c_rr{tc}")

            def emit_trig(e):
                e.activation(pst[:, :], sa[:, :], AF.Sin)
                e.activation(sb_[:, :], sa[:, :], AF.Abs)
                return e.activation(pct[:, :], sb_[:, :], AF.Sin, bias=self.halfpi[:, 0:1], scale=-1.0)
            self.s(emit_trig, reads=sak + ("consts2",), writes=pstk + pctk + sbk, tag=f"l0c_trig{tc}")

            def emit_ints(e):
                e.tensor_tensor(sa[:, :], pct[:, :], pv[:, :], ALU.mult)
                return e.tensor_tensor(sb_[:, :], pst[:, :], pv[:, :], ALU.mult)
            self.v(emit_ints, reads=pctk + pstk + pvk, writes=sak + sbk, tag=f"l0c_int{tc}")
            self.cumsum_pair(sa, sak, sb_, sbk, 0, 1, f"l0c_pcs{tc}")

            def emit_pr(e):
                for h in range(2):
                    e.tensor_tensor(pr[:, h * 512:(h + 1) * 512], pct[:, h * 512:(h + 1) * 512],
                                    self.pb[4 + h][:, :], ALU.mult)
                    e.tensor_tensor(sc_[:, h * 512:(h + 1) * 512], pst[:, h * 512:(h + 1) * 512],
                                    self.pb[6 + h][:, :], ALU.mult)
                return e.tensor_tensor(pr[:, :], pr[:, :], sc_[:, :], ALU.add)
            self.v(emit_pr, reads=("pb4", "pb5", "pb6", "pb7") + pctk + pstk,
                   writes=prk + sck, tag=f"l0c_pr{tc}")

            def emit_ktrig(e, p=p):
                e.activation(ks[p][0][:, :], th[:, :], AF.Sin, scale=PI)
                e.activation(sb_[:, :], th[:, :], AF.Abs)
                return e.activation(kc[p][0][:, :], sb_[:, :], AF.Sin, bias=self.halfpi[:, 0:1], scale=-PI)
            self.s(emit_ktrig, reads=thk + ("consts2",), writes=ks[p][1] + kc[p][1] + sbk, tag=f"l0c_kt{tc}")

            if tc == 0:
                def emit_shz(e):
                    e.memset(kcs[0:1, :], 0.0)
                    return e.memset(kss[0:1, :], 0.0)
                self.v(emit_shz, writes=kcsk + kssk, tag="l0c_shz")
                bk.dma("gpsimd", [(kcs[1:128, :], kc[p][0][0:127, :]),
                                  (kss[1:128, :], ks[p][0][0:127, :])],
                       reads=kc[p][1] + ks[p][1], writes=kcsk + kssk, tag=f"l0c_sh{tc}")
            else:
                bk.dma("gpsimd", [(kcs[1:128, :], kc[p][0][0:127, :]),
                                  (kcs[0:1, :], kc[q][0][127:128, :]),
                                  (kss[1:128, :], ks[p][0][0:127, :]),
                                  (kss[0:1, :], ks[q][0][127:128, :])],
                       reads=kc[p][1] + ks[p][1] + kc[q][1] + ks[q][1],
                       writes=kcsk + kssk, tag=f"l0c_sh{tc}")

            def emit_kint(e, tc=tc):
                e.tensor_scalar(sc_[:, :], kv[:, :], self.vg[:, tc:tc + 1], None, ALU.mult)
                e.tensor_tensor(sa[:, :], kcs[:, :], sc_[:, :], ALU.mult)
                return e.tensor_tensor(sb_[:, :], kss[:, :], sc_[:, :], ALU.mult)
            self.v(emit_kint, reads=kvk + kcsk + kssk + ("vg",), writes=sak + sbk + sck,
                   tag=f"l0c_kint{tc}")
            self.cumsum_pair(sa, sak, sb_, sbk, 2, 3, f"l0c_kcs{tc}")

            def emit_kr(e, p=p):
                for h in range(2):
                    e.tensor_tensor(kr[:, h * 512:(h + 1) * 512],
                                    kc[p][0][:, h * 512:(h + 1) * 512],
                                    self.pb[4 + h][:, :], ALU.mult)
                    e.tensor_tensor(sc_[:, h * 512:(h + 1) * 512],
                                    ks[p][0][:, h * 512:(h + 1) * 512],
                                    self.pb[6 + h][:, :], ALU.mult)
                return e.tensor_tensor(kr[:, :], kr[:, :], sc_[:, :], ALU.add)
            self.v(emit_kr, reads=("pb4", "pb5", "pb6", "pb7") + kc[p][1] + ks[p][1],
                   writes=krk + sck, tag=f"l0c_kr{tc}")

            def emit_bl(e, tc=tc):
                e.tensor_scalar(sd_[:, :], pr[:, :], self.bw0p[:, tc:tc + 1], None, ALU.mult)
                return e.scalar_tensor_tensor(sd_[:, :], kr[:, :], self.bw1p[:, tc:tc + 1],
                                              sd_[:, :], ALU.mult, ALU.add)
            self.v(emit_bl, reads=prk + krk + ("bw0p", "bw1p"), writes=sdk, tag=f"l0c_bl{tc}")

            self.v(lambda e: e.reduce_sum(self.sm[:, 55:56], sd_[:, :], axis=AX),
                   reads=sdk, writes=("sm55",), tag=f"l0c_ls1{tc}")
            self.s(lambda e: e.activation(self.sqdump[:, :], sd_[:, :], AF.Square,
                                          accum_out=self.sm[:, 56:57]),
                   reads=sdk, writes=("sm56", "sqdump"), tag=f"l0c_ls2{tc}")

            def emit_lsm(e):
                e.tensor_scalar(self.sm[:, 55:56], self.sm[:, 55:56], -1.0 / D, None, ALU.mult)
                e.tensor_tensor(self.sm[:, 57:58], self.sm[:, 55:56], self.sm[:, 55:56], ALU.mult)
                e.tensor_scalar(self.sm[:, 56:57], self.sm[:, 56:57], 1.0 / D, None, ALU.mult)
                e.tensor_tensor(self.sm[:, 56:57], self.sm[:, 56:57], self.sm[:, 57:58],
                                ALU.subtract)
                e.tensor_scalar(self.sm[:, 56:57], self.sm[:, 56:57], 1e-5, None, ALU.add)
                return e.reciprocal(self.sm[:, 56:57], self.sm[:, 56:57])
            self.v(emit_lsm, reads=("sm55", "sm56"), writes=("sm55", "sm56", "sm57"),
                   tag=f"l0c_lsm{tc}")
            self.s(lambda e: e.activation(self.sm[:, 56:57], self.sm[:, 56:57], AF.Sqrt),
                   reads=("sm56",), writes=("sm56",), tag=f"l0c_lsq{tc}")

            def emit_lap(e):
                e.tensor_scalar(sd_[:, :], sd_[:, :], self.sm[:, 55:56], self.sm[:, 56:57],
                                ALU.add, ALU.mult)
                e.tensor_tensor(lnt[:, :], sd_[:, :], lng[:, :], ALU.mult)
                return e.tensor_tensor(lnt[:, :], lnt[:, :], lnb[:, :], ALU.add)
            self.v(emit_lap, reads=sdk + ("sm55", "sm56") + lngk + lnbk,
                   writes=lntk + sdk, tag=f"l0c_lap{tc}")
            bk.dma("gpsimd", [(lnr[tc, :, :], lnt[:, :])], reads=lntk,
                   writes=(f"lnD:{tc}",), tag=f"l0c_lst{tc}")

    def l0_wo(self, lnD, sD):
        bk = self.bk
        A = self.arena
        wo, wok = A.view(0, [128, ND, D], F32)
        lnt, lntk = A.view(36864, [128, D], F32)
        lfm, lfmk = A.view(40960, [128, ND, 128], F32)
        xb, xbk = A.view(45056, [128, D], F32)
        stg, stgk = A.view(49152, [128, D], F32)
        bo0, bo0k = self.rep_at(53248, "bo0", D, tag="repbo0")
        wsrc = self.din["Wo0"].rearrange("(c p) n -> p c n", p=128)
        bk.dma("sync", [(wo[:, :, 0:512], wsrc[:, :, 0:512]),
                        (wo[:, :, 512:1024], wsrc[:, :, 512:1024])], writes=wok, tag="l0wo_w")
        lnr = lnD.rearrange("(tc p) d -> tc p d", p=128)
        sr = sD.rearrange("(tc p) d -> tc p d", p=128)
        for tc in range(NT):
            bk.dma("sync", [(lnt[:, :], lnr[tc, :, :])], reads=(f"lnD:{tc}",), writes=lntk,
                   tag=f"l0wo_ld{tc}")
            self.transpose128(lnt, lntk, lfm, lfmk, ND, f"l0wo_t{tc}")
            self.xn_tile(xb[:, :], xbk, tc, f"l0wo_xn{tc}")
            self.v(lambda e: e.tensor_tensor(xb[:, :], xb[:, :], bo0[:, :], ALU.add),
                   reads=xbk + bo0k, writes=xbk, tag=f"l0wo_xb{tc}")
            for eo in range(2):
                pbank = self.pb[eo]
                pk = f"pb{eo}"

                def emit_mm(e, eo=eo, pbank=pbank):
                    return [e.matmul(pbank[:, :], lfm[:, dc, :],
                                     wo[:, dc, eo * 512:eo * 512 + 512],
                                     start=(dc == 0), stop=(dc == ND - 1)) for dc in range(ND)]
                self.mm(emit_mm, reads=lfmk + wok, writes=(pk,), tag=f"l0wo_mm{tc}_{eo}")

                def emit_sev(e, eo=eo, pbank=pbank):
                    return e.scalar_tensor_tensor(stg[:, eo * 512:eo * 512 + 512], pbank[:, :],
                                                  1.0, xb[:, eo * 512:eo * 512 + 512],
                                                  ALU.mult, ALU.add)
                self.v(emit_sev, reads=(pk,) + xbk, writes=stgk, tag=f"l0wo_sev{tc}_{eo}")
            bk.dma("gpsimd", [(sr[tc, :, :], stg[:, :])], reads=stgk, writes=(f"sD:{tc}",),
                   tag=f"l0wo_st{tc}")

    def l0_ffn(self, w1qD, w2qD, hD, sD):
        bk = self.bk
        A = self.arena
        qfm, qfmk = A.view(0, [128, ND, T], BF16)
        xnt, xntk = A.view(36864, [128, D], F32)
        r1, r1k = A.view(40960, [128, D], F32)
        ii, iik = A.view(45056, [128, D], I32)
        stok1 = self.sm[:, 0:NT]
        hr = hD.rearrange("(tc p) f -> tc p f", p=128)
        sr = sD.rearrange("(tc p) d -> tc p d", p=128)

        for tc in range(NT):
            self.xn_tile(xnt[:, :], xntk, tc, f"l0f_xn{tc}")
            self.s(lambda e: e.activation(self.sqdump[:, :], xnt[:, :], AF.Square,
                                          accum_out=self.sm[:, 58:59]),
                   reads=xntk, writes=("sm58", "sqdump"), tag=f"l0f_s2{tc}")

            def emit_rms(e):
                e.tensor_scalar(self.sm[:, 58:59], self.sm[:, 58:59], 1.0 / D, 1e-5,
                                ALU.mult, ALU.add)
                return e.reciprocal(self.sm[:, 58:59], self.sm[:, 58:59])
            self.v(emit_rms, reads=("sm58",), writes=("sm58",), tag=f"l0f_rms{tc}")
            self.s(lambda e: e.activation(self.sm[:, 58:59], self.sm[:, 58:59], AF.Sqrt),
                   reads=("sm58",), writes=("sm58",), tag=f"l0f_rsq{tc}")

            def emit_ram(e, tc=tc):
                e.tensor_scalar(r1[:, :], xnt[:, :], self.sm[:, 58:59], None, ALU.mult)
                e.tensor_reduce(self.sm[:, 59:60], r1[:, :], AX, ALU.max, apply_absolute_value=True)
                e.tensor_scalar(self.sm[:, 59:60], self.sm[:, 59:60], 1e-5, None, ALU.max)
                e.reciprocal(self.sm[:, 60:61], self.sm[:, 59:60])
                return e.tensor_scalar(stok1[:, tc:tc + 1], self.sm[:, 59:60], 1.0 / 127.0,
                                       None, ALU.mult)
            self.v(emit_ram, reads=xntk + ("sm58",), writes=r1k + ("sm59", "sm60", "stok1"),
                   tag=f"l0f_ram{tc}")

            def emit_q(e):
                e.tensor_scalar(r1[:, :], r1[:, :], self.sm[:, 60:61], 127.0, ALU.mult, ALU.mult)
                e.tensor_copy(ii[:, :], r1[:, :])
                e.tensor_copy(r1[:, :], ii[:, :])
                return e.tensor_scalar(r1[:, :], r1[:, :], -128.0, 127.0, ALU.max, ALU.min)
            self.v(emit_q, reads=("sm60",) + r1k, writes=r1k + iik, tag=f"l0f_q{tc}")
            qdst = qfm[:, :, tc * 128:tc * 128 + 128]
            self.transpose128(r1, r1k, qdst, qfmk, ND, f"l0f_t{tc}")

        def emit_stok(e):
            return e.tensor_tensor(stok1, stok1, self.dqw1[:, 0:1].broadcast_to([128, NT]),
                                   ALU.mult)
        self.v(emit_stok, reads=("stok1", "dq_dqw1"), writes=("stok1",), tag="l0f_stok")
        bk.tap("l0_qfm", qfm[:, :, :], reads=qfmk)
        bk.tap("l0_dqw1", self.dqw1[:, :], reads=("dq_dqw1",))

        w1b, w1bk = A.view(49152, [128, 2, ND, 512], BF16)
        h1 = len(w1bk) // 2
        w1kk = [w1bk[:h1], w1bk[h1:]] if h1 else [w1bk, w1bk]
        hst, hstk = A.view(65536, [128, 512], F32)
        hst2, hst2k = A.view(67584, [128, 512], F32)
        f0b1, f0b1k = self.rep_at(69632, "f0b1", 4096, tag="repf0b1")
        w1src = w1qD.rearrange("(c p) f -> p c f", p=128)
        for fo in range(8):
            fb = fo % 2
            bk.dma("sync", [(w1b[:, fb, :, :], w1src[:, :, fo * 512:fo * 512 + 512])],
                   reads=tuple(f"dram_dqw1:{c}" for c in range(ND)),
                   writes=w1kk[fb], tag=f"l0f_w1{fo}")
            for tc in range(NT):
                pbank = self.pb[tc % 4]
                pk = f"pb{tc % 4}"

                def emit_mm(e, tc=tc, fb=fb, pbank=pbank):
                    return [e.matmul(pbank[:, :], qfm[:, dc, tc * 128:tc * 128 + 128],
                                     w1b[:, fb, dc, :], start=(dc == 0), stop=(dc == ND - 1))
                            for dc in range(ND)]
                self.mm(emit_mm, reads=w1kk[fb] + qfmk, writes=(pk,), tag=f"l0f_hmm{fo}_{tc}")

                def emit_ev(e, tc=tc, fo=fo, pbank=pbank):
                    return e.scalar_tensor_tensor(hst[:, :], pbank[:, :], stok1[:, tc:tc + 1],
                                                  f0b1[:, fo * 512:fo * 512 + 512],
                                                  ALU.mult, ALU.add)
                self.v(emit_ev, reads=(pk, "stok1") + f0b1k, writes=hstk, tag=f"l0f_hev{fo}_{tc}")
                self.s(lambda e: e.activation(hst2[:, :], hst[:, :], AF.Gelu),
                       reads=hstk, writes=hst2k, tag=f"l0f_hg{fo}_{tc}")
                bk.dma("gpsimd", [(hr[tc, :, fo * 512:fo * 512 + 512], hst2[:, :])],
                       reads=hst2k, writes=(f"hD:{tc}:{fo}",), tag=f"l0f_hst{fo}_{tc}")

        hq, hqk = A.view(0, [128, 4096], F32)
        hqi, hqik = A.view(16384, [128, 4096], I32)
        hfm, hfmk = A.view(32768, [128, 32, 128], BF16)
        w2b, w2bk = A.view(40960, [128, 32, 512], BF16)
        gst, gstk = A.view(75776, [128, D], F32)
        sst, sstk = A.view(79872, [128, D], F32)
        f0b2, f0b2k = self.rep_at(84992, "f0b2", D, tag="repf0b2")
        stok2 = self.sm[:, 61:62]
        w2src = w2qD.rearrange("(c p) d -> p c d", p=128)
        for tc in range(NT):
            bk.dma("sync", [(hq[:, :], hr[tc, :, :])],
                   reads=tuple(f"hD:{tc}:{fo}" for fo in range(8)), writes=hqk,
                   tag=f"l0f_hld{tc}")
            for j in range(4):
                self.s(lambda e, j=j: e.activation(self.sqdump[:, :], hq[:, j * D:(j + 1) * D],
                                                   AF.Square, accum_out=self.sm[:, 16 + j:17 + j]),
                       reads=hqk, writes=(f"sm{16 + j}", "sqdump"), tag=f"l0f2_s2{tc}_{j}")

            def emit_rms(e):
                e.tensor_reduce(self.sm[:, 58:59], self.sm[:, 16:20], AX, ALU.add)
                e.tensor_scalar(self.sm[:, 58:59], self.sm[:, 58:59], 1.0 / 4096.0, 1e-5,
                                ALU.mult, ALU.add)
                return e.reciprocal(self.sm[:, 58:59], self.sm[:, 58:59])
            self.v(emit_rms, reads=("sm16", "sm17", "sm18", "sm19"), writes=("sm58",),
                   tag=f"l0f2_rms{tc}")
            self.s(lambda e: e.activation(self.sm[:, 58:59], self.sm[:, 58:59], AF.Sqrt),
                   reads=("sm58",), writes=("sm58",), tag=f"l0f2_rsq{tc}")

            def emit_rq(e, tc=tc):
                e.tensor_scalar(hq[:, :], hq[:, :], self.sm[:, 58:59], None, ALU.mult)
                e.tensor_reduce(self.sm[:, 59:60], hq[:, :], AX, ALU.max, apply_absolute_value=True)
                e.tensor_scalar(self.sm[:, 59:60], self.sm[:, 59:60], 1e-5, None, ALU.max)
                e.reciprocal(self.sm[:, 60:61], self.sm[:, 59:60])
                e.tensor_scalar(hq[:, :], hq[:, :], self.sm[:, 60:61], 127.0, ALU.mult, ALU.mult)
                e.tensor_copy(hqi[:, :], hq[:, :])
                e.tensor_copy(hq[:, :], hqi[:, :])
                e.tensor_scalar(hq[:, :], hq[:, :], -128.0, 127.0, ALU.max, ALU.min)
                e.tensor_scalar(stok2, self.sm[:, 59:60], 1.0 / 127.0, None, ALU.mult)
                return e.tensor_tensor(stok2, stok2, self.dqw2[:, 0:1], ALU.mult)
            self.v(emit_rq, reads=hqk + ("sm58", "dq_dqw2"),
                   writes=hqk + hqik + ("sm59", "sm60", "sm61"), tag=f"l0f2_q{tc}")

            self.transpose128(hq, hqk, hfm, hfmk, 32, f"l0f2_t{tc}")

            for eo in range(2):
                bk.dma("sync", [(w2b[:, 0:16, :], w2src[:, 0:16, eo * 512:eo * 512 + 512]),
                                (w2b[:, 16:32, :], w2src[:, 16:32, eo * 512:eo * 512 + 512])],
                       reads=tuple(f"dram_dqw2:{c}" for c in range(32)), writes=w2bk,
                       tag=f"l0f2_w2_{tc}_{eo}")
                pbank = self.pb[eo]
                pk = f"pb{eo}"

                def emit_mm(e, pbank=pbank):
                    return [e.matmul(pbank[:, :], hfm[:, fc, :], w2b[:, fc, :],
                                     start=(fc == 0), stop=(fc == 31)) for fc in range(32)]
                self.mm(emit_mm, reads=w2bk + hfmk, writes=(pk,), tag=f"l0f2_omm{tc}_{eo}")

                def emit_ev(e, eo=eo, pbank=pbank):
                    return e.scalar_tensor_tensor(gst[:, eo * 512:eo * 512 + 512], pbank[:, :],
                                                  stok2, f0b2[:, eo * 512:eo * 512 + 512],
                                                  ALU.mult, ALU.add)
                self.v(emit_ev, reads=(pk, "sm61") + f0b2k, writes=gstk, tag=f"l0f2_oev{tc}_{eo}")
            self.s(lambda e: e.activation(gst[:, :], gst[:, :], AF.Silu),
                   reads=gstk, writes=gstk, tag=f"l0f2_silu{tc}")

            bk.dma("sync", [(sst[:, :], sr[tc, :, :])], reads=(f"sD:{tc}",), writes=sstk,
                   tag=f"l0f2_sld{tc}")

            def emit_comb(e, tc=tc):
                e.tensor_tensor(sst[:, :], sst[:, :], gst[:, :], ALU.mult)
                return e.tensor_tensor(self.xTM[:, tc, :], self.xTM[:, tc, :], sst[:, :], ALU.add)
            self.v(emit_comb, reads=gstk + sstk + (f"xTM:{tc}",), writes=(f"xTM:{tc}",) + sstk,
                   tag=f"l0f2_comb{tc}")



    def layer3(self):
        bk = self.bk
        A = self.arena
        self.ln_stats("l3ln")
        self.build_xnFM("l3fm")
        qD = self.dscr["qD"]; kD = self.dscr["kD"]; vD = self.dscr["vD"]
        sD = self.dscr["sD"]
        sr = sD.rearrange("(tc p) d -> tc p d", p=128)

        latfm, latk = A.view(65664, [128, T], F32)
        wkd, wkdk = A.view(73856, [128, ND, 128], F32)
        bk.dma("sync", [(wkd[:, :, :], self.din["w3kvd"].rearrange("(c p) e -> p c e", p=128))],
               writes=wkdk, tag="l3_wkd")
        for tq in range(4):
            pbank = self.pb[tq % 2]
            pk = f"pb{tq % 2}"

            def emit_mm(e, tq=tq, pbank=pbank):
                return [e.matmul(pbank[:, :], wkd[:, dc, :],
                                 self.xnFM[:, dc, 1 + tq * 512:1 + tq * 512 + 512],
                                 start=(dc == 0), stop=(dc == ND - 1)) for dc in range(ND)]
            self.mm(emit_mm, reads=wkdk + self.xnFM_keys, writes=(pk,), tag=f"l3_lat{tq}")

            def emit_ev(e, tq=tq, pbank=pbank):
                return e.activation(latfm[:, tq * 512:tq * 512 + 512], pbank[:, :], AF.Copy)
            self.s(emit_ev, reads=(pk,), writes=latk, tag=f"l3_latev{tq}")
        wku, wkuk = A.view(78080, [128, 2048], F32)
        stg4 = [A.view(86272, [128, 512], F32), A.view(88320, [128, 512], F32)]
        bk.dma("sync", [(wku[:, :], self.din["w3kvu"])], writes=wkuk, tag="l3_wku")
        for ec in range(8):
            for tq in range(4):
                pbank = self.pb[2 + tq % 2]
                pk = f"pb{2 + tq % 2}"

                def emit_mm(e, ec=ec, tq=tq, pbank=pbank):
                    return e.matmul(pbank[:, :], wku[:, ec * 128:(ec + 1) * 128],
                                    latfm[:, tq * 512:tq * 512 + 512], start=True, stop=True)
                self.mm(emit_mm, reads=wkuk + latk, writes=(pk,), tag=f"l3_kmm{ec}_{tq}")
                sb_, sbk = stg4[tq % 2]

                def emit_ev(e, pbank=pbank, sb_=sb_):
                    return e.activation(sb_[:, :], pbank[:, :], AF.Copy)
                self.s(emit_ev, reads=(pk,), writes=sbk, tag=f"l3_kev{ec}_{tq}")
                bk.dma("gpsimd", [(kD[ec * 128:(ec + 1) * 128, tq * 512:tq * 512 + 512],
                                   sb_[:, :])],
                       reads=sbk, writes=(f"l3kD:{ec}:{tq}",), tag=f"l3_kst{ec}_{tq}")
        for tc in range(NT):
            for eo in range(2):
                pbank = self.pb[4 + eo]
                pk = f"pb{4 + eo}"

                def emit_mm(e, tc=tc, eo=eo, pbank=pbank):
                    return e.matmul(pbank[:, :], latfm[:, tc * 128:tc * 128 + 128],
                                    wku[:, 1024 + eo * 512:1024 + eo * 512 + 512],
                                    start=True, stop=True)
                self.mm(emit_mm, reads=wkuk + latk, writes=(pk,), tag=f"l3_vmm{tc}_{eo}")
                sb_, sbk = stg4[eo]

                def emit_ev(e, pbank=pbank, sb_=sb_):
                    return e.activation(sb_[:, :], pbank[:, :], AF.Copy)
                self.s(emit_ev, reads=(pk,), writes=sbk, tag=f"l3_vev{tc}_{eo}")
                bk.dma("gpsimd", [(vD[tc * 128:(tc + 1) * 128, eo * 512:eo * 512 + 512],
                                   sb_[:, :])],
                       reads=sbk, writes=(f"l3vD:{tc}",), tag=f"l3_vst{tc}_{eo}")
        self.proj_fm_spill_nb("w3q", qD, "l3q")

        wg3, wg3k = A.view(73856, [128, ND, 8], F32)
        bk.dma("sync", [(wg3[:, :, :], self.din["w3g"].rearrange("(c p) e -> p c e", p=128))],
               writes=wg3k, tag="l3r_w")
        self.rweights = bk.sb("rweights", [128, NT, 8], F32)
        for tc in range(NT):
            pbank = self.pb[6]
            pk = "pb6"

            def emit_mm(e, tc=tc, pbank=pbank):
                return [e.matmul(pbank[:, 0:8], self.xnFM[:, dc, 1 + tc * 128:1 + tc * 128 + 128],
                                 wg3[:, dc, :], start=(dc == 0), stop=(dc == ND - 1))
                        for dc in range(ND)]
            self.mm(emit_mm, reads=wg3k + self.xnFM_keys, writes=(pk,), tag=f"l3r_mm{tc}")

            def emit_rt(e, tc=tc, pbank=pbank):
                lg = self.sm[:, 16:24]
                m1 = self.sm[:, 24:25]
                m2 = self.sm[:, 25:26]
                t1 = self.sm[:, 26:34]
                e.tensor_copy(lg, pbank[:, 0:8])
                e.tensor_reduce(m1, lg, AX, ALU.max)
                e.tensor_scalar(t1, lg, m1, None, ALU.is_ge)
                e.tensor_scalar(t1, t1, 1e30, None, ALU.mult)
                e.tensor_tensor(t1, lg, t1, ALU.subtract)
                e.tensor_reduce(m2, t1, AX, ALU.max)
                e.tensor_scalar(self.sm[:, 34:35], m1, -1.0, None, ALU.mult)
                return e.tensor_scalar(self.rweights[:, tc, :], lg, m2, None, ALU.is_ge)
            self.v(emit_rt, reads=(pk,), writes=("rweights", "sm_l3r"), tag=f"l3r_t{tc}")

            def emit_e1(e, tc=tc):
                e.activation(self.sm[:, 16:24], self.sm[:, 16:24], AF.Exp,
                             bias=self.sm[:, 34:35])
                return e.activation(self.sm[:, 35:36], self.sm[:, 25:26], AF.Exp,
                                    bias=self.sm[:, 34:35])
            self.s(emit_e1, reads=("sm_l3r",), writes=("sm_l3r",), tag=f"l3r_e{tc}")

            def emit_w(e, tc=tc):
                e.tensor_scalar(self.sm[:, 35:36], self.sm[:, 35:36], 1.0, None, ALU.add)
                e.reciprocal(self.sm[:, 35:36], self.sm[:, 35:36])
                e.tensor_tensor(self.rweights[:, tc, :], self.rweights[:, tc, :],
                                self.sm[:, 16:24], ALU.mult)
                return e.tensor_scalar(self.rweights[:, tc, :], self.rweights[:, tc, :],
                                       self.sm[:, 35:36], None, ALU.mult)
            self.v(emit_w, reads=("sm_l3r", "rweights"), writes=("rweights", "sm_l3r"),
                   tag=f"l3r_w{tc}")

        attn3, attn3k = A.view(0, [128, ND, 512], F32)
        vpad3, vpad3k = A.view(16384, [128, 16, 65], F32)
        kh3, kh3k = A.view(20608, [128, T], F32)
        qh3, qh3k = A.view(28800, [128, 512], F32)
        expt3 = [A.view(30848, [128, 512], F32), A.view(32896, [128, 512], F32)]
        rrow3, rrow3k = A.view(34944, [128, 512], F32)
        rb3, rb3k = A.view(36992, [128, 512], F32)
        avtmp3, avt3k = A.view(39040, [128, 512], F32)
        cmskT, cmskTk = A.view(41088, [128, 4, 512], F32)
        wo3, wo3k = A.view(49280, [128, ND, D], F32)
        stg5, stg5k = A.view(82048, [128, D], F32)
        bk.dma("sync", [(cmskT[:, :, :], self.din["cmaskT"].rearrange("s p k -> p s k")),
                        (wo3[:, :, 0:512],
                         self.din["w3out"].rearrange("(c p) e -> p c e", p=128)[:, :, 0:512]),
                        (wo3[:, :, 512:1024],
                         self.din["w3out"].rearrange("(c p) e -> p c e", p=128)[:, :, 512:1024])],
               writes=cmskTk + wo3k, tag="l3a_w")
        self.v(lambda e: e.memset(vpad3[:, :, 64:65], 1.0), writes=vpad3k, tag="l3a_vones")
        vsrc3 = vD.rearrange("(kc p) (h dv) -> p kc h dv", p=128, dv=64)
        qrd3 = tuple(f"l3qD:{ec}:{tq}" for ec in range(ND) for tq in range(4))
        krd3 = tuple(f"l3kD:{ec}:{tq}" for ec in range(ND) for tq in range(4))
        vrd3 = tuple(f"l3vD:{tc}" for tc in range(NT))
        for qc in range(4):
            for h in range(16):
                bk.dma("sync", [(kh3[0:64, :], kD[h * 64:(h + 1) * 64, :]),
                                (qh3[0:64, :], qD[h * 64:(h + 1) * 64, qc * 512:qc * 512 + 512]),
                                (vpad3[:, :, 0:64], vsrc3[:, :, h, :])],
                       reads=qrd3 + krd3 + vrd3, writes=kh3k + qh3k + vpad3k,
                       tag=f"l3a_ld{qc}_{h}")
                nkc = 4 * (qc + 1)
                for kc in range(nkc):
                    sb_ = self.pb[2 + kc % 2]
                    spk = f"pb{2 + kc % 2}"

                    def emit_s(e, kc=kc, sb_=sb_):
                        return e.matmul(sb_[:, :], kh3[0:64, kc * 128:kc * 128 + 128],
                                        qh3[0:64, :], start=True, stop=True)
                    self.mm(emit_s, reads=kh3k + qh3k, writes=(spk,), tag=f"l3a_s{qc}_{h}_{kc}")
                    ex, exk = expt3[kc % 2]

                    def emit_e(e, sb_=sb_, ex=ex):
                        return e.activation(ex[:, :], sb_[:, :], AF.Exp, scale=0.125)
                    self.s(emit_e, reads=(spk,), writes=exk, tag=f"l3a_e{qc}_{h}_{kc}")
                    if kc >= 4 * qc:
                        s_ = kc - 4 * qc

                        def emit_m(e, s_=s_, ex=ex):
                            return e.tensor_tensor(ex[:, :], ex[:, :], cmskT[:, s_, :], ALU.mult)
                        self.v(emit_m, reads=exk + cmskTk, writes=exk,
                               tag=f"l3a_m{qc}_{h}_{kc}")

                    def emit_av(e, kc=kc, ex=ex, nkc=nkc):
                        return e.matmul(self.pb[4][0:65, :], vpad3[:, kc, :], ex[:, :],
                                        start=(kc == 0), stop=(kc == nkc - 1))
                    self.mm(emit_av, reads=expt3[kc % 2][1] + vpad3k, writes=("pb4",),
                            tag=f"l3a_av{qc}_{h}_{kc}")

                def emit_cp(e):
                    return e.activation(rrow3[64:65, :], self.pb[4][64:65, :], AF.Copy)
                self.s(emit_cp, reads=("pb4",), writes=rrow3k, tag=f"l3a_cp{qc}_{h}")
                bk.dma("gpsimd", [(rrow3[0:1, :], rrow3[64:65, :])], reads=rrow3k,
                       writes=rrow3k, tag=f"l3a_mvr{qc}_{h}")

                def emit_r(e):
                    return e.reciprocal(rrow3[0:1, :], rrow3[0:1, :])
                self.v(emit_r, reads=rrow3k, writes=rrow3k, tag=f"l3a_r{qc}_{h}")

                def emit_bc(e):
                    return e.matmul(self.pb[5][0:64, :], self.ones1[0:1, 0:64], rrow3[0:1, :],
                                    start=True, stop=True)
                self.mm(emit_bc, reads=rrow3k + ("consts",), writes=("pb5",),
                        tag=f"l3a_bc{qc}_{h}")

                def emit_rbe(e):
                    return e.activation(rb3[0:64, :], self.pb[5][0:64, :], AF.Copy)
                self.s(emit_rbe, reads=("pb5",), writes=rb3k, tag=f"l3a_rbe{qc}_{h}")
                if h % 2 == 0:
                    def emit_at(e, h=h):
                        return e.tensor_tensor(attn3[0:64, h // 2, :], self.pb[4][0:64, :],
                                               rb3[0:64, :], ALU.mult)
                    self.v(emit_at, reads=("pb4",) + rb3k, writes=attn3k, tag=f"l3a_at{qc}_{h}")
                else:
                    def emit_at(e):
                        return e.tensor_tensor(avtmp3[0:64, :], self.pb[4][0:64, :],
                                               rb3[0:64, :], ALU.mult)
                    self.v(emit_at, reads=("pb4",) + rb3k, writes=avt3k, tag=f"l3a_at{qc}_{h}")
                    bk.dma("gpsimd", [(attn3[64:128, h // 2, :], avtmp3[0:64, :])],
                           reads=avt3k, writes=attn3k, tag=f"l3a_mv{qc}_{h}")
            for ts_ in range(4):
                for eo in range(2):
                    pbank = self.pb[eo]
                    pk = f"pb{eo}"

                    def emit_mm(e, ts_=ts_, eo=eo, pbank=pbank):
                        return [e.matmul(pbank[:, :], attn3[:, c, ts_ * 128:ts_ * 128 + 128],
                                         wo3[:, c, eo * 512:eo * 512 + 512],
                                         start=(c == 0), stop=(c == ND - 1)) for c in range(ND)]
                    self.mm(emit_mm, reads=attn3k + wo3k, writes=(pk,),
                            tag=f"l3wo_mm{qc}_{ts_}_{eo}")

                    def emit_ev(e, eo=eo, pbank=pbank):
                        return e.activation(stg5[:, eo * 512:eo * 512 + 512], pbank[:, :],
                                            AF.Copy)
                    self.s(emit_ev, reads=(pk,), writes=stg5k, tag=f"l3wo_ev{qc}_{ts_}_{eo}")
                tc = qc * 4 + ts_
                bk.dma("gpsimd", [(sr[tc, :, :], stg5[:, :])], reads=stg5k,
                       writes=(f"sD:{tc}",), tag=f"l3wo_st{qc}_{ts_}")

        self.l3_moe(sD)

    def proj_fm_spill_nb(self, wnm, outD, tag):
        bk = self.bk
        A = self.arena
        wbuf, wkeys = A.view(90368, [128, 2, ND, 128], F32)
        h1 = len(wkeys) // 2
        wbk = [wkeys[:h1], wkeys[h1:]] if h1 else [wkeys, wkeys]
        stg = [A.view(98560, [128, 512], F32), A.view(100608, [128, 512], F32)]
        wsrc = self.din[wnm]
        for ec in range(ND):
            eb = ec % 2
            bk.dma("sync", [(wbuf[:, eb, :, :],
                             wsrc[:, ec * 128:(ec + 1) * 128].rearrange("(c p) e -> p c e", p=128))],
                   writes=wbk[eb], tag=f"{tag}w{ec}")
            for tq in range(4):
                pbank = self.pb[6 + tq % 2]
                pk = f"pb{6 + tq % 2}"

                def emit_mm(e, tq=tq, eb=eb, pbank=pbank):
                    return [e.matmul(pbank[:, :], wbuf[:, eb, dc, :],
                                     self.xnFM[:, dc, 1 + tq * 512:1 + tq * 512 + 512],
                                     start=(dc == 0), stop=(dc == ND - 1)) for dc in range(ND)]
                self.mm(emit_mm, reads=wbk[eb] + self.xnFM_keys, writes=(pk,),
                        tag=f"{tag}mm{ec}_{tq}")
                sb_, sbk = stg[tq % 2]

                def emit_ev(e, pbank=pbank, sb_=sb_):
                    return e.activation(sb_[:, :], pbank[:, :], AF.Copy)
                self.s(emit_ev, reads=(pk,), writes=sbk, tag=f"{tag}ev{ec}_{tq}")
                bk.dma("gpsimd", [(outD[ec * 128:(ec + 1) * 128, tq * 512:tq * 512 + 512],
                                   sb_[:, :])],
                       reads=sbk, writes=(f"{tag}D:{ec}:{tq}",), tag=f"{tag}st{ec}_{tq}")

    def l3_moe(self, sD):
        bk = self.bk
        A = self.arena
        sr = sD.rearrange("(tc p) d -> tc p d", p=128)
        hfm, hfmk = A.view(0, [128, 16, 512], F32)
        w1b, w1bk = A.view(32768, [128, 2, ND, 128], F32)
        hw1 = len(w1bk) // 2
        w1kk = [w1bk[:hw1], w1bk[hw1:]] if hw1 else [w1bk, w1bk]
        w2b, w2bk = A.view(40960, [128, 2, 512], F32)
        hw2 = len(w2bk) // 2
        w2kk = [w2bk[:hw2], w2bk[hw2:]] if hw2 else [w2bk, w2bk]
        b1c, b1ck = A.view(45056, [128, 8, 16], F32)
        b2r, b2rk = A.view(45568 + 32 if False else 45600, [128, 8, D], F32)
        acc, acck = A.view(78368, [128, 4, D], F32)
        gst6, gst6k = A.view(94752, [128, 512], F32)
        sst6, sst6k = A.view(96800, [128, D], F32)
        bk.dma("sync", [(b1c[:, :, :], self.din["w3b1C"])], writes=b1ck, tag="l3m_b1")
        for e_ in range(8):
            bk.dma("gpsimd", [(b2r[:, e_, :],
                               self.din["w3b2"][e_:e_ + 1, :].partition_broadcast(128).squeeze(1))],
                   writes=b2rk, tag=f"l3m_b2{e_}")
        w1src = self.din["w3W1"]
        w2src = self.din["w3W2"]
        for tq in range(4):
            self.v(lambda e: e.memset(acc[:, :, :], 0.0), writes=acck, tag=f"l3m_z{tq}")
            for e_ in range(8):
                for hc in range(16):
                    eb = hc % 2
                    bk.dma("sync", [(w1b[:, eb, :, :],
                                     w1src[e_, :, hc * 128:(hc + 1) * 128]
                                     .rearrange("(c p) f -> p c f", p=128))],
                           writes=w1kk[eb], tag=f"l3m_w1{tq}_{e_}_{hc}")
                    pbank = self.pb[hc % 2]
                    pk = f"pb{hc % 2}"

                    def emit_mm(e, tq=tq, eb=eb, pbank=pbank):
                        return [e.matmul(pbank[:, :], w1b[:, eb, dc, :],
                                         self.xnFM[:, dc, 1 + tq * 512:1 + tq * 512 + 512],
                                         start=(dc == 0), stop=(dc == ND - 1))
                                for dc in range(ND)]
                    self.mm(emit_mm, reads=w1kk[eb] + self.xnFM_keys, writes=(pk,),
                            tag=f"l3m_m1{tq}_{e_}_{hc}")

                    def emit_ev(e, e_=e_, hc=hc, pbank=pbank):
                        return e.activation(hfm[:, hc, :], pbank[:, :], AF.Relu,
                                            bias=b1c[:, e_, hc:hc + 1])
                    self.s(emit_ev, reads=(pk,) + b1ck, writes=hfmk, tag=f"l3m_e1{tq}_{e_}_{hc}")
                for hc in range(16):
                    eb = hc % 2
                    bk.dma("sync", [(w2b[:, eb, :], w2src[e_, hc * 128:(hc + 1) * 128, 0:512])],
                           writes=w2kk[eb], tag=f"l3m_w2b{tq}_{e_}_{hc}")
                    for ts_ in range(4):
                        pbank = self.pb[ts_]
                        pk = f"pb{ts_}"

                        def emit_mm(e, ts_=ts_, hc=hc, eb=eb, pbank=pbank):
                            return e.matmul(pbank[:, :], hfm[:, hc, ts_ * 128:ts_ * 128 + 128],
                                            w2b[:, eb, :], start=(hc == 0), stop=(hc == 15))
                        self.mm(emit_mm, reads=w2kk[eb] + hfmk, writes=(pk,),
                                tag=f"l3m_m2a{tq}_{e_}_{hc}_{ts_}")
                    bk.dma("sync", [(w2b[:, eb, :], w2src[e_, hc * 128:(hc + 1) * 128, 512:1024])],
                           writes=w2kk[eb], tag=f"l3m_w2c{tq}_{e_}_{hc}")
                    for ts_ in range(4):
                        pbank = self.pb[4 + ts_]
                        pk = f"pb{4 + ts_}"

                        def emit_mm(e, ts_=ts_, hc=hc, eb=eb, pbank=pbank):
                            return e.matmul(pbank[:, :], hfm[:, hc, ts_ * 128:ts_ * 128 + 128],
                                            w2b[:, eb, :], start=(hc == 0), stop=(hc == 15))
                        self.mm(emit_mm, reads=w2kk[eb] + hfmk, writes=(pk,),
                                tag=f"l3m_m2b{tq}_{e_}_{hc}_{ts_}")
                for ts_ in range(4):
                    tc = tq * 4 + ts_
                    for eo in range(2):
                        pbank = self.pb[eo * 4 + ts_]
                        pk = f"pb{eo * 4 + ts_}"

                        def emit_acc(e, tc=tc, ts_=ts_, eo=eo, e_=e_, pbank=pbank):
                            e.scalar_tensor_tensor(gst6[:, :], pbank[:, :], 1.0,
                                                   b2r[:, e_, eo * 512:eo * 512 + 512],
                                                   ALU.mult, ALU.add)
                            return e.scalar_tensor_tensor(
                                acc[:, ts_, eo * 512:eo * 512 + 512], gst6[:, :],
                                self.rweights[:, tc, e_:e_ + 1],
                                acc[:, ts_, eo * 512:eo * 512 + 512], ALU.mult, ALU.add)
                        self.v(emit_acc, reads=(pk, "rweights") + b2rk + acck + gst6k,
                               writes=acck + gst6k, tag=f"l3m_ac{tq}_{e_}_{ts_}_{eo}")
            for ts_ in range(4):
                tc = tq * 4 + ts_
                self.s(lambda e, ts_=ts_: e.activation(acc[:, ts_, :], acc[:, ts_, :], AF.Silu),
                       reads=acck, writes=acck, tag=f"l3m_si{tq}_{ts_}")
                bk.dma("sync", [(sst6[:, :], sr[tc, :, :])], reads=(f"sD:{tc}",),
                       writes=sst6k, tag=f"l3m_sld{tc}")

                def emit_cb(e, tc=tc, ts_=ts_):
                    e.tensor_tensor(sst6[:, :], sst6[:, :], acc[:, ts_, :], ALU.mult)
                    return e.tensor_tensor(self.xTM[:, tc, :], self.xTM[:, tc, :],
                                           sst6[:, :], ALU.add)
                self.v(emit_cb, reads=acck + sst6k + (f"xTM:{tc}",),
                       writes=(f"xTM:{tc}",) + sst6k, tag=f"l3m_cb{tc}")

    def quant_fp4(self, wnm, R, Ccols, outD, tag):
        bk = self.bk
        A = self.arena
        PC = 1408 if Ccols % 1408 == 0 else Ccols
        while PC > 1408:
            PC //= 2
        npc = Ccols // PC
        nb = PC // 16
        base = 65664
        wv, wk = A.view(base, [128, PC], F32)
        xn_, xnk = A.view(base + 5632, [128, PC], F32)
        aa, aak = A.view(base + 11264, [128, PC], F32)
        sg, sgk = A.view(base + 16896, [128, PC], F32)
        c1, c1k = A.view(base + 22528, [128, PC], F32)
        c2, c2k = A.view(base + 28160, [128, PC], F32)
        tt_, ttk = A.view(base + 33792, [128, PC], F32)
        bm, bmk = A.view(base + 39424, [128, 1408 // 16], F32)
        rc, rck = A.view(base + 40064, [128, 1408 // 16], F32)
        stg, stgk = A.view(base + 40704, [128, 128], F32)
        wsrc = self.din[wnm].rearrange("(c p) n -> c p n", p=128)
        for rc_i in range(R // 128):
            for pc_i in range(npc):
                bk.dma("sync", [(wv[:, :], wsrc[rc_i, :, pc_i * PC:pc_i * PC + PC])],
                       writes=wk, tag=f"{tag}ld{rc_i}_{pc_i}")

                def emit_q(e, nb=nb):
                    w3 = wv[:, :].rearrange("p (b i) -> p b i", i=16)
                    e.tensor_reduce(bm[:, 0:nb].unsqueeze(-1), w3, AX, ALU.max,
                                    apply_absolute_value=True)
                    e.tensor_scalar(bm[:, 0:nb], bm[:, 0:nb], 1.0 / 6.0, 1e-12, ALU.mult, ALU.max)
                    e.reciprocal(rc[:, 0:nb], bm[:, 0:nb])
                    rc3 = rc[:, 0:nb].unsqueeze(-1).broadcast_to([128, nb, 16])
                    return e.tensor_tensor(xn_[:, :].rearrange("p (b i) -> p b i", i=16), w3,
                                           rc3, ALU.mult)
                self.v(emit_q, reads=wk, writes=xnk + bmk + rck, tag=f"{tag}n{rc_i}_{pc_i}")

                def emit_ab(e):
                    e.activation(aa[:, :], xn_[:, :], AF.Abs)
                    return e.activation(sg[:, :], xn_[:, :], AF.Sign)
                self.s(emit_ab, reads=xnk, writes=aak + sgk, tag=f"{tag}ab{rc_i}_{pc_i}")

                def emit_st(e, nb=nb):
                    e.tensor_single_scalar(c1[:, :], aa[:, :], 0.25, ALU.is_ge)
                    for mid in (0.75, 1.25, 1.75):
                        e.scalar_tensor_tensor(c1[:, :], aa[:, :], mid, c1[:, :],
                                               ALU.is_ge, ALU.add)
                    e.tensor_single_scalar(c2[:, :], aa[:, :], 2.5, ALU.is_ge)
                    e.scalar_tensor_tensor(c2[:, :], aa[:, :], 3.5, c2[:, :], ALU.is_ge, ALU.add)
                    e.tensor_scalar(tt_[:, :], aa[:, :], 5.0, 2.0, ALU.is_ge, ALU.mult)
                    e.tensor_tensor(c2[:, :], c2[:, :], tt_[:, :], ALU.add)
                    e.scalar_tensor_tensor(c1[:, :], c1[:, :], 0.5, c2[:, :], ALU.mult, ALU.add)
                    e.tensor_tensor(c1[:, :], c1[:, :], sg[:, :], ALU.mult)
                    bm3 = bm[:, 0:nb].unsqueeze(-1).broadcast_to([128, nb, 16])
                    return e.tensor_tensor(c1[:, :].rearrange("p (b i) -> p b i", i=16),
                                           c1[:, :].rearrange("p (b i) -> p b i", i=16),
                                           bm3, ALU.mult)
                self.v(emit_st, reads=aak + sgk + bmk, writes=c1k + c2k + ttk,
                       tag=f"{tag}st{rc_i}_{pc_i}")
                for j in range(PC // 128):
                    pbank = self.pb[6 + j % 2]
                    pk = f"pb{6 + j % 2}"

                    def emit_tr(e, j=j, pbank=pbank):
                        return e.transpose(pbank[:, 0:128], c1[:, j * 128:(j + 1) * 128],
                                           self.ident[:, :])
                    self.mm(emit_tr, reads=c1k + ("consts",), writes=(pk,),
                            tag=f"{tag}tr{rc_i}_{pc_i}_{j}")

                    def emit_ev(e, pbank=pbank):
                        return e.activation(stg[:, :], pbank[:, 0:128], AF.Copy)
                    self.s(emit_ev, reads=(pk,), writes=stgk, tag=f"{tag}te{rc_i}_{pc_i}_{j}")
                    bk.dma("gpsimd",
                           [(outD[pc_i * PC + j * 128: pc_i * PC + (j + 1) * 128,
                                  rc_i * 128:(rc_i + 1) * 128], stg[:, :])],
                           reads=stgk, writes=(f"{tag}D:{pc_i * (PC // 128) + j}:{rc_i}",),
                           tag=f"{tag}sto{rc_i}_{pc_i}_{j}")

    def layer2(self):
        bk = self.bk
        A = self.arena
        xsD = self.dram_scratch("xsD", [2048, T])
        gateD = self.dram_scratch("gateD", [2048, T])
        xcD = self.dram_scratch("xcD", [2048, T])
        mD = self.dram_scratch("mD", [2048, T])
        y2D = self.dram_scratch("y2D", [2048, T])
        browsD = self.dram_scratch("browsD", [16, T])
        crowsD = self.dram_scratch("crowsD", [16, T])
        rsrowD = self.dram_scratch("rsrowD", [2, T])
        wgqD = self.dram_scratch("wgqD", [D, 2816])
        wuqD = self.dram_scratch("wuqD", [D, 2816])
        wdqD = self.dram_scratch("wdqD", [2816, D])
        qD = self.dscr["qD"]; kD = self.dscr["kD"]; vD = self.dscr["vD"]
        sD = self.dscr["sD"]; hD = self.dscr["hD"]
        sr = sD.rearrange("(tc p) d -> tc p d", p=128)

        self.quant_fp4("w2gate", 2816, D, wgqD, "l2wg")
        self.quant_fp4("w2up", 2816, D, wuqD, "l2wu")
        self.quant_fp4("w2down", D, 2816, wdqD, "l2wd")

        self.ln_stats("l2ln")
        self.build_xnFM("l2fm")

        wbuf, wkeys = A.view(65664, [128, 2, ND, 128], F32)
        h1 = len(wkeys) // 2
        wbk = [wkeys[:h1], wkeys[h1:]] if h1 else [wkeys, wkeys]
        stg2 = [A.view(73856, [128, 512], F32), A.view(75904, [128, 512], F32)]
        w2in = self.din["W2inT"]

        def fm_region(e0, nchunks, dstD, dst_row0, tag):
            for ci in range(nchunks):
                ec = e0 + ci * 128
                eb = ci % 2
                bk.dma("sync", [(wbuf[:, eb, :, :],
                                 w2in[:, ec:ec + 128].rearrange("(c p) e -> p c e", p=128))],
                       writes=wbk[eb], tag=f"{tag}w{ci}")
                for tq in range(4):
                    pbank = self.pb[tq % 2]
                    pk = f"pb{tq % 2}"

                    def emit_mm(e, tq=tq, eb=eb, pbank=pbank):
                        return [e.matmul(pbank[:, :], wbuf[:, eb, dc, :],
                                         self.xnFM[:, dc, 1 + tq * 512:1 + tq * 512 + 512],
                                         start=(dc == 0), stop=(dc == ND - 1))
                                for dc in range(ND)]
                    self.mm(emit_mm, reads=wbk[eb] + self.xnFM_keys, writes=(pk,),
                            tag=f"{tag}mm{ci}_{tq}")
                    sb_, sbk = stg2[tq % 2]

                    def emit_ev(e, pbank=pbank, sb_=sb_):
                        return e.activation(sb_[:, :], pbank[:, :], AF.Copy)
                    self.s(emit_ev, reads=(pk,), writes=sbk, tag=f"{tag}ev{ci}_{tq}")
                    bk.dma("gpsimd",
                           [(dstD[dst_row0 + ci * 128: dst_row0 + (ci + 1) * 128,
                                  tq * 512:tq * 512 + 512], sb_[:, :])],
                           reads=sbk, writes=(f"{tag}D:{ci}:{tq}",), tag=f"{tag}st{ci}_{tq}")

        fm_region(0, 8, qD, 0, "l2q")
        fm_region(1024, 2, kD, 0, "l2k")
        fm_region(1792, 16, xsD, 0, "l2xs")
        fm_region(3840, 16, gateD, 0, "l2gt")
        wbufv, wvk = A.view(77952, [128, ND, 512], F32)
        bk.dma("sync", [(wbufv[:, :, :],
                         w2in[:, 1280:1792].rearrange("(c p) e -> p c e", p=128))],
               writes=wvk, tag="l2v_w")
        for tc in range(NT):
            pbank = self.pb[2 + tc % 2]
            pk = f"pb{2 + tc % 2}"

            def emit_mm(e, tc=tc, pbank=pbank):
                return [e.matmul(pbank[:, :], self.xnFM[:, dc, 1 + tc * 128:1 + tc * 128 + 128],
                                 wbufv[:, dc, :], start=(dc == 0), stop=(dc == ND - 1))
                        for dc in range(ND)]
            self.mm(emit_mm, reads=wvk + self.xnFM_keys, writes=(pk,), tag=f"l2v_mm{tc}")
            sb_, sbk = stg2[tc % 2]

            def emit_ev(e, pbank=pbank, sb_=sb_):
                return e.activation(sb_[:, :], pbank[:, :], AF.Copy)
            self.s(emit_ev, reads=(pk,), writes=sbk, tag=f"l2v_ev{tc}")
            bk.dma("gpsimd", [(vD[tc * 128:(tc + 1) * 128, 0:512], sb_[:, :])],
                   reads=sbk, writes=(f"l2vD:{tc}",), tag=f"l2v_st{tc}")

        ssp, sspk = A.view(0, [128, NT, 96], F32)
        xs_sl, xsk = A.view(8192, [128, 3 + T], F32)
        xc_sl, xck = A.view(16416, [128, T], F32)
        acc_sl, acck = A.view(24608, [128, T], F32)
        cw, cwk = A.view(32800, [128, 16, 4], F32)
        cb, cbk = A.view(33056, [128, 16], F32)
        wxsl, wxk = A.view(33120, [128, 96], F32)
        bk.dma("sync", [(cw[:, :, :], self.din["convw"]), (cb[:, :], self.din["convb"])],
               writes=cwk + cbk, tag="l2c_w")
        self.v(lambda e: e.memset(ssp[:, :, :], 0.0), writes=sspk, tag="l2c_z")
        self.v(lambda e: e.memset(xs_sl[:, 0:3], 0.0), writes=xsk, tag="l2c_zp")
        for sl in range(16):
            bk.dma("sync", [(xs_sl[:, 3:3 + T], xsD[sl * 128:(sl + 1) * 128, :]),
                            (wxsl[:, :], self.din["W2xT"][sl * 128:(sl + 1) * 128, :])],
                   reads=tuple(f"l2xsD:{sl}:{tq}" for tq in range(4)),
                   writes=xsk + wxk, tag=f"l2c_ld{sl}")

            def emit_conv(e, sl=sl):
                e.tensor_scalar(acc_sl[:, :], xs_sl[:, 0:T], cw[:, sl, 0:1], None, ALU.mult)
                for j in range(1, 4):
                    e.scalar_tensor_tensor(acc_sl[:, :], xs_sl[:, j:j + T], cw[:, sl, j:j + 1],
                                           acc_sl[:, :], ALU.mult, ALU.add)
                return acc_sl and e.tensor_copy(acc_sl[:, 0:1], acc_sl[:, 0:1])
            self.v(emit_conv, reads=xsk + cwk, writes=acck, tag=f"l2c_cv{sl}")

            def emit_silu(e, sl=sl):
                return e.activation(xc_sl[:, :], acc_sl[:, :], AF.Silu, bias=cb[:, sl:sl + 1])
            self.s(emit_silu, reads=acck + cbk, writes=xck, tag=f"l2c_si{sl}")
            bk.dma("gpsimd", [(xcD[sl * 128:(sl + 1) * 128, :], xc_sl[:, :])],
                   reads=xck, writes=(f"xcD:{sl}",), tag=f"l2c_st{sl}")
            for tc in range(NT):
                pbank = self.pb[4 + tc % 2]
                pk = f"pb{4 + tc % 2}"

                def emit_mm(e, tc=tc, pbank=pbank):
                    return e.matmul(pbank[:, 0:96], xc_sl[:, tc * 128:tc * 128 + 128],
                                    wxsl[:, :], start=True, stop=True)
                self.mm(emit_mm, reads=xck + wxk, writes=(pk,), tag=f"l2c_pm{sl}_{tc}")

                def emit_add(e, tc=tc, pbank=pbank):
                    return e.tensor_tensor(ssp[:, tc, :], ssp[:, tc, :], pbank[:, 0:96], ALU.add)
                self.v(emit_add, reads=(pk,) + sspk, writes=sspk, tag=f"l2c_pa{sl}_{tc}")

        dtlw, dtlwk = A.view(39072, [128, 64], F32)
        blw, blwk = A.view(39328, [128, 16], F32)
        clw, clwk = A.view(39392, [128, 16], F32)
        bk.dma("gpsimd",
               [(dtlw[:, :], self.din["dtlnw"].partition_broadcast(128).squeeze(1)),
                (blw[:, :], self.din["Blnw"].partition_broadcast(128).squeeze(1)),
                (clw[:, :], self.din["Clnw"].partition_broadcast(128).squeeze(1))],
               writes=dtlwk + blwk + clwk, tag="l2p3_lw")
        dtn, dtnk = A.view(40096, [128, NT, 64], F32)
        btok, btokk = A.view(44192, [128, NT, 16], F32)
        ctok, ctokk = A.view(45216, [128, NT, 16], F32)
        EPS2 = 1e-6
        for tc in range(NT):
            for (nm, off, w_, rep, repk, dst, dstk) in (
                    ("dt", 0, 64, dtlw, dtlwk, dtn, dtnk),
                    ("B", 64, 16, blw, blwk, btok, btokk),
                    ("C", 80, 16, clw, clwk, ctok, ctokk)):
                def emit_s2(e, off=off, w_=w_, tc=tc):
                    return e.activation(self.sqdump[:, 0:w_], ssp[:, tc, off:off + w_],
                                        AF.Square, accum_out=self.sm[:, 48:49])
                self.s(emit_s2, reads=sspk, writes=("sm48", "sqdump"), tag=f"l2p3_{nm}s{tc}")

                def emit_rs(e, w_=w_):
                    e.tensor_scalar(self.sm[:, 48:49], self.sm[:, 48:49], 1.0 / w_, EPS2,
                                    ALU.mult, ALU.add)
                    return e.reciprocal(self.sm[:, 48:49], self.sm[:, 48:49])
                self.v(emit_rs, reads=("sm48",), writes=("sm48",), tag=f"l2p3_{nm}r{tc}")
                self.s(lambda e: e.activation(self.sm[:, 48:49], self.sm[:, 48:49], AF.Sqrt),
                       reads=("sm48",), writes=("sm48",), tag=f"l2p3_{nm}q{tc}")

                def emit_ap(e, off=off, w_=w_, tc=tc, rep=rep, dst=dst):
                    e.tensor_scalar(self.sqdump[:, 0:w_], ssp[:, tc, off:off + w_],
                                    self.sm[:, 48:49], None, ALU.mult)
                    return e.tensor_tensor(dst[:, tc, :], self.sqdump[:, 0:w_], rep[:, 0:w_],
                                           ALU.mult)
                self.v(emit_ap, reads=sspk + ("sm48",) + repk, writes=dstk + ("sqdump",),
                       tag=f"l2p3_{nm}a{tc}")
        dtfm, dtfmk = A.view(46240, [128, T], F32)
        bct, bctk = A.view(54432, [128, 2, T], F32)
        for tc in range(NT):
            def emit_tr(e, tc=tc):
                return e.transpose(self.pb[6][0:64, 0:128], dtn[:, tc, :], self.ident[:, :])
            self.mm(emit_tr, reads=dtnk + ("consts",), writes=("pb6",), tag=f"l2p3_t{tc}")

            def emit_te(e, tc=tc):
                return e.activation(dtfm[0:64, tc * 128:tc * 128 + 128],
                                    self.pb[6][0:64, 0:128], AF.Copy)
            self.s(emit_te, reads=("pb6",), writes=dtfmk, tag=f"l2p3_te{tc}")

            def emit_tr2(e, tc=tc):
                e.transpose(self.pb[7][0:16, 0:128], btok[:, tc, :], self.ident[:, :])
                return e.transpose(self.pb[7][0:16, 128:256], ctok[:, tc, :], self.ident[:, :])
            self.mm(emit_tr2, reads=btokk + ctokk + ("consts",), writes=("pb7",),
                    tag=f"l2p3_t2{tc}")

            def emit_te2(e, tc=tc):
                e.activation(bct[0:16, 0, tc * 128:tc * 128 + 128], self.pb[7][0:16, 0:128],
                             AF.Copy)
                return e.activation(bct[0:16, 1, tc * 128:tc * 128 + 128],
                                    self.pb[7][0:16, 128:256], AF.Copy)
            self.s(emit_te2, reads=("pb7",), writes=bctk, tag=f"l2p3_te2{tc}")
        self.v(lambda e: e.tensor_scalar(bct[0:16, 0, :], bct[0:16, 0, :], -1.0, None, ALU.mult),
               reads=bctk, writes=bctk, tag="l2p3_negb")
        bk.dma("gpsimd", [(browsD, bct[0:16, 0, :]), (crowsD, bct[0:16, 1, :])],
               reads=bctk, writes=("brows", "crows"), tag="l2p3_spl")

        wdt, wdtk = A.view(70816, [128, T], F32)
        negb, negbk = A.view(79008, [128, 16], F32)
        msl, mslk = A.view(79072, [128, 512], F32)
        bk.dma("sync", [(wdt[0:64, :], self.din["W2dtT"]), (negb[:, :], self.din["bdt"])],
               writes=wdtk + negbk, tag="l2p4_w")
        self.v(lambda e: e.tensor_scalar(negb[:, :], negb[:, :], -1.0, None, ALU.mult),
               reads=negbk, writes=negbk, tag="l2p4_neg")
        for sl in range(16):
            for tq in range(4):
                pbank = self.pb[tq % 2]
                pk = f"pb{tq % 2}"

                def emit_mm(e, sl=sl, tq=tq, pbank=pbank):
                    return e.matmul(pbank[:, :], wdt[0:64, sl * 128:sl * 128 + 128],
                                    dtfm[0:64, tq * 512:tq * 512 + 512], start=True, stop=True)
                self.mm(emit_mm, reads=wdtk + dtfmk, writes=(pk,), tag=f"l2p4_mm{sl}_{tq}")

                def emit_sig(e, sl=sl, pbank=pbank):
                    e.activation(msl[:, :], pbank[:, :], AF.Sigmoid, bias=negb[:, sl:sl + 1],
                                 scale=-1.0)
                    return e.activation(msl[:, :], msl[:, :], AF.Ln)
                self.s(emit_sig, reads=(pk,) + negbk, writes=mslk, tag=f"l2p4_sg{sl}_{tq}")
                bk.dma("gpsimd", [(mD[sl * 128:(sl + 1) * 128, tq * 512:tq * 512 + 512],
                                   msl[:, :])],
                       reads=mslk, writes=(f"mD:{sl}:{tq}",), tag=f"l2p4_st{sl}_{tq}")

        anse, ansek = A.view(0, [128, 16, 16], F32)
        alog, alogk = A.view(1024, [128, 16, 16], F32)
        dpar, dpark = A.view(2048, [128, 16], F32)
        gl2w, gl2wk = A.view(2112, [128, 16], F32)
        bk.dma("sync", [(alog[:, :, :], self.din["AlogC"]), (dpar[:, :], self.din["DparC"]),
                        (gl2w[:, :], self.din["ln2wC"])],
               writes=alogk + dpark + gl2wk, tag="l2p5_w")
        self.s(lambda e: e.activation(anse[:, :, :], alog[:, :, :], AF.Exp),
               reads=alogk, writes=ansek, tag="l2p5_exp")
        m_s, m_sk = A.view(8192, [128, T], F32)
        u_s, u_sk = A.view(16384, [128, T], F32)
        xc2, xc2k = A.view(24576, [128, T], F32)
        Gt, Gtk = A.view(32768, [128, T], F32)
        bt_, btk_ = A.view(40960, [128, T], F32)
        yp, ypk = A.view(49152, [128, T], F32)
        brep = [A.view(57344, [128, T], F32), A.view(65536, [128, T], F32)]
        crep = [A.view(73728, [128, T], F32), A.view(81920, [128, T], F32)]
        at_, atk_ = A.view(90112, [128, T], F32)
        gsl, gslk = A.view(98304, [128, T], F32)
        sq2, sq2k = A.view(106496, [128, 512], F32)
        sums2, sums2k = A.view(108544, [1, T], F32)
        self.v(lambda e: e.memset(sums2[:, :], 0.0), writes=sums2k, tag="l2p5_zs")
        for sl in range(16):
            bk.dma("sync", [(m_s[:, :], mD[sl * 128:(sl + 1) * 128, :]),
                            (xc2[:, :], xcD[sl * 128:(sl + 1) * 128, :]),
                            (gsl[:, :], gateD[sl * 128:(sl + 1) * 128, :])],
                   reads=tuple(f"mD:{sl}:{tq}" for tq in range(4)) + (f"xcD:{sl}",) +
                         tuple(f"l2gtD:{sl}:{tq}" for tq in range(4)),
                   writes=m_sk + xc2k + gslk, tag=f"l2p5_ld{sl}")

            def emit_u(e):
                return e.tensor_tensor(u_s[:, :], m_s[:, :], xc2[:, :], ALU.mult)
            self.v(emit_u, reads=m_sk + xc2k, writes=u_sk, tag=f"l2p5_u{sl}")
            self.v(lambda e: e.memset(yp[:, :], 0.0), writes=ypk, tag=f"l2p5_zy{sl}")
            for n in range(16):
                nb_ = n % 2
                brp, brpk = brep[nb_]
                crp, crpk = crep[nb_]
                bk.dma("sync", [(brp[:, :], browsD[n:n + 1, :].partition_broadcast(128).squeeze(1)),
                                (crp[:, :], crowsD[n:n + 1, :].partition_broadcast(128).squeeze(1))],
                       reads=("brows", "crows"), writes=brpk + crpk, tag=f"l2p5_br{sl}_{n}")

                def emit_a(e, sl=sl, n=n):
                    return e.activation(at_[:, :], m_s[:, :], AF.Exp,
                                        scale=anse[:, sl, n:n + 1])
                self.s(emit_a, reads=m_sk + ansek, writes=atk_, tag=f"l2p5_a{sl}_{n}")

                def emit_b(e, brp=brp):
                    return e.tensor_tensor(bt_[:, :], u_s[:, :], brp[:, :], ALU.mult)
                self.v(emit_b, reads=u_sk + brpk, writes=btk_, tag=f"l2p5_b{sl}_{n}")

                def emit_scan(e):
                    return e.tensor_tensor_scan(Gt[:, :], at_[:, :], bt_[:, :], 0.0,
                                                ALU.mult, ALU.add)
                self.v(emit_scan, reads=atk_ + btk_, writes=Gtk, tag=f"l2p5_s{sl}_{n}")

                def emit_ro(e, crp=crp):
                    return e.tensor_tensor(Gt[:, :], Gt[:, :], crp[:, :], ALU.mult)
                self.v(emit_ro, reads=Gtk + crpk, writes=Gtk, tag=f"l2p5_c{sl}_{n}")

                def emit_acc(e):
                    return e.tensor_tensor(yp[:, :], yp[:, :], Gt[:, :], ALU.add)
                self.bk.t("gpsimd", emit_acc, reads=Gtk + ypk, writes=ypk,
                          tag=f"l2p5_y{sl}_{n}")
            def emit_y2(e, sl=sl):
                return e.scalar_tensor_tensor(yp[:, :], xc2[:, :], dpar[:, sl:sl + 1],
                                              yp[:, :], ALU.mult, ALU.add)
            self.v(emit_y2, reads=xc2k + dpark + ypk, writes=ypk, tag=f"l2p5_y2{sl}")
            self.s(lambda e: e.activation(gsl[:, :], gsl[:, :], AF.Silu),
                   reads=gslk, writes=gslk, tag=f"l2p5_sg{sl}")

            def emit_y3(e):
                return e.tensor_tensor(yp[:, :], yp[:, :], gsl[:, :], ALU.mult)
            self.v(emit_y3, reads=ypk + gslk, writes=ypk, tag=f"l2p5_y3{sl}")
            bk.dma("gpsimd", [(y2D[sl * 128:(sl + 1) * 128, :], yp[:, :])],
                   reads=ypk, writes=(f"y2D:{sl}",), tag=f"l2p5_st{sl}")
            for tq in range(4):
                def emit_sq(e, tq=tq):
                    return e.activation(sq2[:, :], yp[:, tq * 512:tq * 512 + 512], AF.Square)
                self.s(emit_sq, reads=ypk, writes=sq2k, tag=f"l2p5_q{sl}_{tq}")

                def emit_sm(e, pb_=self.pb[2 + tq % 2]):
                    return e.matmul(pb_[0:1, :], self.onescol[:, :], sq2[:, :],
                                    start=True, stop=True)
                self.mm(emit_sm, reads=sq2k + ("consts2",), writes=(f"pb{2 + tq % 2}",),
                        tag=f"l2p5_m{sl}_{tq}")

                def emit_ad(e, tq=tq, pb_=self.pb[2 + tq % 2]):
                    return e.tensor_tensor(sums2[0:1, tq * 512:tq * 512 + 512],
                                           sums2[0:1, tq * 512:tq * 512 + 512],
                                           pb_[0:1, :], ALU.add)
                self.v(emit_ad, reads=(f"pb{2 + tq % 2}",) + sums2k, writes=sums2k,
                       tag=f"l2p5_sa{sl}_{tq}")
        def emit_rs2(e):
            e.tensor_scalar(sums2[:, :], sums2[:, :], 1.0 / 2048.0, EPS2, ALU.mult, ALU.add)
            return e.reciprocal(sums2[:, :], sums2[:, :])
        self.v(emit_rs2, reads=sums2k, writes=sums2k, tag="l2p5_rs2a")
        self.s(lambda e: e.activation(sums2[:, :], sums2[:, :], AF.Sqrt),
               reads=sums2k, writes=sums2k, tag="l2p5_rs2b")
        bk.dma("gpsimd", [(rsrowD[1:2, :], sums2[0:1, :])], reads=sums2k,
               writes=("rsrow2",), tag="l2p5_rsp")

        attn, attnk = A.view(0, [128, 16, 512], F32)
        vpad2, vpad2k = A.view(32768, [128, 16, 128], F32)
        kh2, kh2k = A.view(40960, [128, T], F32)
        qh2, qh2k = A.view(49152, [128, 512], F32)
        prb = [A.view(51200, [128, 512], F32), A.view(53248, [128, 512], F32)]
        prbT, prbTk = A.view(55296, [128, 16, 512], F32)
        cmsk, cmskk = A.view(88064, [128, 4, 512], F32)
        rrow2, rrow2k = A.view(96256, [128, 640], F32)
        rb2, rb2k = A.view(98304, [128, 512], F32)
        sq1, sq1k = A.view(100352, [128, 512], F32)
        sums1, sums1k = A.view(102400, [1, T], F32)
        scol2 = self.sm[:, 16:32]
        r4 = self.sm[:, 48:52]
        bk.dma("sync", [(cmsk[:, :, :], self.din["cmask"].rearrange("s p k -> p s k"))],
               writes=cmskk, tag="l2a_cm")
        self.v(lambda e: e.memset(sums1[:, :], 0.0), writes=sums1k, tag="l2a_zs")
        vsrc2 = vD.rearrange("(kc p) e -> p kc e", p=128)
        qrd = tuple(f"l2qD:{c}:{t_}" for c in range(8) for t_ in range(4))
        krd = tuple(f"l2kD:{c}:{t_}" for c in range(2) for t_ in range(4))
        vrd = tuple(f"l2vD:{tc}" for tc in range(NT))
        for qc in range(4):
            for h in range(16):
                g = h // 4
                bk.dma("sync", [(kh2[0:64, :], kD[g * 64:(g + 1) * 64, :]),
                                (qh2[0:64, :], qD[h * 64:(h + 1) * 64, qc * 512:qc * 512 + 512]),
                                (vpad2[:, :, :], vsrc2[:, :, g * 128:(g + 1) * 128])],
                       reads=qrd + krd + vrd, writes=kh2k + qh2k + vpad2k,
                       tag=f"l2a_ld{qc}_{h}")
                for i in range(4):
                    tq = qc * 4 + i
                    nks = qc + 1
                    for ks in range(nks):
                        pbank = self.pb[ks % 2]
                        pk = f"pb{ks % 2}"

                        def emit_s(e, i=i, ks=ks, pbank=pbank):
                            return e.matmul(pbank[:, :], qh2[0:64, i * 128:(i + 1) * 128],
                                            kh2[0:64, ks * 512:ks * 512 + 512],
                                            start=True, stop=True)
                        self.mm(emit_s, reads=qh2k + kh2k, writes=(pk,),
                                tag=f"l2a_s{qc}_{h}_{i}_{ks}")
                        pr, prk = prb[ks % 2]
                        if ks < qc:
                            def emit_e(e, ks=ks, pbank=pbank, pr=pr):
                                return e.activation(pr[:, :], pbank[:, :], AF.Exp, scale=0.125,
                                                    accum_out=scol2[:, ks:ks + 1])
                            self.s(emit_e, reads=(pk,), writes=prk + (f"sm{16 + ks}",),
                                   tag=f"l2a_e{qc}_{h}_{i}_{ks}")
                        else:
                            def emit_e(e, pbank=pbank, pr=pr):
                                return e.activation(pr[:, :], pbank[:, :], AF.Exp, scale=0.125)
                            self.s(emit_e, reads=(pk,), writes=prk, tag=f"l2a_e{qc}_{h}_{i}_{ks}")

                            def emit_msk(e, i=i, ks=ks, pr=pr):
                                return e.scalar_tensor_tensor(pr[:, :], pr[:, :], 1.0,
                                                              cmsk[:, i, :], ALU.mult, ALU.mult,
                                                              accum_out=scol2[:, ks:ks + 1])
                            self.v(emit_msk, reads=prk + cmskk, writes=prk + (f"sm{16 + ks}",),
                                   tag=f"l2a_m{qc}_{h}_{i}_{ks}")
                        for j in range(4):
                            pt = self.pb[2 + j % 2]
                            ptk = f"pb{2 + j % 2}"

                            def emit_tr(e, j=j, pt=pt, pr=pr):
                                return e.transpose(pt[:, 0:128], pr[:, j * 128:(j + 1) * 128],
                                                   self.ident[:, :])
                            self.mm(emit_tr, reads=prk + ("consts",), writes=(ptk,),
                                    tag=f"l2a_t{qc}_{h}_{i}_{ks}_{j}")

                            def emit_te(e, i=i, ks=ks, j=j, pt=pt):
                                return e.activation(prbT[:, ks * 4 + j, i * 128:(i + 1) * 128],
                                                    pt[:, 0:128], AF.Copy)
                            self.s(emit_te, reads=(ptk,), writes=prbTk,
                                   tag=f"l2a_tev{qc}_{h}_{i}_{ks}_{j}")
                    def emit_rs(e, i=i, nks=nks):
                        e.tensor_reduce(self.sm[:, 52:53], scol2[:, 0:nks], AX, ALU.add)
                        return e.reciprocal(r4[:, i:i + 1], self.sm[:, 52:53])
                    self.v(emit_rs, reads=tuple(f"sm{16 + k_}" for k_ in range(nks)),
                           writes=("sm52", "smr4"), tag=f"l2a_rs{qc}_{h}_{i}")
                def emit_t4(e):
                    return e.transpose(self.pb[6][0:4, 0:128], r4[:, :], self.ident[:, :])
                self.mm(emit_t4, reads=("smr4", "consts"), writes=("pb6",), tag=f"l2a_t4{qc}_{h}")

                def emit_t4e(e):
                    return e.activation(rrow2[0:4, 512:640], self.pb[6][0:4, 0:128], AF.Copy)
                self.s(emit_t4e, reads=("pb6",), writes=rrow2k, tag=f"l2a_t4e{qc}_{h}")
                bk.dma("gpsimd", [(rrow2[0:1, 0:512].rearrange("p (a b) -> p a b", a=4),
                                   rrow2[0:4, 512:640].unsqueeze(1))],
                       reads=rrow2k, writes=rrow2k, tag=f"l2a_mv{qc}_{h}")

                def emit_bc(e):
                    return e.matmul(self.pb[5][:, :], self.ones1[:, :], rrow2[0:1, 0:512],
                                    start=True, stop=True)
                self.mm(emit_bc, reads=rrow2k + ("consts",), writes=("pb5",),
                        tag=f"l2a_bc{qc}_{h}")

                def emit_rbe(e):
                    return e.activation(rb2[:, :], self.pb[5][:, :], AF.Copy)
                self.s(emit_rbe, reads=("pb5",), writes=rb2k, tag=f"l2a_rbe{qc}_{h}")
                for kc in range(4 * (qc + 1)):
                    def emit_av(e, kc=kc):
                        return e.matmul(self.pb[4][:, :], vpad2[:, kc, :], prbT[:, kc, :],
                                        start=(kc == 0), stop=(kc == 4 * (qc + 1) - 1))
                    self.mm(emit_av, reads=prbTk + vpad2k, writes=("pb4",),
                            tag=f"l2a_av{qc}_{h}_{kc}")

                def emit_at(e, h=h):
                    return e.tensor_tensor(attn[:, h, :], self.pb[4][:, :], rb2[:, :], ALU.mult)
                self.v(emit_at, reads=("pb4",) + rb2k, writes=attnk, tag=f"l2a_at{qc}_{h}")

                def emit_sq(e, h=h):
                    return e.activation(sq1[:, :], attn[:, h, :], AF.Square)
                self.s(emit_sq, reads=attnk, writes=sq1k, tag=f"l2a_q{qc}_{h}")

                def emit_sm1(e):
                    return e.matmul(self.pb[7][0:1, :], self.onescol[:, :], sq1[:, :],
                                    start=True, stop=True)
                self.mm(emit_sm1, reads=sq1k + ("consts2",), writes=("pb7",),
                        tag=f"l2a_sm{qc}_{h}")

                def emit_sa(e, qc=qc):
                    return e.tensor_tensor(sums1[0:1, qc * 512:qc * 512 + 512],
                                           sums1[0:1, qc * 512:qc * 512 + 512],
                                           self.pb[7][0:1, :], ALU.add)
                self.v(emit_sa, reads=("pb7",) + sums1k, writes=sums1k, tag=f"l2a_sa{qc}_{h}")

            if qc == 0:
                bk.tap("l2_attn0", attn[:, :, :], reads=attnk)
            if qc == 1:
                bk.tap("l2_attn1", attn[:, :, :], reads=attnk)
            def emit_rs1(e, qc=qc):
                e.tensor_scalar(sums1[0:1, qc * 512:qc * 512 + 512],
                                sums1[0:1, qc * 512:qc * 512 + 512], 1.0 / 2048.0, EPS2,
                                ALU.mult, ALU.add)
                return e.reciprocal(sums1[0:1, qc * 512:qc * 512 + 512],
                                    sums1[0:1, qc * 512:qc * 512 + 512])
            self.v(emit_rs1, reads=sums1k, writes=sums1k, tag=f"l2b_rs{qc}")
            self.s(lambda e, qc=qc: e.activation(sums1[0:1, qc * 512:qc * 512 + 512],
                                                 sums1[0:1, qc * 512:qc * 512 + 512], AF.Sqrt),
                   reads=sums1k, writes=sums1k, tag=f"l2b_rq{qc}")
            bk.dma("gpsimd", [(rsrowD[0:1, qc * 512:qc * 512 + 512],
                               sums1[0:1, qc * 512:qc * 512 + 512])],
                   reads=sums1k, writes=("rsrow1",), tag=f"l2b_sp{qc}")
            rs1b, rs1bk = A.view(65536, [128, 512], F32)
            rs2b, rs2bk = A.view(67584, [128, 512], F32)
            y2q, y2qk = A.view(32768, [128, 16, 512], F32)
            bk.dma("sync", [(rs1b[:, :], rsrowD[0:1, qc * 512:qc * 512 + 512]
                             .partition_broadcast(128).squeeze(1)),
                            (rs2b[:, :], rsrowD[1:2, qc * 512:qc * 512 + 512]
                             .partition_broadcast(128).squeeze(1)),
                            (y2q[:, :, :], y2D.rearrange("(sl p) t -> p sl t", p=128)
                             [:, :, qc * 512:qc * 512 + 512])],
                   reads=("rsrow1", "rsrow2") + tuple(f"y2D:{sl}" for sl in range(16)),
                   writes=rs1bk + rs2bk + y2qk, tag=f"l2b_ld{qc}")
            l1w, l1wk = A.view(69632, [128, 16], F32)
            l2w, l2wk = A.view(69696, [128, 16], F32)
            bk.dma("sync", [(l1w[:, :], self.din["ln1wC"]),
                            (l2w[:, :], self.din["ln2wC"])],
                   writes=l1wk + l2wk, tag=f"l2b_l1w{qc}")

            def emit_half(e):
                e.tensor_scalar(l1w[:, :], l1w[:, :], 0.5, None, ALU.mult)
                return e.tensor_scalar(l2w[:, :], l2w[:, :], 0.5, None, ALU.mult)
            self.v(emit_half, reads=l1wk + l2wk, writes=l1wk + l2wk, tag=f"l2b_hf{qc}")
            for c in range(16):
                def emit_cb(e, c=c):
                    e.scalar_tensor_tensor(sq1[:, :], attn[:, c, :], l1w[:, c:c + 1],
                                           rs1b[:, :], ALU.mult, ALU.mult)
                    e.scalar_tensor_tensor(attn[:, c, :], y2q[:, c, :], l2w[:, c:c + 1],
                                           rs2b[:, :], ALU.mult, ALU.mult)
                    return e.tensor_tensor(attn[:, c, :], attn[:, c, :], sq1[:, :], ALU.add)
                self.v(emit_cb, reads=attnk + y2qk + rs1bk + rs2bk + l1wk + l2wk + sq1k,
                       writes=attnk + sq1k, tag=f"l2b_cb{qc}_{c}")
            if qc == 0:
                bk.tap("l2_comb0", attn[:, :, :], reads=attnk)
            wob, wobk = A.view(70656, [128, 2, 1024], F32)
            w2osrc = self.din["W2outT"].rearrange("(c p) e -> c p e", p=128)
            for c in range(16):
                cb_ = c % 2
                bk.dma("sync", [(wob[:, cb_, :], w2osrc[c, :, :])],
                       writes=(f"wob{cb_}",), tag=f"l2b_w{qc}_{c}")
                for ts_ in range(4):
                    for eo in range(2):
                        pbank = self.pb[ts_ * 2 + eo]
                        pk = f"pb{ts_ * 2 + eo}"

                        def emit_mm(e, c=c, cb_=cb_, ts_=ts_, eo=eo, pbank=pbank):
                            return e.matmul(pbank[:, :], attn[:, c, ts_ * 128:ts_ * 128 + 128],
                                            wob[:, cb_, eo * 512:eo * 512 + 512],
                                            start=(c == 0), stop=(c == 15))
                        self.mm(emit_mm, reads=(f"wob{cb_}",) + attnk, writes=(pk,),
                                tag=f"l2b_mm{qc}_{c}_{ts_}_{eo}")
            stg3, stg3k = A.view(78848, [128, 512], F32)
            for ts_ in range(4):
                for eo in range(2):
                    pbank = self.pb[ts_ * 2 + eo]
                    pk = f"pb{ts_ * 2 + eo}"

                    def emit_ev(e, pbank=pbank):
                        return e.activation(stg3[:, :], pbank[:, :], AF.Copy)
                    self.s(emit_ev, reads=(pk,), writes=stg3k, tag=f"l2b_ev{qc}_{ts_}_{eo}")
                    tc = qc * 4 + ts_
                    bk.dma("gpsimd", [(sr[tc, :, eo * 512:eo * 512 + 512], stg3[:, :])],
                           reads=stg3k, writes=(f"sD:{tc}",), tag=f"l2b_st{qc}_{ts_}_{eo}")

        self.l2_ffn(wgqD, wuqD, wdqD, sD)

    def fp4_act_quant(self, x, xk, W, tmps, tag):
        (sgn, sgnk), (aa, aak), (c2, c2k), (tt_, ttk), (bm, bmk), (rc, rck) = tmps
        nb = W // 16

        def emit_n(e, nb=nb):
            x3 = x.rearrange("p (b i) -> p b i", i=16)
            e.tensor_reduce(bm[:, 0:nb].unsqueeze(-1), x3, AX, ALU.max,
                            apply_absolute_value=True)
            e.tensor_scalar(bm[:, 0:nb], bm[:, 0:nb], 1.0 / 6.0, 1e-12, ALU.mult, ALU.max)
            e.reciprocal(rc[:, 0:nb], bm[:, 0:nb])
            rc3 = rc[:, 0:nb].unsqueeze(-1).broadcast_to([128, nb, 16])
            return e.tensor_tensor(x3, x3, rc3, ALU.mult)
        self.v(emit_n, reads=xk, writes=xk + bmk + rck, tag=f"{tag}n")

        def emit_ab(e):
            e.activation(sgn[:, 0:W], x, AF.Sign)
            return e.activation(aa[:, 0:W], x, AF.Abs)
        self.s(emit_ab, reads=xk, writes=sgnk + aak, tag=f"{tag}ab")

        def emit_st(e, nb=nb):
            a_ = aa[:, 0:W]
            e.tensor_single_scalar(x, a_, 0.25, ALU.is_ge)
            for mid in (0.75, 1.25, 1.75):
                e.scalar_tensor_tensor(x, a_, mid, x, ALU.is_ge, ALU.add)
            e.tensor_single_scalar(c2[:, 0:W], a_, 2.5, ALU.is_ge)
            e.scalar_tensor_tensor(c2[:, 0:W], a_, 3.5, c2[:, 0:W], ALU.is_ge, ALU.add)
            e.tensor_scalar(tt_[:, 0:W], a_, 5.0, 2.0, ALU.is_ge, ALU.mult)
            e.tensor_tensor(c2[:, 0:W], c2[:, 0:W], tt_[:, 0:W], ALU.add)
            e.scalar_tensor_tensor(x, x, 0.5, c2[:, 0:W], ALU.mult, ALU.add)
            e.tensor_tensor(x, x, sgn[:, 0:W], ALU.mult)
            bm3 = bm[:, 0:nb].unsqueeze(-1).broadcast_to([128, nb, 16])
            return e.tensor_tensor(x.rearrange("p (b i) -> p b i", i=16),
                                   x.rearrange("p (b i) -> p b i", i=16), bm3, ALU.mult)
        self.v(emit_st, reads=aak + sgnk + bmk, writes=xk + c2k + ttk, tag=f"{tag}st")

    def l2_ffn(self, wgqD, wuqD, wdqD, sD):
        bk = self.bk
        A = self.arena
        sr = sD.rearrange("(tc p) d -> tc p d", p=128)
        I2 = 2816
        xq, xqk = A.view(0, [128, D], F32)
        xqfm, xqfmk = A.view(4096, [128, ND, 256], F32)
        hbuf, hbk = A.view(12288, [128, 2, I2], F32)
        sgn, sgnk = A.view(34816, [128, I2], F32)
        aa, aak = A.view(46080, [128, I2], F32)
        c2, c2k = A.view(57344, [128, I2], F32)
        tt_, ttk = A.view(68608, [128, I2], F32)
        bm, bmk = A.view(79872, [128, 176], F32)
        rc, rck = A.view(80576 - 64 if False else 80608, [128, 176], F32)
        hqfm, hqfmk = A.view(81344, [128, 22, 256], F32)
        wpc, wpck = A.view(103872, [128, ND, 512], F32)
        wdp, wdpk = A.view(103872, [128, D], F32)
        gst, gstk = A.view(112128, [128, 512], F32)
        sst, sstk = A.view(108032, [128, 1024], F32)
        tmps = ((sgn, sgnk), (aa, aak), (c2, c2k), (tt_, ttk), (bm, bmk), (rc, rck))
        gbuf, gbufk = A.view(81344, [128, 512], F32)

        ipieces = [(0, 512), (512, 512), (1024, 512), (1536, 512), (2048, 512), (2560, 256)]
        for tq8 in range(8):
            for half in range(2):
                tc = tq8 * 2 + half
                self.xn_tile(xq[:, :], xqk, tc, f"l2f_xn{tc}")
                self.fp4_act_quant(xq[:, :], xqk, D, tmps, f"l2f_xq{tc}")
                for grp in range(0, ND, 4):
                    pt = self.pb[2 + (grp // 4) % 2]
                    ptk = f"pb{2 + (grp // 4) % 2}"

                    def emit_tr(e, grp=grp, pt=pt):
                        return [e.transpose(pt[:, j * 128:(j + 1) * 128],
                                            xq[:, (grp + j) * 128:(grp + j + 1) * 128],
                                            self.ident[:, :]) for j in range(4)]
                    self.mm(emit_tr, reads=xqk + ("consts",), writes=(ptk,),
                            tag=f"l2f_tr{tc}_{grp}")

                    def emit_te(e, grp=grp, half=half, pt=pt):
                        return e.activation(xqfm[:, grp:grp + 4, half * 128:half * 128 + 128],
                                            pt[:, :].rearrange("p (j c) -> p j c", c=128),
                                            AF.Copy)
                    self.s(emit_te, reads=(ptk,), writes=xqfmk, tag=f"l2f_te{tc}_{grp}")
            for wD_, wtag, sil in ((wgqD, "g", True), (wuqD, "u", False)):
                for pi, (i0, iw) in enumerate(ipieces):
                    bk.dma("sync", [(wpc[:, :, 0:iw],
                                     wD_.rearrange("(c p) i -> p c i", p=128)[:, :, i0:i0 + iw])],
                           reads=tuple(f"l2w{wtag}D:{c}:{r}" for c in range(D // 128)
                                       for r in range(22)) if False else
                                 tuple(f"l2w{wtag}D:{cc}:{rr}" for cc in range(8) for rr in range(22)),
                           writes=wpck, tag=f"l2f_{wtag}w{tq8}_{pi}")
                    for half in range(2):
                        pbank = self.pb[half]
                        pk = f"pb{half}"

                        def emit_mm(e, half=half, iw=iw, pbank=pbank):
                            return [e.matmul(pbank[:, 0:iw],
                                             xqfm[:, dc, half * 128:half * 128 + 128],
                                             wpc[:, dc, 0:iw], start=(dc == 0),
                                             stop=(dc == ND - 1)) for dc in range(ND)]
                        self.mm(emit_mm, reads=wpck + xqfmk, writes=(pk,),
                                tag=f"l2f_{wtag}mm{tq8}_{pi}_{half}")
                        if sil:
                            def emit_ev(e, half=half, i0=i0, iw=iw, pbank=pbank):
                                return e.activation(hbuf[:, half, i0:i0 + iw],
                                                    pbank[:, 0:iw], AF.Silu)
                            self.s(emit_ev, reads=(pk,), writes=hbk,
                                   tag=f"l2f_{wtag}ev{tq8}_{pi}_{half}")
                        else:
                            def emit_ev(e, half=half, i0=i0, iw=iw, pbank=pbank):
                                return e.tensor_tensor(hbuf[:, half, i0:i0 + iw],
                                                       hbuf[:, half, i0:i0 + iw],
                                                       pbank[:, 0:iw], ALU.mult)
                            self.v(emit_ev, reads=(pk,) + hbk, writes=hbk,
                                   tag=f"l2f_{wtag}ev{tq8}_{pi}_{half}")
            for half in range(2):
                tc = tq8 * 2 + half
                self.fp4_act_quant(hbuf[:, half, :], hbk, I2, tmps, f"l2f_hq{tc}")
                for grp in range(22):
                    pt = self.pb[2 + grp % 2]
                    ptk = f"pb{2 + grp % 2}"

                    def emit_tr(e, grp=grp, half=half, pt=pt):
                        return e.transpose(pt[:, 0:128],
                                           hbuf[:, half, grp * 128:(grp + 1) * 128],
                                           self.ident[:, :])
                    self.mm(emit_tr, reads=hbk + ("consts",), writes=(ptk,),
                            tag=f"l2f_htr{tc}_{grp}")

                    def emit_te(e, grp=grp, half=half, pt=pt):
                        return e.activation(hqfm[:, grp, half * 128:half * 128 + 128],
                                            pt[:, 0:128], AF.Copy)
                    self.s(emit_te, reads=(ptk,), writes=hqfmk, tag=f"l2f_hte{tc}_{grp}")
            for ic in range(22):
                bk.dma("sync", [(wdp[:, :], wdqD[ic * 128:(ic + 1) * 128, :])],
                       reads=tuple(f"l2wdD:{cc}:{rr}" for cc in range(22) for rr in range(8)),
                       writes=wdpk, tag=f"l2f_dw{tq8}_{ic}")
                for half in range(2):
                    for eo in range(2):
                        pbank = self.pb[4 + half * 2 + eo]
                        pk = f"pb{4 + half * 2 + eo}"

                        def emit_mm(e, ic=ic, half=half, eo=eo, pbank=pbank):
                            return e.matmul(pbank[:, :], hqfm[:, ic, half * 128:half * 128 + 128],
                                            wdp[:, eo * 512:eo * 512 + 512],
                                            start=(ic == 0), stop=(ic == 21))
                        self.mm(emit_mm, reads=wdpk + hqfmk, writes=(pk,),
                                tag=f"l2f_dmm{tq8}_{ic}_{half}_{eo}")
            for half in range(2):
                tc = tq8 * 2 + half
                bk.dma("sync", [(sst[:, :], sr[tc, :, :])],
                       reads=(f"sD:{tc}",), writes=sstk, tag=f"l2f_sld{tc}")
                for eo in range(2):
                    pbank = self.pb[4 + half * 2 + eo]
                    pk = f"pb{4 + half * 2 + eo}"

                    def emit_g(e, eo=eo, pbank=pbank):
                        return e.activation(gst[:, :], pbank[:, :], AF.Silu)
                    self.s(emit_g, reads=(pk,), writes=gstk, tag=f"l2f_g{tc}_{eo}")

                    def emit_cb(e, tc=tc, eo=eo):
                        e.tensor_tensor(sst[:, eo * 512:eo * 512 + 512],
                                        sst[:, eo * 512:eo * 512 + 512], gst[:, :], ALU.mult)
                        return e.tensor_tensor(self.xTM[:, tc, eo * 512:eo * 512 + 512],
                                               self.xTM[:, tc, eo * 512:eo * 512 + 512],
                                               sst[:, eo * 512:eo * 512 + 512], ALU.add)
                    self.v(emit_cb, reads=gstk + sstk + (f"xTM:{tc}",),
                           writes=(f"xTM:{tc}",) + sstk, tag=f"l2f_cb{tc}_{eo}")

    def proj_fm_spill(self, wnm, bcol_nm, outD, tag):
        bk = self.bk
        A = self.arena
        wbuf, wkeys = A.view(65664, [128, 2, ND, 128], F32)
        h1 = len(wkeys) // 2
        wbk = [wkeys[:h1], wkeys[h1:]] if h1 else [wkeys, wkeys]
        stg = [A.view(73856, [128, 512], F32), A.view(75904, [128, 512], F32)]
        bcol = bk.sb(f"bcol_{tag}", [128, ND], F32)
        bk.dma("sync", [(bcol[:, :], self.din[bcol_nm])],
               writes=(f"bcol_{tag}",), tag=f"{tag}bc")
        wsrc = self.din[wnm]
        for ec in range(ND):
            eb = ec % 2
            srcw = wsrc[:, ec * 128:(ec + 1) * 128].rearrange("(c p) e -> p c e", p=128)
            bk.dma("sync", [(wbuf[:, eb, :, :], srcw)], writes=wbk[eb], tag=f"{tag}w{ec}")
            for tq in range(4):
                pbank = self.pb[tq % 2]
                pk = f"pb{tq % 2}"

                def emit_mm(e, tq=tq, eb=eb, pbank=pbank):
                    return [e.matmul(pbank[:, :], wbuf[:, eb, dc, :],
                                     self.xnFM[:, dc, 1 + tq * 512:1 + tq * 512 + 512],
                                     start=(dc == 0), stop=(dc == ND - 1)) for dc in range(ND)]
                self.mm(emit_mm, reads=wbk[eb] + self.xnFM_keys, writes=(pk,),
                        tag=f"{tag}mm{ec}_{tq}")
                sb_, sbk = stg[tq % 2]

                def emit_ev(e, ec=ec, pbank=pbank, sb_=sb_):
                    return e.tensor_scalar(sb_[:, :], pbank[:, :], bcol[:, ec:ec + 1], None, ALU.add)
                self.v(emit_ev, reads=(pk, f"bcol_{tag}"), writes=sbk, tag=f"{tag}ev{ec}_{tq}")
                bk.dma("gpsimd", [(outD[ec * 128:(ec + 1) * 128, tq * 512:tq * 512 + 512], sb_[:, :])],
                       reads=sbk, writes=(f"{tag}D:{ec}:{tq}",), tag=f"{tag}st{ec}_{tq}")

    def layer1(self):
        bk = self.bk
        A = self.arena
        self.ln_stats("l1ln")
        self.build_xnFM("l1fm")
        qD = self.dram_scratch("qD", [D, T])
        kD = self.dram_scratch("kD", [D, T])
        vD = self.dram_scratch("vD", [T, D])
        sD = self.dscr["sD"]
        sr = sD.rearrange("(tc p) d -> tc p d", p=128)

        self.proj_fm_spill("Wq1T", "bq1C", qD, "l1q")
        self.proj_fm_spill("Wk1T", "bk1C", kD, "l1k")
        self.proj_spill("Wv1T", D, vD, "bv1R", 110592, None, "l1v")

        s64, s64k = A.view(118784, [128, NT, 64], F32)
        win, wink = A.view(102400, [128, ND, 64], F32)
        binr, binrk = A.view(106496, [128, 64], F32)
        bk.dma("sync", [(win[:, :, :], self.din["f1Win"].rearrange("(c p) m -> p c m", p=128))],
               writes=wink, tag="l1f_win")
        bk.dma("gpsimd", [(binr[:, :], self.din["f1bin"].partition_broadcast(128).squeeze(1))],
               writes=binrk, tag="l1f_bin")
        for tc in range(NT):
            pbank = self.pb[2 + tc % 2]
            pk = f"pb{2 + tc % 2}"

            def emit_mm(e, tc=tc, pbank=pbank):
                return [e.matmul(pbank[:, 0:64], self.xnFM[:, dc, 1 + tc * 128:1 + tc * 128 + 128],
                                 win[:, dc, :], start=(dc == 0), stop=(dc == ND - 1))
                        for dc in range(ND)]
            self.mm(emit_mm, reads=wink + self.xnFM_keys, writes=(pk,), tag=f"l1f_smm{tc}")

            def emit_ev(e, tc=tc, pbank=pbank):
                return e.scalar_tensor_tensor(s64[:, tc, :], pbank[:, 0:64], 1.0, binr[:, :],
                                              ALU.mult, ALU.add)
            self.v(emit_ev, reads=(pk,) + binrk, writes=s64k, tag=f"l1f_sev{tc}")

        attn, attnk = A.view(0, [128, ND, 512], F32)
        vpad, vpadk = A.view(16384, [128, 16, 65], F32)
        khb, khk = A.view(20608, [128, T], F32)
        qhb, qhk = A.view(28800, [128, 512], F32)
        expt = [A.view(30848, [128, 512], F32), A.view(32896, [128, 512], F32)]
        rrow, rrowk = A.view(34944, [128, 512], F32)
        rb, rbk = A.view(36992, [128, 512], F32)
        avtmp, avtmpk = A.view(39040, [128, 512], F32)
        wo, wok = A.view(41088, [128, ND, D], F32)
        stg, stgk = A.view(73856, [128, D], F32)
        bo1r, bo1k = self.rep_at(77952, "bo1", D, tag="repbo1")
        bk.dma("sync", [(wo[:, :, 0:512], self.din["Wo1T"].rearrange("(c p) e -> p c e", p=128)[:, :, 0:512]),
                        (wo[:, :, 512:1024], self.din["Wo1T"].rearrange("(c p) e -> p c e", p=128)[:, :, 512:1024])],
               writes=wok, tag="l1wo_w")
        self.v(lambda e: e.memset(vpad[:, :, 64:65], 1.0), writes=vpadk, tag="l1_vones")

        vre = vD.rearrange("(kc p) (h dv) -> p kc h dv", p=128, dv=64)
        for qc in range(4):
            for h in range(16):
                pairs = [(khb[0:64, :], kD[h * 64:(h + 1) * 64, :]),
                         (qhb[0:64, :], qD[h * 64:(h + 1) * 64, qc * 512:qc * 512 + 512])]
                if qc == 0 or True:
                    pairs.append((vpad[:, :, 0:64], vre[:, :, h, :]))
                rd = tuple(f"l1qD:{ec}:{tq}" for ec in range(ND) for tq in range(4)) + \
                     tuple(f"l1kD:{ec}:{tq}" for ec in range(ND) for tq in range(4)) + \
                     tuple(f"l1vD:{tc}" for tc in range(NT))
                bk.dma("sync", pairs, reads=rd, writes=khk + qhk + vpadk,
                       tag=f"l1a_ld{qc}_{h}")
                for kc in range(16):
                    sb_ = self.pb[2 + kc % 2]
                    spk = f"pb{2 + kc % 2}"

                    def emit_s(e, kc=kc, sb_=sb_):
                        return e.matmul(sb_[:, :], khb[0:64, kc * 128:kc * 128 + 128],
                                        qhb[0:64, :], start=True, stop=True)
                    self.mm(emit_s, reads=khk + qhk, writes=(spk,), tag=f"l1a_s{qc}_{h}_{kc}")
                    ex, exk = expt[kc % 2]

                    def emit_e(e, sb_=sb_, ex=ex):
                        return e.activation(ex[:, :], sb_[:, :], AF.Exp, scale=0.125)
                    self.s(emit_e, reads=(spk,), writes=exk, tag=f"l1a_e{qc}_{h}_{kc}")

                    def emit_av(e, kc=kc, ex=ex):
                        return e.matmul(self.pb[4][0:65, :], vpad[:, kc, :], ex[:, :],
                                        start=(kc == 0), stop=(kc == 15))
                    self.mm(emit_av, reads=expt[kc % 2][1] + vpadk, writes=("pb4",),
                            tag=f"l1a_av{qc}_{h}_{kc}")
                def emit_cp(e):
                    return e.activation(rrow[64:65, :], self.pb[4][64:65, :], AF.Copy)
                self.s(emit_cp, reads=("pb4",), writes=rrowk, tag=f"l1a_cp{qc}_{h}")
                bk.dma("gpsimd", [(rrow[0:1, :], rrow[64:65, :])], reads=rrowk,
                       writes=rrowk, tag=f"l1a_mvr{qc}_{h}")

                def emit_r(e):
                    return e.reciprocal(rrow[0:1, :], rrow[0:1, :])
                self.v(emit_r, reads=rrowk, writes=rrowk, tag=f"l1a_r{qc}_{h}")

                def emit_bc(e):
                    return e.matmul(self.pb[5][0:64, :], self.ones1[0:1, 0:64], rrow[0:1, :],
                                    start=True, stop=True)
                self.mm(emit_bc, reads=rrowk + ("consts",), writes=("pb5",), tag=f"l1a_bc{qc}_{h}")

                def emit_rbe(e):
                    return e.activation(rb[0:64, :], self.pb[5][0:64, :], AF.Copy)
                self.s(emit_rbe, reads=("pb5",), writes=rbk, tag=f"l1a_rbe{qc}_{h}")
                if h % 2 == 0:
                    def emit_at(e, h=h):
                        return e.tensor_tensor(attn[0:64, h // 2, :], self.pb[4][0:64, :],
                                               rb[0:64, :], ALU.mult)
                    self.v(emit_at, reads=("pb4",) + rbk, writes=attnk, tag=f"l1a_at{qc}_{h}")
                else:
                    def emit_at(e):
                        return e.tensor_tensor(avtmp[0:64, :], self.pb[4][0:64, :],
                                               rb[0:64, :], ALU.mult)
                    self.v(emit_at, reads=("pb4",) + rbk, writes=avtmpk, tag=f"l1a_at{qc}_{h}")
                    bk.dma("gpsimd", [(attn[64:128, h // 2, :], avtmp[0:64, :])],
                           reads=avtmpk, writes=attnk, tag=f"l1a_mv{qc}_{h}")
            for ts_ in range(4):
                for eo in range(2):
                    pbank = self.pb[eo]
                    pk = f"pb{eo}"

                    def emit_mm(e, ts_=ts_, eo=eo, pbank=pbank):
                        return [e.matmul(pbank[:, :], attn[:, c, ts_ * 128:ts_ * 128 + 128],
                                         wo[:, c, eo * 512:eo * 512 + 512],
                                         start=(c == 0), stop=(c == ND - 1)) for c in range(ND)]
                    self.mm(emit_mm, reads=attnk + wok, writes=(pk,), tag=f"l1wo_mm{qc}_{ts_}_{eo}")

                    def emit_ev(e, eo=eo, pbank=pbank):
                        return e.scalar_tensor_tensor(stg[:, eo * 512:eo * 512 + 512], pbank[:, :],
                                                      1.0, bo1r[:, eo * 512:eo * 512 + 512],
                                                      ALU.mult, ALU.add)
                    self.v(emit_ev, reads=(pk,) + bo1k, writes=stgk, tag=f"l1wo_ev{qc}_{ts_}_{eo}")
                tc = qc * 4 + ts_
                bk.dma("gpsimd", [(sr[tc, :, :], stg[:, :])], reads=stgk,
                       writes=(f"sD:{tc}",), tag=f"l1wo_st{tc}")

        wout, woutk = A.view(0, [128, D], F32)
        w1r, w1rk = A.view(4096, [128, 1024], F32)
        b1r, b1rk = A.view(8192, [128, 1024], F32)
        w2r, w2rk = A.view(12288, [128, 1024], F32)
        b2r, b2rk = A.view(16384, [128, 64], F32)
        boutr, boutk = A.view(20480, [128, D], F32)
        ht, htk = A.view(24576, [128, 1024], F32)
        tmp, tmpk = A.view(28672, [128, 1024], F32)
        g64, g64k = A.view(32768, [128, 64], F32)
        g64f, g64fk = A.view(33280, [128, 128], F32)
        gst, gstk = A.view(36864, [128, D], F32)
        sst, sstk = A.view(40960, [128, D], F32)
        bk.dma("sync", [(wout[0:64, :], self.din["f1Wout"])], writes=woutk, tag="l1f_wout")
        bk.dma("gpsimd",
               [(w1r[:, :], self.din["f1W1"].partition_broadcast(128).squeeze(1)),
                (b1r[:, :], self.din["f1b1"].partition_broadcast(128).squeeze(1)),
                (w2r[:, :], self.din["f1W2"].partition_broadcast(128).squeeze(1)),
                (b2r[:, :], self.din["f1b2"].partition_broadcast(128).squeeze(1)),
                (boutr[:, :], self.din["f1bout"].partition_broadcast(128).squeeze(1))],
               writes=w1rk + b1rk + w2rk + b2rk + boutk, tag="l1f_reps")
        for tc in range(NT):
            def emit_h(e, tc=tc):
                sb3 = s64[:, tc, :].unsqueeze(-1).broadcast_to([128, 64, 16])
                e.tensor_tensor(ht[:, :].rearrange("p (m i) -> p m i", i=16), sb3,
                                w1r[:, :].rearrange("p (m i) -> p m i", i=16), ALU.mult)
                e.tensor_tensor(ht[:, :], ht[:, :], b1r[:, :], ALU.add)
                e.tensor_relu(tmp[:, :], ht[:, :])
                e.tensor_tensor(tmp[:, :], tmp[:, :], w2r[:, :], ALU.mult)
                e.tensor_reduce(g64[:, :].unsqueeze(-1), tmp[:, :].rearrange("p (m i) -> p m i", i=16),
                                AX, ALU.add)
                return e.tensor_tensor(g64[:, :], g64[:, :], b2r[:, :], ALU.add)
            self.v(emit_h, reads=s64k + w1rk + b1rk + w2rk + b2rk,
                   writes=htk + tmpk + g64k, tag=f"l1f_h{tc}")

            def emit_tr(e):
                return e.transpose(self.pb[6][0:64, 0:128], g64[:, :], self.ident[:, :])
            self.mm(emit_tr, reads=g64k + ("consts",), writes=("pb6",), tag=f"l1f_tr{tc}")

            def emit_tev(e):
                return e.activation(g64f[0:64, :], self.pb[6][0:64, 0:128], AF.Copy)
            self.s(emit_tev, reads=("pb6",), writes=g64fk, tag=f"l1f_tev{tc}")
            for eo in range(2):
                pbank = self.pb[eo]
                pk = f"pb{eo}"

                def emit_om(e, eo=eo, pbank=pbank):
                    return e.matmul(pbank[:, :], g64f[0:64, :], wout[0:64, eo * 512:eo * 512 + 512],
                                    start=True, stop=True)
                self.mm(emit_om, reads=g64fk + woutk, writes=(pk,), tag=f"l1f_om{tc}_{eo}")

                def emit_oev(e, eo=eo, pbank=pbank):
                    return e.scalar_tensor_tensor(gst[:, eo * 512:eo * 512 + 512], pbank[:, :],
                                                  1.0, boutr[:, eo * 512:eo * 512 + 512],
                                                  ALU.mult, ALU.add)
                self.v(emit_oev, reads=(pk,) + boutk, writes=gstk, tag=f"l1f_oev{tc}_{eo}")
            self.s(lambda e: e.activation(gst[:, :], gst[:, :], AF.Silu),
                   reads=gstk, writes=gstk, tag=f"l1f_silu{tc}")
            bk.dma("sync", [(sst[:, :], sr[tc, :, :])], reads=(f"sD:{tc}",), writes=sstk,
                   tag=f"l1f_sld{tc}")

            def emit_comb(e, tc=tc):
                e.tensor_tensor(sst[:, :], sst[:, :], gst[:, :], ALU.mult)
                return e.tensor_tensor(self.xTM[:, tc, :], self.xTM[:, tc, :], sst[:, :], ALU.add)
            self.v(emit_comb, reads=gstk + sstk + (f"xTM:{tc}",), writes=(f"xTM:{tc}",) + sstk,
                   tag=f"l1f_comb{tc}")



def prep_inputs(inputs):
    C = np.ascontiguousarray
    f = np.float32
    d = {}
    d["pos_phases"] = C(np.asarray(inputs["L0_attn_pos_phases"]), dtype=f)
    d["Wpv"] = C(np.asarray(inputs["L0_attn_Wpv"]), dtype=f)
    d["Wk"] = C(np.asarray(inputs["L0_attn_Wk"]), dtype=f)
    d["Wkv"] = C(np.asarray(inputs["L0_attn_Wkv"]), dtype=f)
    d["Wo0"] = C(np.asarray(inputs["L0_attn_Wo"]), dtype=f)
    d["Wg1"] = C(np.asarray(inputs["L0_attn_Wg1"]), dtype=f)
    d["Wb1"] = C(np.asarray(inputs["L0_attn_Wb1"]), dtype=f)
    d["Wg2T"] = C(np.asarray(inputs["L0_attn_Wg2"]).reshape(1, -1), dtype=f)
    d["Wb2T"] = C(np.asarray(inputs["L0_attn_Wb2"]).T, dtype=f)
    for nm, key in (("bpv", "L0_attn_bpv"), ("bk0", "L0_attn_bk"), ("bkv", "L0_attn_bkv"),
                    ("bg1", "L0_attn_bg1"), ("bg2", "L0_attn_bg2"), ("bb1", "L0_attn_bb1"),
                    ("bb2", "L0_attn_bb2"), ("ln_g", "L0_attn_ln_g"), ("ln_b", "L0_attn_ln_b"),
                    ("bo0", "L0_attn_bo"), ("f0b1", "L0_ffn_b1"), ("f0b2", "L0_ffn_b2")):
        d[nm] = C(np.asarray(inputs[key]).reshape(1, -1), dtype=f)
    d["f0w1T"] = C(np.asarray(inputs["L0_ffn_w1"]).T, dtype=f)
    d["f0w2T"] = C(np.asarray(inputs["L0_ffn_w2"]).T, dtype=f)
    d["Wq1T"] = C(np.asarray(inputs["L1_attn_Wq"]).T, dtype=f)
    d["Wk1T"] = C(np.asarray(inputs["L1_attn_Wk"]).T, dtype=f)
    d["Wv1T"] = C(np.asarray(inputs["L1_attn_Wv"]).T, dtype=f)
    d["Wo1T"] = C(np.asarray(inputs["L1_attn_Wo"]).T, dtype=f)
    d["bq1C"] = C(np.asarray(inputs["L1_attn_bq"]).reshape(8, 128).T, dtype=f)
    d["bk1C"] = C(np.asarray(inputs["L1_attn_bk"]).reshape(8, 128).T, dtype=f)
    d["bv1R"] = C(np.asarray(inputs["L1_attn_bv"]).reshape(1, -1), dtype=f)
    d["bo1"] = C(np.asarray(inputs["L1_attn_bo"]).reshape(1, -1), dtype=f)
    d["f1Win"] = C(np.asarray(inputs["L1_ffn_Win"]), dtype=f)
    d["f1Wout"] = C(np.asarray(inputs["L1_ffn_Wout"]), dtype=f)
    d["f1bin"] = C(np.asarray(inputs["L1_ffn_bin_"]).reshape(1, -1), dtype=f)
    d["f1W1"] = C(np.asarray(inputs["L1_ffn_W1"]).reshape(1, -1), dtype=f)
    d["f1b1"] = C(np.asarray(inputs["L1_ffn_b1"]).reshape(1, -1), dtype=f)
    d["f1W2"] = C(np.asarray(inputs["L1_ffn_W2"]).reshape(1, -1), dtype=f)
    d["f1b2"] = C(np.asarray(inputs["L1_ffn_b2"]).reshape(1, -1), dtype=f)
    d["f1bout"] = C(np.asarray(inputs["L1_ffn_bout"]).reshape(1, -1), dtype=f)
    d["W2inT"] = C(np.asarray(inputs["L2_attn_W_in"]).T, dtype=f)
    d["convw"] = C(np.asarray(inputs["L2_attn_conv_w"]).reshape(16, 128, 4).transpose(1, 0, 2), dtype=f)
    d["convb"] = C(np.asarray(inputs["L2_attn_conv_b"]).reshape(16, 128).T, dtype=f)
    d["W2xT"] = C(np.asarray(inputs["L2_attn_W_x"]).T, dtype=f)
    d["dtlnw"] = C(np.asarray(inputs["L2_attn_dt_ln_w"]).reshape(1, -1), dtype=f)
    d["Blnw"] = C(np.asarray(inputs["L2_attn_B_ln_w"]).reshape(1, -1), dtype=f)
    d["Clnw"] = C(np.asarray(inputs["L2_attn_C_ln_w"]).reshape(1, -1), dtype=f)
    d["W2dtT"] = C(np.asarray(inputs["L2_attn_W_dt"]).T, dtype=f)
    d["bdt"] = C(np.asarray(inputs["L2_attn_b_dt"]).reshape(16, 128).T, dtype=f)
    d["AlogC"] = C(np.asarray(inputs["L2_attn_A_log"]).reshape(16, 128, 16).transpose(1, 0, 2), dtype=f)
    d["DparC"] = C(np.asarray(inputs["L2_attn_D_param"]).reshape(16, 128).T, dtype=f)
    d["ln1wC"] = C(np.asarray(inputs["L2_attn_ln1_w"]).reshape(16, 128).T, dtype=f)
    d["ln2wC"] = C(np.asarray(inputs["L2_attn_ln2_w"]).reshape(16, 128).T, dtype=f)
    d["W2outT"] = C(np.asarray(inputs["L2_attn_W_out"]).T, dtype=f)
    d["w2gate"] = C(np.asarray(inputs["L2_ffn_w_gate"]), dtype=f)
    d["w2up"] = C(np.asarray(inputs["L2_ffn_w_up"]), dtype=f)
    d["w2down"] = C(np.asarray(inputs["L2_ffn_w_down"]), dtype=f)
    cm = np.zeros((4, 128, 512), dtype=f)
    for s_ in range(4):
        for pq in range(128):
            cm[s_, pq, :] = (np.arange(512) <= s_ * 128 + pq)
    d["cmask"] = cm
    d["w3kvd"] = C(np.asarray(inputs["L3_attn_w_kv_down"]), dtype=f)
    d["w3kvu"] = C(np.asarray(inputs["L3_attn_w_kv_up"]), dtype=f)
    d["w3q"] = C(np.asarray(inputs["L3_attn_w_q"]), dtype=f)
    d["w3out"] = C(np.asarray(inputs["L3_attn_w_out"]), dtype=f)
    d["w3g"] = C(np.asarray(inputs["L3_ffn_Wg"]), dtype=f)
    d["w3W1"] = C(np.asarray(inputs["L3_ffn_W1"]), dtype=f)
    d["w3W2"] = C(np.asarray(inputs["L3_ffn_W2"]), dtype=f)
    d["w3b1C"] = C(np.asarray(inputs["L3_ffn_b1"]).reshape(8, 16, 128).transpose(2, 0, 1), dtype=f)
    d["w3b2"] = C(np.asarray(inputs["L3_ffn_b2"]), dtype=f)
    cmT = np.zeros((4, 128, 512), dtype=f)
    for s_ in range(4):
        for pk in range(128):
            cmT[s_, pk, :] = (np.arange(512) >= s_ * 128 + pk)
    d["cmaskT"] = cmT
    d["ident"] = np.eye(128, dtype=f)
    d["tri"] = np.triu(np.ones((128, 128), dtype=f))
    d["onesrow"] = np.ones((1, 128), dtype=f)
    return d


def build_program(n_layers=None, taps=()):
    global DEBUG_TAPS
    DEBUG_TAPS = tuple(taps)
    nl = N_LAYERS if n_layers is None else n_layers
    nc = bass.Bass("TRN2", target_bir_lowering=False)
    bk = BK(nc)
    M = Model(bk)
    M.declare_inputs()
    M.build(nl)
    bk.finalize()
    return nc, M, bk


def kernel(**inputs):
    nc, M, bk = build_program()
    shared = prep_inputs(inputs)
    x = np.ascontiguousarray(np.asarray(inputs["L0_attn_x"]), dtype=np.float32)
    in_maps = []
    for c in range(8):
        m = dict(shared)
        m["x"] = np.ascontiguousarray(x[c])
        in_maps.append(m)
    res = run_bass_kernel_spmd(nc, in_maps, core_ids=list(range(8)))
    out = np.stack([res.results[c]["out"] for c in range(8)], axis=0)
    return out


# version 05
# speedup vs baseline: 1.0931x; 1.0931x over previous
import math
import numpy as np
import concourse.bass as bass
import concourse.mybir as mybir
from concourse.bass_utils import run_bass_kernel_spmd

F32 = mybir.dt.float32
BF16 = mybir.dt.bfloat16
I32 = mybir.dt.int32
U8 = mybir.dt.uint8
AF = mybir.ActivationFunctionType
ALU = mybir.AluOpType
AX = mybir.AxisListType.X

T = 2048
D = 1024
NT = T // 128
ND = D // 128
TWO_PI = float(2.0 * math.pi)
PI = float(math.pi)

N_LAYERS = 4
DEBUG_TAPS = ()

ENGINES = ("sync", "scalar", "vector", "tensor", "gpsimd")



class Task:
    __slots__ = ("eng", "kind", "emit", "waits", "tag", "count_after")


class BK:
    def __init__(self, nc):
        self.nc = nc
        self.tasks = {e: [] for e in ENGINES}
        self.count = {e: 0 for e in ENGINES}
        self.writer = {}
        self.readers = {}
        self.waited = {e: {} for e in ENGINES}
        self._names = {}
        self.taps = {}

    def name(self, base):
        i = self._names.get(base, 0)
        self._names[base] = i + 1
        return f"{base}_{i}" if i else base

    def sb(self, base, shape, dtype):
        return self.nc.alloc_sbuf_tensor(self.name(base), list(shape), dtype).ap()

    def psum(self, base, shape, dtype=F32):
        return self.nc.alloc_psum_tensor(self.name(base), list(shape), dtype).ap()

    def t(self, eng, emit, reads=(), writes=(), kind="c", tag=""):
        waits = {}

        def need(e2, c2, dep_is_dma=False):
            if (e2 != eng or dep_is_dma) and c2 > self.waited[eng].get(e2, 0):
                waits[e2] = max(waits.get(e2, 0), c2)

        for r in reads:
            w = self.writer.get(r)
            if w is not None:
                need(*w)
        for r in writes:
            w = self.writer.get(r)
            if w is not None:
                need(*w)
            for (e2, d2), c2 in self.readers.get(r, {}).items():
                need(e2, c2, d2)
        for e2, c2 in waits.items():
            self.waited[eng][e2] = c2
        tk = Task()
        tk.eng, tk.emit, tk.waits, tk.tag = eng, emit, waits, tag
        if kind == "c":
            tk.kind = "c"
            self.count[eng] += 1
        else:
            n = int(kind.split(":")[1]) if ":" in kind else 1
            tk.kind = "d"
            self.count[eng] += 16 * n
        tk.count_after = self.count[eng]
        self.tasks[eng].append(tk)
        cnt = self.count[eng]
        is_dma = tk.kind == "d"
        for r in writes:
            self.writer[r] = (eng, cnt, is_dma)
            self.readers.pop(r, None)
        for r in reads:
            self.readers.setdefault(r, {})[(eng, is_dma)] = cnt
        return tk

    def dma(self, eng, pairs, reads=(), writes=(), tag=""):
        pairs = list(pairs)

        def emit(e, pairs=pairs):
            return [e.dma_start(out=o, in_=i) for o, i in pairs]

        return self.t(eng, emit, reads=reads, writes=writes, kind=f"d:{len(pairs)}", tag=tag)

    def tap(self, nm, ap, reads):
        if nm not in DEBUG_TAPS or nm in self.taps:
            return
        out = self.nc.dram_tensor("tap_" + nm, list(ap.shape), ap.dtype, kind="ExternalOutput").ap()
        self.taps[nm] = out
        self.dma("gpsimd", [(out, ap)], reads=reads, tag=f"tap_{nm}")

    def check_deadlock(self):
        idx = {e: 0 for e in ENGINES}
        done = {e: 0 for e in ENGINES}
        total = sum(len(v) for v in self.tasks.values())
        ndone = 0
        while ndone < total:
            progress = False
            for e in ENGINES:
                while idx[e] < len(self.tasks[e]):
                    tk = self.tasks[e][idx[e]]
                    if all(done[e2] >= c2 for e2, c2 in tk.waits.items()):
                        done[e] = tk.count_after
                        idx[e] += 1
                        ndone += 1
                        progress = True
                    else:
                        break
            if not progress:
                lines = []
                for e in ENGINES:
                    if idx[e] < len(self.tasks[e]):
                        tk = self.tasks[e][idx[e]]
                        miss = {e2: (c2, done[e2]) for e2, c2 in tk.waits.items() if done[e2] < c2}
                        lines.append(f"{e}[{idx[e]}] tag={tk.tag} needs={miss}")
                raise RuntimeError("task-graph deadlock:\n" + "\n".join(lines))

    def finalize(self):
        self.check_deadlock()
        nc = self.nc
        sems = {e: nc.alloc_semaphore(f"sem_{e}") for e in ENGINES}

        _PASS = {"dma_start", "wait_ge", "drain", "nop", "engine_nop",
                 "dma_start_transpose", "memzero"}

        class Shim:

            def __init__(self, e):
                self._e = e

            def __getattr__(self, nm):
                fn = getattr(self._e, nm)
                if nm.startswith("_") or nm in _PASS or not callable(fn):
                    return fn

                def w(*a, **kw):
                    inst = fn(*a, **kw)
                    self._e.drain()
                    return inst
                return w

        with nc.Block() as block:
            def mk(eng_name):
                def body(eng):
                    if eng_name in ("vector", "scalar"):
                        eng = Shim(eng)
                    for tk in self.tasks[eng_name]:
                        for e2, c2 in sorted(tk.waits.items()):
                            eng.wait_ge(sems[e2], c2)
                        try:
                            insts = tk.emit(eng)
                        except Exception as ex:
                            raise RuntimeError(f"emit failed in task {tk.tag}: {ex}") from ex
                        if tk.kind == "d":
                            for inst in insts:
                                inst.then_inc(sems[eng_name], 16)
                        else:
                            if isinstance(insts, (list, tuple)):
                                insts = insts[-1]
                            assert insts is not None, f"task {tk.tag} returned None"
                            insts.then_inc(sems[eng_name], 1)
                return body
            for e in ENGINES:
                getattr(block, e)(mk(e))


class Arena:
    def __init__(self, bk, name, bytes_per_part, block=8192):
        self.bk = bk
        self.nm = name
        self.block = block
        self.tsr = bk.nc.alloc_sbuf_tensor(name, [128, bytes_per_part], U8)
        self.nbytes = bytes_per_part

    def view(self, off, shape, dtype):
        esz = mybir.dt.size(dtype)
        n = 1
        for s_ in shape[1:]:
            n *= s_
        nb = n * esz
        assert off % 32 == 0 and off + nb <= self.nbytes, (off, nb, self.nbytes)
        ap = self.tsr.ap()[:, off:off + nb].bitcast(dtype)
        if len(shape) > 2:
            dims = " ".join(f"s{i}" for i in range(1, len(shape)))
            ap = ap.rearrange(f"p ({dims}) -> p {dims}",
                              **{f"s{i}": shape[i] for i in range(1, len(shape))})
        keys = tuple(f"{self.nm}:{b}" for b in range(off // self.block,
                                                     (off + nb + self.block - 1) // self.block))
        return ap, keys



class Model:
    def __init__(self, bk: BK):
        self.bk = bk
        self.nc = bk.nc
        self.din = {}
        self.dscr = {}

    def dram_in(self, nm, shape, dtype=F32):
        ap = self.nc.dram_tensor(nm, list(shape), dtype, kind="ExternalInput").ap()
        self.din[nm] = ap
        return ap

    def dram_scratch(self, nm, shape, dtype=F32):
        kind = "ExternalOutput" if nm in DEBUG_TAPS else "Internal"
        ap = self.nc.dram_tensor(nm, list(shape), dtype, kind=kind).ap()
        self.dscr[nm] = ap
        return ap

    def declare_inputs(self):
        d = self.dram_in
        d("x", [T, D])
        d("pos_phases", [16384, D])
        for w in ("Wpv", "Wk", "Wkv", "Wo0"):
            d(w, [D, D])
        d("Wg1", [2 * D, D])
        d("Wb1", [D, 512])
        d("Wg2T", [1, D])
        d("Wb2T", [2, 512])
        for b, n in (("bpv", D), ("bk0", D), ("bkv", D), ("bg1", D), ("bg2", 1),
                     ("bb1", 512), ("bb2", 2), ("ln_g", D), ("ln_b", D), ("bo0", D),
                     ("f0b1", 4096), ("f0b2", D)):
            d(b, [1, n])
        d("f0w1T", [D, 4096])
        d("f0w2T", [4096, D])
        for w in ("Wq1T", "Wk1T", "Wv1T", "Wo1T"):
            d(w, [D, D])
        for b in ("bq1C", "bk1C"):
            d(b, [128, ND])
        d("bv1R", [1, D])
        d("bo1", [1, D])
        d("f1Win", [D, 64])
        d("f1Wout", [64, D])
        for b, n in (("f1bin", 64), ("f1W1", 1024), ("f1b1", 1024), ("f1W2", 1024),
                     ("f1b2", 64), ("f1bout", D)):
            d(b, [1, n])
        d("W2inT", [D, 5888])
        d("convw", [128, 16, 4])
        d("convb", [128, 16])
        d("W2xT", [2048, 96])
        d("dtlnw", [1, 64])
        d("Blnw", [1, 16])
        d("Clnw", [1, 16])
        d("W2dtT", [64, 2048])
        d("bdt", [128, 16])
        d("AlogC", [128, 16, 16])
        d("DparC", [128, 16])
        d("ln1wC", [128, 16])
        d("ln2wC", [128, 16])
        d("W2outT", [2048, D])
        d("w2gate", [2816, D])
        d("w2up", [2816, D])
        d("w2down", [D, 2816])
        d("cmask", [4, 128, 512])
        d("w3kvd", [D, 128])
        d("w3kvu", [128, 2048])
        d("w3q", [D, D])
        d("w3out", [D, D])
        d("w3g", [D, 8])
        d("w3W1", [8, D, 2048])
        d("w3W2", [8, 2048, D])
        d("w3b1C", [128, 8, 16])
        d("w3b2", [8, D])
        d("cmaskT", [4, 128, 512])
        d("ident", [128, 128])
        d("tri", [128, 128])
        d("onesrow", [1, 128])

    def v(self, emit, reads=(), writes=(), tag=""):
        return self.bk.t("vector", emit, reads, writes, tag=tag)

    def s(self, emit, reads=(), writes=(), tag=""):
        return self.bk.t("scalar", emit, reads, writes, tag=tag)

    def mm(self, emit, reads=(), writes=(), tag=""):
        return self.bk.t("tensor", emit, reads, writes, tag=tag)

    def rep_at(self, off, nm, width, tag=None):
        ap, keys = self.arena.view(off, [128, width], F32)
        self.bk.dma("gpsimd", [(ap[:, :], self.din[nm].partition_broadcast(128).squeeze(1))],
                    writes=keys, tag=tag or f"rep_{nm}")
        return ap, keys

    def build(self, n_layers):
        bk = self.bk
        self.ident = bk.sb("identc", [128, 128], F32)
        self.tri = bk.sb("tric", [128, 128], F32)
        self.ones1 = bk.sb("ones1c", [1, 128], F32)
        bk.dma("sync", [(self.ident[:, :], self.din["ident"]),
                        (self.tri[:, :], self.din["tri"]),
                        (self.ones1[:, :], self.din["onesrow"])],
               writes=("consts",), tag="consts")
        self.onescol = bk.sb("onescol", [128, 1], F32)
        self.one1 = bk.sb("one1", [1, 1], F32)
        self.halfpi = bk.sb("halfpi", [128, 1], F32)

        def emit_c2(e):
            e.memset(self.onescol[:, :], 1.0)
            e.memset(self.one1[:, :], 1.0)
            return e.memset(self.halfpi[:, :], PI / 2)
        self.v(emit_c2, writes=("consts2",), tag="onescol")

        self.xTM = bk.sb("xTM", [128, NT, D], F32)
        x_r = self.din["x"].rearrange("(tc p) d -> p tc d", p=128)
        for h in range(4):
            bk.dma("sync", [(self.xTM[:, 4 * h:4 * h + 4, :], x_r[:, 4 * h:4 * h + 4, :])],
                   writes=tuple(f"xTM:{tc}" for tc in range(4 * h, 4 * h + 4)), tag=f"loadx{h}")

        self.negmu = bk.sb("negmu", [128, NT], F32)
        self.lninv = bk.sb("lninv", [128, NT], F32)
        self.st1 = bk.sb("st1", [128, NT], F32)
        self.st2 = bk.sb("st2", [128, NT], F32)
        self.sm = bk.sb("smallpool", [128, 64], F32)
        self.sqdump = bk.sb("sqdump", [128, D], F32)
        self.carries = [bk.sb(f"carry{i}", [1, D], F32) for i in range(4)]
        self.gcar = bk.sb("gcar", [1, 1], F32)
        self.pb = [bk.psum(f"pb{i}", [128, 512], F32) for i in range(8)]
        self.arena = Arena(bk, "arena", 122880)
        self.xnFM, self.xnFM_keys = self.arena.view(0, [128, ND, 2052], F32)

        def xsnap(nm):
            if nm not in DEBUG_TAPS:
                return
            snap = self.nc.dram_tensor("tap_" + nm, [T, D], F32, kind="ExternalOutput").ap()
            snp = snap.rearrange("(tc p) d -> p tc d", p=128)
            bk.dma("gpsimd", [(snp[:, 4 * h:4 * h + 4, :], self.xTM[:, 4 * h:4 * h + 4, :])
                              for h in range(4)],
                   reads=tuple(f"xTM:{tc}" for tc in range(NT)), tag="snap_" + nm)

        if n_layers >= 1:
            self.layer0()
            xsnap("x1")
        if n_layers >= 2:
            self.layer1()
            xsnap("x2")
        if n_layers >= 3:
            self.layer2()
            xsnap("x3")
        if n_layers >= 4:
            self.layer3()

        out = self.nc.dram_tensor("out", [T, D], F32, kind="ExternalOutput").ap()
        out_r = out.rearrange("(tc p) d -> p tc d", p=128)
        bk.dma("gpsimd", [(out_r[:, 4 * h:4 * h + 4, :], self.xTM[:, 4 * h:4 * h + 4, :])
                          for h in range(4)],
               reads=tuple(f"xTM:{tc}" for tc in range(NT)), writes=("out_stored",),
               tag="store_out")
        self.v(lambda e: e.memset(self.sm[:, 63:64], 0.0), reads=("out_stored",),
               writes=("sm63",), tag="final_wait")

    def ln_stats(self, tag, eps=1e-5):
        for tc in range(NT):
            self.v(lambda e, tc=tc: e.reduce_sum(self.st1[:, tc:tc + 1], self.xTM[:, tc, :], axis=AX),
                   reads=(f"xTM:{tc}",), writes=(f"st1:{tc}",), tag=f"{tag}s1_{tc}")
            self.s(lambda e, tc=tc: e.activation(self.sqdump[:, :], self.xTM[:, tc, :], AF.Square,
                                                 accum_out=self.st2[:, tc:tc + 1]),
                   reads=(f"xTM:{tc}",), writes=(f"st2:{tc}", "sqdump"), tag=f"{tag}s2_{tc}")
        allr = tuple(f"st1:{tc}" for tc in range(NT)) + tuple(f"st2:{tc}" for tc in range(NT))

        def emit_small(e):
            e.tensor_scalar(self.negmu[:, :], self.st1[:, :], -1.0 / D, None, ALU.mult)
            e.tensor_tensor(self.st1[:, :], self.negmu[:, :], self.negmu[:, :], ALU.mult)
            e.tensor_scalar(self.st2[:, :], self.st2[:, :], 1.0 / D, None, ALU.mult)
            e.tensor_tensor(self.st2[:, :], self.st2[:, :], self.st1[:, :], ALU.subtract)
            e.tensor_scalar(self.st2[:, :], self.st2[:, :], float(eps), None, ALU.add)
            return e.reciprocal(self.st2[:, :], self.st2[:, :])
        self.v(emit_small, reads=allr, writes=("negmu",) + allr, tag=f"{tag}small")
        self.s(lambda e: e.activation(self.lninv[:, :], self.st2[:, :], AF.Sqrt),
               reads=allr, writes=("lninv",), tag=f"{tag}inv")

    def xn_tile(self, dst, dst_keys, tc, tag):
        self.v(lambda e: e.tensor_scalar(dst, self.xTM[:, tc, :],
                                         self.negmu[:, tc:tc + 1], self.lninv[:, tc:tc + 1],
                                         ALU.add, ALU.mult),
               reads=(f"xTM:{tc}", "negmu", "lninv"), writes=tuple(dst_keys), tag=tag)

    def transpose128(self, src, src_keys, dst, dst_keys, n_blocks, tag, cast=None):
        for g0 in range(0, n_blocks, 4):
            nb = min(4, n_blocks - g0)
            pbank = self.pb[4 + (g0 // 4) % 2]
            pk = f"pb{4 + (g0 // 4) % 2}"

            def emit_tr(e, g0=g0, nb=nb, pbank=pbank, src=src):
                return [e.transpose(pbank[:, j * 128:(j + 1) * 128],
                                    src[:, (g0 + j) * 128:(g0 + j + 1) * 128],
                                    self.ident[:, :]) for j in range(nb)]
            self.mm(emit_tr, reads=tuple(src_keys) + ("consts",), writes=(pk,), tag=f"{tag}tr{g0}")

            def emit_ev(e, g0=g0, nb=nb, pbank=pbank, dst=dst):
                return e.activation(dst[:, g0:g0 + nb, :],
                                    pbank[:, 0:nb * 128].rearrange("p (j c) -> p j c", c=128),
                                    AF.Copy)
            self.s(emit_ev, reads=(pk,), writes=tuple(dst_keys), tag=f"{tag}ev{g0}")

    def build_xnFM(self, tag):
        self.v(lambda e: e.memset(self.xnFM[:, :, 0:1], 0.0), writes=self.xnFM_keys,
               tag=f"{tag}pad")
        av, ak = self.arena.view(65664, [128, D], F32)
        for tc in range(NT):
            self.xn_tile(av[:, :], ak, tc, f"{tag}xn{tc}")
            fmv = self.xnFM[:, :, 1 + tc * 128:1 + tc * 128 + 128]
            self.transpose128(av, ak, fmv, self.xnFM_keys, ND, f"{tag}t{tc}")

    def layer0(self):
        bk = self.bk
        self.ln_stats("l0ln")
        self.build_xnFM("l0fm")
        bk.tap("l0_xnfm", self.xnFM[:, :, :], reads=self.xnFM_keys)

        pvD = self.dram_scratch("pvD", [T, D])
        kvD = self.dram_scratch("kvD", [T, D])
        thD = self.dram_scratch("thD", [T, D])
        lnD = self.dram_scratch("lnD", [T, D])
        sD = self.dram_scratch("sD", [T, D])
        hD = self.dram_scratch("hD", [T, 4096])
        w1qD = self.dram_scratch("w1qD", [D, 4096], BF16)
        w2qD = self.dram_scratch("w2qD", [4096, D], BF16)

        self.quant_ternary("f0w1T", w1qD, D, 4096, "dqw1")
        self.quant_ternary("f0w2T", w2qD, 4096, D, "dqw2")

        self.proj_spill("Wpv", D, pvD, "bpv", 110592, None, "l0pv")
        self.proj_spill("Wkv", D, kvD, "bkv", 114688, None, "l0kv")
        self.proj_spill("Wk", D, thD, "bk0", 118784, AF.Tanh, "l0th")

        self.vg = bk.sb("vg_l0", [128, NT], F32)
        self.bw0p = bk.sb("bw0p", [128, NT], F32)
        self.bw1p = bk.sb("bw1p", [128, NT], F32)
        self.l0_gates()
        self.l0_blendw()
        self.l0_gatecum()
        self.l0_chains(pvD, kvD, thD, lnD)
        self.l0_wo(lnD, sD)
        self.l0_ffn(w1qD, w2qD, hD, sD)

    def quant_ternary(self, wnm, wqD, K, N, dqkey):
        bk = self.bk
        nk = K // 128
        base = 69632
        wv, wk = self.arena.view(base, [128, N], F32)
        qi, qik = self.arena.view(base + 16384, [128, N], I32)
        qb, qbk = self.arena.view(base + 32768, [128, N], BF16)
        parts = self.sm[:, 32:33]
        self.v(lambda e: e.memset(parts, 0.0), writes=("sm32",), tag=f"{dqkey}z")
        wsrc = self.din[wnm].rearrange("(c p) n -> c p n", p=128)
        for c in range(nk):
            bk.dma("sync", [(wv[:, :], wsrc[c, :, :])], writes=wk, tag=f"{dqkey}ld{c}")

            def emit_abs(e):
                e.tensor_reduce(self.sm[:, 33:34], wv[:, :], AX, ALU.add, apply_absolute_value=True)
                return e.tensor_tensor(parts, parts, self.sm[:, 33:34], ALU.add)
            self.v(emit_abs, reads=wk, writes=("sm32", "sm33"), tag=f"{dqkey}abs{c}")
        pbank = self.pb[4]
        self.mm(lambda e: e.matmul(pbank[0:1, 0:1], parts, self.onescol[:, :], start=True, stop=True),
                reads=("sm32", "consts2"), writes=("pb4",), tag=f"{dqkey}tot")
        dqrow = self.sm[0:1, 34:35]

        def emit_mean(e):
            e.tensor_scalar(dqrow, pbank[0:1, 0:1], 1.0 / (K * N), None, ALU.mult)
            return e.tensor_scalar(dqrow, dqrow, 1e-5, None, ALU.max)
        self.v(emit_mean, reads=("pb4",), writes=("sm34",), tag=f"{dqkey}mean")
        dq = bk.sb(f"dq_{dqkey}", [128, 2], F32)
        dqd = self.dram_scratch(f"dqd_{dqkey}", [1, 1])
        bk.dma("gpsimd", [(dqd, dqrow)], reads=("sm34",), writes=(f"dqd_{dqkey}",),
               tag=f"{dqkey}spill")
        bk.dma("gpsimd", [(dq[:, 0:1], dqd.partition_broadcast(128).squeeze(1))],
               reads=(f"dqd_{dqkey}",), writes=(f"dq_{dqkey}",), tag=f"{dqkey}rep")
        self.v(lambda e: e.reciprocal(dq[:, 1:2], dq[:, 0:1]),
               reads=(f"dq_{dqkey}",), writes=(f"dq_{dqkey}",), tag=f"{dqkey}rec")
        setattr(self, dqkey, dq)
        wqdst = wqD.rearrange("(c p) n -> c p n", p=128)
        for c in range(nk):
            bk.dma("sync", [(wv[:, :], wsrc[c, :, :])], writes=wk, tag=f"{dqkey}ld2_{c}")

            def emit_q(e):
                e.tensor_scalar(wv[:, :], wv[:, :], dq[:, 1:2], None, ALU.mult)
                e.tensor_copy(qi[:, :], wv[:, :])
                e.tensor_copy(wv[:, :], qi[:, :])
                e.tensor_scalar(wv[:, :], wv[:, :], -1.0, 1.0, ALU.max, ALU.min)
                return e.tensor_copy(qb[:, :], wv[:, :])
            self.v(emit_q, reads=wk + (f"dq_{dqkey}",), writes=wk + qik + qbk, tag=f"{dqkey}q{c}")
            bk.dma("gpsimd", [(wqdst[c, :, :], qb[:, :])], reads=qbk,
                   writes=(f"dram_{dqkey}:{c}",), tag=f"{dqkey}st{c}")

    def proj_spill(self, wnm, outw, outD, bias_nm, bias_off, act, tag):
        bk = self.bk
        wbuf, wkeys = self.arena.view(69632, [128, 2, ND, 512], F32)
        h1 = len(wkeys) // 2
        wbk = [wkeys[:h1], wkeys[h1:]]
        stg = [self.arena.view(102400, [128, outw], F32),
               self.arena.view(106496, [128, outw], F32)]
        bias, bias_k = self.rep_at(bias_off, bias_nm, outw, tag=f"{tag}bias")
        wsrc = self.din[wnm]
        outr = outD.rearrange("(tc p) n -> tc p n", p=128)
        nE = outw // 512
        for eb in range(nE):
            src = wsrc[:, eb * 512:eb * 512 + 512].rearrange("(c p) n -> p c n", p=128)
            bk.dma("sync", [(wbuf[:, eb, :, :], src)], writes=wbk[eb], tag=f"{tag}w{eb}")
        for tc in range(NT):
            sb_, sbk = stg[tc % 2]
            for eb in range(nE):
                pbank = self.pb[(2 * (tc % 2) + eb) % 4]
                pk = f"pb{(2 * (tc % 2) + eb) % 4}"

                def emit_mm(e, tc=tc, eb=eb, pbank=pbank):
                    return [e.matmul(pbank[:, :], self.xnFM[:, dc, 1 + tc * 128:1 + tc * 128 + 128],
                                     wbuf[:, eb, dc, :], start=(dc == 0), stop=(dc == ND - 1))
                            for dc in range(ND)]
                self.mm(emit_mm, reads=wbk[eb] + self.xnFM_keys, writes=(pk,),
                        tag=f"{tag}mm{eb}_{tc}")

                def emit_ev(e, eb=eb, pbank=pbank, sb_=sb_):
                    return e.scalar_tensor_tensor(sb_[:, eb * 512:eb * 512 + 512], pbank[:, :], 1.0,
                                                  bias[:, eb * 512:eb * 512 + 512],
                                                  ALU.mult, ALU.add)
                self.v(emit_ev, reads=(pk,) + bias_k, writes=sbk, tag=f"{tag}ev{eb}_{tc}")
            if act is not None:
                def emit_act(e, sb_=sb_):
                    return e.activation(sb_[:, :], sb_[:, :], act)
                self.s(emit_act, reads=sbk, writes=sbk, tag=f"{tag}act{tc}")
            bk.dma("gpsimd", [(outr[tc, :, :], sb_[:, :])], reads=sbk,
                   writes=(f"{tag}D:{tc}",), tag=f"{tag}st{tc}")

    def l0_gates(self):
        bk = self.bk
        wbuf, wkeys = self.arena.view(69632, [128, 2 * ND, 512], F32)
        gh, ghk = self.arena.view(102400, [128, 512], F32)
        gh2, gh2k = self.arena.view(104448, [128, 512], F32)
        tmp, tmpk = self.arena.view(106496, [128, 512], F32)
        bg1, bg1k = self.rep_at(110592, "bg1", D, tag="repbg1")
        wg2, wg2k = self.rep_at(114688, "Wg2T", D, tag="repwg2")
        bg2, bg2k = self.rep_at(118784, "bg2", 1, tag="repbg2")
        wsrc = self.din["Wg1"]
        for eo in range(2):
            src = wsrc[:, eo * 512:eo * 512 + 512].rearrange("(c p) n -> p c n", p=128)
            bk.dma("sync", [(wbuf[:, :, :], src)], writes=wkeys, tag=f"l0g_w{eo}")
            for tc in range(NT):
                pbank = self.pb[tc % 4]
                pk = f"pb{tc % 4}"

                def emit_mm(e, tc=tc, pbank=pbank):
                    insts = []
                    col = 1 + tc * 128
                    for dc in range(ND):
                        insts.append(e.matmul(pbank[:, :], self.xnFM[:, dc, col:col + 128],
                                              wbuf[:, dc, :], start=(dc == 0), stop=False))
                    for dc in range(ND):
                        insts.append(e.matmul(pbank[:, :], self.xnFM[:, dc, col - 1:col + 127],
                                              wbuf[:, ND + dc, :], start=False, stop=(dc == ND - 1)))
                    return insts
                self.mm(emit_mm, reads=wkeys + self.xnFM_keys, writes=(pk,), tag=f"l0g_mm{eo}_{tc}")

                def emit_ev(e, eo=eo, pbank=pbank):
                    return e.scalar_tensor_tensor(gh[:, :], pbank[:, :], 1.0,
                                                  bg1[:, eo * 512:eo * 512 + 512], ALU.mult, ALU.add)
                self.v(emit_ev, reads=(pk,) + bg1k, writes=ghk, tag=f"l0g_ev{eo}_{tc}")
                self.s(lambda e: e.activation(gh2[:, :], gh[:, :], AF.Gelu),
                       reads=ghk, writes=gh2k, tag=f"l0g_gelu{eo}_{tc}")

                def emit_dot(e, tc=tc, eo=eo):
                    e.tensor_tensor(tmp[:, :], gh2[:, :], wg2[:, eo * 512:eo * 512 + 512], ALU.mult)
                    e.reduce_sum(self.sm[:, 35:36], tmp[:, :], axis=AX)
                    if eo == 0:
                        return e.tensor_copy(self.vg[:, tc:tc + 1], self.sm[:, 35:36])
                    return e.tensor_tensor(self.vg[:, tc:tc + 1], self.vg[:, tc:tc + 1],
                                           self.sm[:, 35:36], ALU.add)
                self.v(emit_dot, reads=gh2k + wg2k, writes=("vg", "sm35") + tmpk,
                       tag=f"l0g_dot{eo}_{tc}")

        def emit_fin(e):
            return e.tensor_scalar(self.vg[:, :], self.vg[:, :], bg2[:, 0:1], None, ALU.add)
        self.v(emit_fin, reads=("vg",) + bg2k, writes=("vg",), tag="l0g_bias")
        self.s(lambda e: e.activation(self.vg[:, :], self.vg[:, :], AF.Sigmoid),
               reads=("vg",), writes=("vg",), tag="l0g_sig")

    def l0_blendw(self):
        bk = self.bk
        wbuf, wkeys = self.arena.view(69632, [128, ND, 512], F32)
        bh, bhk = self.arena.view(86016, [128, 512], F32)
        bh2, bh2k = self.arena.view(88064, [128, 512], F32)
        tmp, tmpk = self.arena.view(90112, [128, 512], F32)
        bb1, bb1k = self.rep_at(92160, "bb1", 512, tag="repbb1")
        bb2, bb2k = self.rep_at(94208, "bb2", 2, tag="repbb2")
        wb2, wb2k = self.arena.view(96256, [128, 2, 512], F32)
        bk.dma("gpsimd",
               [(wb2[:, 0, :], self.din["Wb2T"][0:1, :].partition_broadcast(128).squeeze(1)),
                (wb2[:, 1, :], self.din["Wb2T"][1:2, :].partition_broadcast(128).squeeze(1))],
               writes=wb2k, tag="repwb2")
        src = self.din["Wb1"].rearrange("(c p) n -> p c n", p=128)
        bk.dma("sync", [(wbuf[:, :, :], src)], writes=wkeys, tag="l0b_w")
        z0 = self.sm[:, 36:37]
        z1 = self.sm[:, 37:38]
        for tc in range(NT):
            pbank = self.pb[tc % 4]
            pk = f"pb{tc % 4}"

            def emit_mm(e, tc=tc, pbank=pbank):
                col = 1 + tc * 128
                return [e.matmul(pbank[:, :], self.xnFM[:, dc, col:col + 128],
                                 wbuf[:, dc, :], start=(dc == 0), stop=(dc == ND - 1))
                        for dc in range(ND)]
            self.mm(emit_mm, reads=wkeys + self.xnFM_keys, writes=(pk,), tag=f"l0b_mm{tc}")

            def emit_ev(e, pbank=pbank):
                return e.scalar_tensor_tensor(bh[:, :], pbank[:, :], 1.0, bb1[:, :],
                                              ALU.mult, ALU.add)
            self.v(emit_ev, reads=(pk,) + bb1k, writes=bhk, tag=f"l0b_ev{tc}")
            self.s(lambda e: e.activation(bh2[:, :], bh[:, :], AF.Gelu),
                   reads=bhk, writes=bh2k, tag=f"l0b_gelu{tc}")

            def emit_dots(e, tc=tc):
                e.tensor_tensor(tmp[:, :], bh2[:, :], wb2[:, 0, :], ALU.mult)
                e.reduce_sum(z0, tmp[:, :], axis=AX)
                e.tensor_tensor(tmp[:, :], bh2[:, :], wb2[:, 1, :], ALU.mult)
                e.reduce_sum(z1, tmp[:, :], axis=AX)
                e.tensor_tensor(z0, z0, z1, ALU.subtract)
                e.tensor_tensor(z1, bb2[:, 0:1], bb2[:, 1:2], ALU.subtract)
                return e.tensor_tensor(self.bw0p[:, tc:tc + 1], z0, z1, ALU.add)
            self.v(emit_dots, reads=bh2k + wb2k + bb2k,
                   writes=("bw0p", "sm36", "sm37") + tmpk, tag=f"l0b_dot{tc}")
        self.s(lambda e: e.activation(self.bw1p[:, :], self.bw0p[:, :], AF.Sigmoid, scale=-1.0),
               reads=("bw0p",), writes=("bw1p",), tag="l0b_sig1")
        self.s(lambda e: e.activation(self.bw0p[:, :], self.bw0p[:, :], AF.Sigmoid),
               reads=("bw0p",), writes=("bw0p",), tag="l0b_sig0")
        self.v(lambda e: e.tensor_scalar(self.bw0p[:, :], self.bw0p[:, :], 1.0 / 32.0, None, ALU.mult),
               reads=("bw0p",), writes=("bw0p",), tag="l0b_sc0")

    def l0_gatecum(self):
        self.v(lambda e: e.memset(self.gcar[:, :], 0.0), writes=("gcar",), tag="l0gc_z")
        pbank = self.pb[6]
        pcar = self.pb[7]
        for tc in range(NT):
            def emit_mm(e, tc=tc):
                e.matmul(pbank[:, 0:1], self.tri[:, :], self.vg[:, tc:tc + 1], start=True, stop=False)
                e.matmul(pbank[:, 0:1], self.ones1[:, :], self.gcar[:, :],
                         start=False, stop=True)
                e.matmul(pcar[0:1, 0:1], self.onescol[:, :], self.vg[:, tc:tc + 1],
                         start=True, stop=False)
                return e.matmul(pcar[0:1, 0:1], self.one1[:, :], self.gcar[:, :],
                                start=False, stop=True)
            self.mm(emit_mm, reads=("vg", "gcar", "consts", "consts2"),
                    writes=("pb6", "pb7"), tag=f"l0gc_mm{tc}")

            def emit_ex(e, tc=tc):
                e.tensor_copy(self.gcar[:, :], pcar[0:1, 0:1])
                return e.tensor_scalar(self.sm[:, 39 + tc:40 + tc], pbank[:, 0:1], 1.0, None, ALU.max)
            self.v(emit_ex, reads=("pb6", "pb7"), writes=("gcar", f"gcm:{tc}"), tag=f"l0gc_ex{tc}")
        allg = tuple(f"gcm:{tc}" for tc in range(NT))
        self.s(lambda e: e.activation(self.sm[:, 39:39 + NT], self.sm[:, 39:39 + NT], AF.Sqrt),
               reads=allg, writes=allg, tag="l0gc_sqrt")
        self.v(lambda e: e.reciprocal(self.sm[:, 39:39 + NT], self.sm[:, 39:39 + NT]),
               reads=allg, writes=allg, tag="l0gc_rec")

        def emit_bw1(e):
            e.tensor_tensor(self.bw1p[:, :], self.bw1p[:, :], self.sm[:, 39:39 + NT], ALU.mult)
            return e.tensor_scalar(self.bw1p[:, :], self.bw1p[:, :], 1.0 / 32.0, None, ALU.mult)
        self.v(emit_bw1, reads=allg + ("bw1p",), writes=("bw1p",), tag="l0gc_bw1")

    def carry_ap(self, ci):
        return self.carries[ci][0:1, :]

    def cumsum_pair(self, za, zak, zb, zbk, ci_a, ci_b, tag):
        def emit_mm(e, ci_a=ci_a, ci_b=ci_b):
            insts = []
            for h in range(2):
                ca = self.carry_ap(ci_a)[:, h * 512:(h + 1) * 512]
                cb = self.carry_ap(ci_b)[:, h * 512:(h + 1) * 512]
                insts.append(e.matmul(self.pb[4 + h][:, :], self.tri[:, :],
                                      za[:, h * 512:(h + 1) * 512], start=True, stop=False))
                insts.append(e.matmul(self.pb[4 + h][:, :], self.ones1[:, :], ca,
                                      start=False, stop=True))
                insts.append(e.matmul(self.pb[6 + h][:, :], self.tri[:, :],
                                      zb[:, h * 512:(h + 1) * 512], start=True, stop=False))
                insts.append(e.matmul(self.pb[6 + h][:, :], self.ones1[:, :], cb,
                                      start=False, stop=True))
                insts.append(e.matmul(self.pb[0 + h][0:1, :], self.onescol[:, :],
                                      za[:, h * 512:(h + 1) * 512], start=True, stop=False))
                insts.append(e.matmul(self.pb[0 + h][0:1, :], self.one1[:, :], ca,
                                      start=False, stop=True))
                insts.append(e.matmul(self.pb[2 + h][0:1, :], self.onescol[:, :],
                                      zb[:, h * 512:(h + 1) * 512], start=True, stop=False))
                insts.append(e.matmul(self.pb[2 + h][0:1, :], self.one1[:, :], cb,
                                      start=False, stop=True))
            return insts
        self.mm(emit_mm, reads=tuple(zak) + tuple(zbk) + ("carry", "consts", "consts2"),
                writes=("pb0", "pb1", "pb2", "pb3", "pb4", "pb5", "pb6", "pb7"), tag=f"{tag}mm")

        def emit_car(e, ci_a=ci_a, ci_b=ci_b):
            for h in range(2):
                e.tensor_copy(self.carry_ap(ci_a)[:, h * 512:(h + 1) * 512],
                              self.pb[0 + h][0:1, :])
            e.tensor_copy(self.carry_ap(ci_b)[:, 0:512], self.pb[2][0:1, :])
            return e.tensor_copy(self.carry_ap(ci_b)[:, 512:1024], self.pb[3][0:1, :])
        self.v(emit_car, reads=("pb0", "pb1", "pb2", "pb3"), writes=("carry",), tag=f"{tag}car")

    def l0_chains(self, pvD, kvD, thD, lnD):
        bk = self.bk
        A = self.arena
        pv, pvk = A.view(0, [128, D], F32)
        kv, kvk = A.view(4096, [128, D], F32)
        th, thk = A.view(8192, [128, D], F32)
        pp, ppk = A.view(12288, [128, D], F32)
        sa, sak = A.view(16384, [128, D], F32)
        sb_, sbk = A.view(20480, [128, D], F32)
        sc_, sck = A.view(24576, [128, D], F32)
        sd_, sdk = A.view(28672, [128, D], F32)
        ii, iik = A.view(32768, [128, D], I32)
        kc = [A.view(36864, [128, D], F32), A.view(40960, [128, D], F32)]
        ks = [A.view(45056, [128, D], F32), A.view(49152, [128, D], F32)]
        kcs, kcsk = A.view(53248, [128, D], F32)
        kss, kssk = A.view(57344, [128, D], F32)
        pr, prk = A.view(61440, [128, D], F32)
        kr, krk = A.view(65536, [128, D], F32)
        lnt, lntk = A.view(69632, [128, D], F32)
        pst, pstk = A.view(73728, [128, D], F32)
        pct, pctk = A.view(77824, [128, D], F32)
        lng, lngk = self.rep_at(81920, "ln_g", D, tag="replng")
        lnb, lnbk = self.rep_at(86016, "ln_b", D, tag="replnb")
        def emit_cz(e):
            for i in range(4):
                e.memset(self.carries[i][:, :], 0.0)
            return e.memset(self.carries[3][:, :], 0.0)
        self.v(emit_cz, writes=("carry",), tag="l0c_z")

        pvr = pvD.rearrange("(tc p) d -> tc p d", p=128)
        kvr = kvD.rearrange("(tc p) d -> tc p d", p=128)
        thr = thD.rearrange("(tc p) d -> tc p d", p=128)
        lnr = lnD.rearrange("(tc p) d -> tc p d", p=128)
        ppsrc = self.din["pos_phases"]

        for tc in range(NT):
            p = tc % 2
            q = 1 - p
            bk.dma("sync", [(pp[:, :], ppsrc[tc * 128:(tc + 1) * 128, :]),
                            (pv[:, :], pvr[tc, :, :]),
                            (kv[:, :], kvr[tc, :, :]),
                            (th[:, :], thr[tc, :, :])],
                   reads=(f"l0pvD:{tc}", f"l0kvD:{tc}", f"l0thD:{tc}"),
                   writes=ppk + pvk + kvk + thk, tag=f"l0c_ld{tc}")

            def emit_rr(e):
                e.tensor_scalar(sa[:, :], pp[:, :], 1.0 / TWO_PI, None, ALU.mult)
                e.tensor_copy(ii[:, :], sa[:, :])
                e.tensor_copy(sa[:, :], ii[:, :])
                return e.scalar_tensor_tensor(sa[:, :], sa[:, :], -TWO_PI, pp[:, :],
                                              ALU.mult, ALU.add)
            self.v(emit_rr, reads=ppk, writes=sak + iik, tag=f"l0c_rr{tc}")

            def emit_trig(e):
                e.activation(pst[:, :], sa[:, :], AF.Sin)
                e.activation(sb_[:, :], sa[:, :], AF.Abs)
                return e.activation(pct[:, :], sb_[:, :], AF.Sin, bias=self.halfpi[:, 0:1], scale=-1.0)
            self.s(emit_trig, reads=sak + ("consts2",), writes=pstk + pctk + sbk, tag=f"l0c_trig{tc}")

            def emit_ints(e):
                e.tensor_tensor(sa[:, :], pct[:, :], pv[:, :], ALU.mult)
                return e.tensor_tensor(sb_[:, :], pst[:, :], pv[:, :], ALU.mult)
            self.v(emit_ints, reads=pctk + pstk + pvk, writes=sak + sbk, tag=f"l0c_int{tc}")
            self.cumsum_pair(sa, sak, sb_, sbk, 0, 1, f"l0c_pcs{tc}")

            def emit_pr(e):
                for h in range(2):
                    e.tensor_tensor(pr[:, h * 512:(h + 1) * 512], pct[:, h * 512:(h + 1) * 512],
                                    self.pb[4 + h][:, :], ALU.mult)
                    e.tensor_tensor(sc_[:, h * 512:(h + 1) * 512], pst[:, h * 512:(h + 1) * 512],
                                    self.pb[6 + h][:, :], ALU.mult)
                return e.tensor_tensor(pr[:, :], pr[:, :], sc_[:, :], ALU.add)
            self.v(emit_pr, reads=("pb4", "pb5", "pb6", "pb7") + pctk + pstk,
                   writes=prk + sck, tag=f"l0c_pr{tc}")

            def emit_ktrig(e, p=p):
                e.activation(ks[p][0][:, :], th[:, :], AF.Sin, scale=PI)
                e.activation(sb_[:, :], th[:, :], AF.Abs)
                return e.activation(kc[p][0][:, :], sb_[:, :], AF.Sin, bias=self.halfpi[:, 0:1], scale=-PI)
            self.s(emit_ktrig, reads=thk + ("consts2",), writes=ks[p][1] + kc[p][1] + sbk, tag=f"l0c_kt{tc}")

            if tc == 0:
                def emit_shz(e):
                    e.memset(kcs[0:1, :], 0.0)
                    return e.memset(kss[0:1, :], 0.0)
                self.v(emit_shz, writes=kcsk + kssk, tag="l0c_shz")
                bk.dma("gpsimd", [(kcs[1:128, :], kc[p][0][0:127, :]),
                                  (kss[1:128, :], ks[p][0][0:127, :])],
                       reads=kc[p][1] + ks[p][1], writes=kcsk + kssk, tag=f"l0c_sh{tc}")
            else:
                bk.dma("gpsimd", [(kcs[1:128, :], kc[p][0][0:127, :]),
                                  (kcs[0:1, :], kc[q][0][127:128, :]),
                                  (kss[1:128, :], ks[p][0][0:127, :]),
                                  (kss[0:1, :], ks[q][0][127:128, :])],
                       reads=kc[p][1] + ks[p][1] + kc[q][1] + ks[q][1],
                       writes=kcsk + kssk, tag=f"l0c_sh{tc}")

            def emit_kint(e, tc=tc):
                e.tensor_scalar(sc_[:, :], kv[:, :], self.vg[:, tc:tc + 1], None, ALU.mult)
                e.tensor_tensor(sa[:, :], kcs[:, :], sc_[:, :], ALU.mult)
                return e.tensor_tensor(sb_[:, :], kss[:, :], sc_[:, :], ALU.mult)
            self.v(emit_kint, reads=kvk + kcsk + kssk + ("vg",), writes=sak + sbk + sck,
                   tag=f"l0c_kint{tc}")
            self.cumsum_pair(sa, sak, sb_, sbk, 2, 3, f"l0c_kcs{tc}")

            def emit_kr(e, p=p):
                for h in range(2):
                    e.tensor_tensor(kr[:, h * 512:(h + 1) * 512],
                                    kc[p][0][:, h * 512:(h + 1) * 512],
                                    self.pb[4 + h][:, :], ALU.mult)
                    e.tensor_tensor(sc_[:, h * 512:(h + 1) * 512],
                                    ks[p][0][:, h * 512:(h + 1) * 512],
                                    self.pb[6 + h][:, :], ALU.mult)
                return e.tensor_tensor(kr[:, :], kr[:, :], sc_[:, :], ALU.add)
            self.v(emit_kr, reads=("pb4", "pb5", "pb6", "pb7") + kc[p][1] + ks[p][1],
                   writes=krk + sck, tag=f"l0c_kr{tc}")

            def emit_bl(e, tc=tc):
                e.tensor_scalar(sd_[:, :], pr[:, :], self.bw0p[:, tc:tc + 1], None, ALU.mult)
                return e.scalar_tensor_tensor(sd_[:, :], kr[:, :], self.bw1p[:, tc:tc + 1],
                                              sd_[:, :], ALU.mult, ALU.add)
            self.v(emit_bl, reads=prk + krk + ("bw0p", "bw1p"), writes=sdk, tag=f"l0c_bl{tc}")

            self.v(lambda e: e.reduce_sum(self.sm[:, 55:56], sd_[:, :], axis=AX),
                   reads=sdk, writes=("sm55",), tag=f"l0c_ls1{tc}")
            self.s(lambda e: e.activation(self.sqdump[:, :], sd_[:, :], AF.Square,
                                          accum_out=self.sm[:, 56:57]),
                   reads=sdk, writes=("sm56", "sqdump"), tag=f"l0c_ls2{tc}")

            def emit_lsm(e):
                e.tensor_scalar(self.sm[:, 55:56], self.sm[:, 55:56], -1.0 / D, None, ALU.mult)
                e.tensor_tensor(self.sm[:, 57:58], self.sm[:, 55:56], self.sm[:, 55:56], ALU.mult)
                e.tensor_scalar(self.sm[:, 56:57], self.sm[:, 56:57], 1.0 / D, None, ALU.mult)
                e.tensor_tensor(self.sm[:, 56:57], self.sm[:, 56:57], self.sm[:, 57:58],
                                ALU.subtract)
                e.tensor_scalar(self.sm[:, 56:57], self.sm[:, 56:57], 1e-5, None, ALU.add)
                return e.reciprocal(self.sm[:, 56:57], self.sm[:, 56:57])
            self.v(emit_lsm, reads=("sm55", "sm56"), writes=("sm55", "sm56", "sm57"),
                   tag=f"l0c_lsm{tc}")
            self.s(lambda e: e.activation(self.sm[:, 56:57], self.sm[:, 56:57], AF.Sqrt),
                   reads=("sm56",), writes=("sm56",), tag=f"l0c_lsq{tc}")

            def emit_lap(e):
                e.tensor_scalar(sd_[:, :], sd_[:, :], self.sm[:, 55:56], self.sm[:, 56:57],
                                ALU.add, ALU.mult)
                e.tensor_tensor(lnt[:, :], sd_[:, :], lng[:, :], ALU.mult)
                return e.tensor_tensor(lnt[:, :], lnt[:, :], lnb[:, :], ALU.add)
            self.v(emit_lap, reads=sdk + ("sm55", "sm56") + lngk + lnbk,
                   writes=lntk + sdk, tag=f"l0c_lap{tc}")
            bk.dma("gpsimd", [(lnr[tc, :, :], lnt[:, :])], reads=lntk,
                   writes=(f"lnD:{tc}",), tag=f"l0c_lst{tc}")

    def l0_wo(self, lnD, sD):
        bk = self.bk
        A = self.arena
        wo, wok = A.view(0, [128, ND, D], F32)
        lnt, lntk = A.view(36864, [128, D], F32)
        lfm, lfmk = A.view(40960, [128, ND, 128], F32)
        xb, xbk = A.view(45056, [128, D], F32)
        stg, stgk = A.view(49152, [128, D], F32)
        bo0, bo0k = self.rep_at(53248, "bo0", D, tag="repbo0")
        wsrc = self.din["Wo0"].rearrange("(c p) n -> p c n", p=128)
        bk.dma("sync", [(wo[:, :, 0:512], wsrc[:, :, 0:512]),
                        (wo[:, :, 512:1024], wsrc[:, :, 512:1024])], writes=wok, tag="l0wo_w")
        lnr = lnD.rearrange("(tc p) d -> tc p d", p=128)
        sr = sD.rearrange("(tc p) d -> tc p d", p=128)
        for tc in range(NT):
            bk.dma("sync", [(lnt[:, :], lnr[tc, :, :])], reads=(f"lnD:{tc}",), writes=lntk,
                   tag=f"l0wo_ld{tc}")
            self.transpose128(lnt, lntk, lfm, lfmk, ND, f"l0wo_t{tc}")
            self.xn_tile(xb[:, :], xbk, tc, f"l0wo_xn{tc}")
            self.v(lambda e: e.tensor_tensor(xb[:, :], xb[:, :], bo0[:, :], ALU.add),
                   reads=xbk + bo0k, writes=xbk, tag=f"l0wo_xb{tc}")
            for eo in range(2):
                pbank = self.pb[eo]
                pk = f"pb{eo}"

                def emit_mm(e, eo=eo, pbank=pbank):
                    return [e.matmul(pbank[:, :], lfm[:, dc, :],
                                     wo[:, dc, eo * 512:eo * 512 + 512],
                                     start=(dc == 0), stop=(dc == ND - 1)) for dc in range(ND)]
                self.mm(emit_mm, reads=lfmk + wok, writes=(pk,), tag=f"l0wo_mm{tc}_{eo}")

                def emit_sev(e, eo=eo, pbank=pbank):
                    return e.scalar_tensor_tensor(stg[:, eo * 512:eo * 512 + 512], pbank[:, :],
                                                  1.0, xb[:, eo * 512:eo * 512 + 512],
                                                  ALU.mult, ALU.add)
                self.v(emit_sev, reads=(pk,) + xbk, writes=stgk, tag=f"l0wo_sev{tc}_{eo}")
            bk.dma("gpsimd", [(sr[tc, :, :], stg[:, :])], reads=stgk, writes=(f"sD:{tc}",),
                   tag=f"l0wo_st{tc}")

    def l0_ffn(self, w1qD, w2qD, hD, sD):
        bk = self.bk
        A = self.arena
        qfm, qfmk = A.view(0, [128, ND, T], BF16)
        xnt, xntk = A.view(36864, [128, D], F32)
        r1, r1k = A.view(40960, [128, D], F32)
        ii, iik = A.view(45056, [128, D], I32)
        stok1 = self.sm[:, 0:NT]
        hr = hD.rearrange("(tc p) f -> tc p f", p=128)
        sr = sD.rearrange("(tc p) d -> tc p d", p=128)

        for tc in range(NT):
            self.xn_tile(xnt[:, :], xntk, tc, f"l0f_xn{tc}")
            self.s(lambda e: e.activation(self.sqdump[:, :], xnt[:, :], AF.Square,
                                          accum_out=self.sm[:, 58:59]),
                   reads=xntk, writes=("sm58", "sqdump"), tag=f"l0f_s2{tc}")

            def emit_rms(e):
                e.tensor_scalar(self.sm[:, 58:59], self.sm[:, 58:59], 1.0 / D, 1e-5,
                                ALU.mult, ALU.add)
                return e.reciprocal(self.sm[:, 58:59], self.sm[:, 58:59])
            self.v(emit_rms, reads=("sm58",), writes=("sm58",), tag=f"l0f_rms{tc}")
            self.s(lambda e: e.activation(self.sm[:, 58:59], self.sm[:, 58:59], AF.Sqrt),
                   reads=("sm58",), writes=("sm58",), tag=f"l0f_rsq{tc}")

            def emit_ram(e, tc=tc):
                e.tensor_scalar(r1[:, :], xnt[:, :], self.sm[:, 58:59], None, ALU.mult)
                e.tensor_reduce(self.sm[:, 59:60], r1[:, :], AX, ALU.max, apply_absolute_value=True)
                e.tensor_scalar(self.sm[:, 59:60], self.sm[:, 59:60], 1e-5, None, ALU.max)
                e.reciprocal(self.sm[:, 60:61], self.sm[:, 59:60])
                return e.tensor_scalar(stok1[:, tc:tc + 1], self.sm[:, 59:60], 1.0 / 127.0,
                                       None, ALU.mult)
            self.v(emit_ram, reads=xntk + ("sm58",), writes=r1k + ("sm59", "sm60", "stok1"),
                   tag=f"l0f_ram{tc}")

            def emit_q(e):
                e.tensor_scalar(r1[:, :], r1[:, :], self.sm[:, 60:61], 127.0, ALU.mult, ALU.mult)
                e.tensor_copy(ii[:, :], r1[:, :])
                e.tensor_copy(r1[:, :], ii[:, :])
                return e.tensor_scalar(r1[:, :], r1[:, :], -128.0, 127.0, ALU.max, ALU.min)
            self.v(emit_q, reads=("sm60",) + r1k, writes=r1k + iik, tag=f"l0f_q{tc}")
            qdst = qfm[:, :, tc * 128:tc * 128 + 128]
            self.transpose128(r1, r1k, qdst, qfmk, ND, f"l0f_t{tc}")

        def emit_stok(e):
            return e.tensor_tensor(stok1, stok1, self.dqw1[:, 0:1].broadcast_to([128, NT]),
                                   ALU.mult)
        self.v(emit_stok, reads=("stok1", "dq_dqw1"), writes=("stok1",), tag="l0f_stok")
        bk.tap("l0_qfm", qfm[:, :, :], reads=qfmk)
        bk.tap("l0_dqw1", self.dqw1[:, :], reads=("dq_dqw1",))

        w1b, w1bk = A.view(49152, [128, 2, ND, 512], BF16)
        h1 = len(w1bk) // 2
        w1kk = [w1bk[:h1], w1bk[h1:]] if h1 else [w1bk, w1bk]
        hst, hstk = A.view(65536, [128, 512], F32)
        hst2, hst2k = A.view(67584, [128, 512], F32)
        f0b1, f0b1k = self.rep_at(69632, "f0b1", 4096, tag="repf0b1")
        w1src = w1qD.rearrange("(c p) f -> p c f", p=128)
        for fo in range(8):
            fb = fo % 2
            bk.dma("sync", [(w1b[:, fb, :, :], w1src[:, :, fo * 512:fo * 512 + 512])],
                   reads=tuple(f"dram_dqw1:{c}" for c in range(ND)),
                   writes=w1kk[fb], tag=f"l0f_w1{fo}")
            for tc in range(NT):
                pbank = self.pb[tc % 4]
                pk = f"pb{tc % 4}"

                def emit_mm(e, tc=tc, fb=fb, pbank=pbank):
                    return [e.matmul(pbank[:, :], qfm[:, dc, tc * 128:tc * 128 + 128],
                                     w1b[:, fb, dc, :], start=(dc == 0), stop=(dc == ND - 1))
                            for dc in range(ND)]
                self.mm(emit_mm, reads=w1kk[fb] + qfmk, writes=(pk,), tag=f"l0f_hmm{fo}_{tc}")

                def emit_ev(e, tc=tc, fo=fo, pbank=pbank):
                    return e.scalar_tensor_tensor(hst[:, :], pbank[:, :], stok1[:, tc:tc + 1],
                                                  f0b1[:, fo * 512:fo * 512 + 512],
                                                  ALU.mult, ALU.add)
                self.v(emit_ev, reads=(pk, "stok1") + f0b1k, writes=hstk, tag=f"l0f_hev{fo}_{tc}")
                self.s(lambda e: e.activation(hst2[:, :], hst[:, :], AF.Gelu),
                       reads=hstk, writes=hst2k, tag=f"l0f_hg{fo}_{tc}")
                bk.dma("gpsimd", [(hr[tc, :, fo * 512:fo * 512 + 512], hst2[:, :])],
                       reads=hst2k, writes=(f"hD:{tc}:{fo}",), tag=f"l0f_hst{fo}_{tc}")

        hq, hqk = A.view(0, [128, 4096], F32)
        hqi, hqik = A.view(16384, [128, 4096], I32)
        hfm, hfmk = A.view(32768, [128, 32, 128], BF16)
        w2b, w2bk = A.view(40960, [128, 32, 512], BF16)
        gst, gstk = A.view(75776, [128, D], F32)
        sst, sstk = A.view(79872, [128, D], F32)
        f0b2, f0b2k = self.rep_at(84992, "f0b2", D, tag="repf0b2")
        stok2 = self.sm[:, 61:62]
        w2src = w2qD.rearrange("(c p) d -> p c d", p=128)
        for tc in range(NT):
            bk.dma("sync", [(hq[:, :], hr[tc, :, :])],
                   reads=tuple(f"hD:{tc}:{fo}" for fo in range(8)), writes=hqk,
                   tag=f"l0f_hld{tc}")
            for j in range(4):
                self.s(lambda e, j=j: e.activation(self.sqdump[:, :], hq[:, j * D:(j + 1) * D],
                                                   AF.Square, accum_out=self.sm[:, 16 + j:17 + j]),
                       reads=hqk, writes=(f"sm{16 + j}", "sqdump"), tag=f"l0f2_s2{tc}_{j}")

            def emit_rms(e):
                e.tensor_reduce(self.sm[:, 58:59], self.sm[:, 16:20], AX, ALU.add)
                e.tensor_scalar(self.sm[:, 58:59], self.sm[:, 58:59], 1.0 / 4096.0, 1e-5,
                                ALU.mult, ALU.add)
                return e.reciprocal(self.sm[:, 58:59], self.sm[:, 58:59])
            self.v(emit_rms, reads=("sm16", "sm17", "sm18", "sm19"), writes=("sm58",),
                   tag=f"l0f2_rms{tc}")
            self.s(lambda e: e.activation(self.sm[:, 58:59], self.sm[:, 58:59], AF.Sqrt),
                   reads=("sm58",), writes=("sm58",), tag=f"l0f2_rsq{tc}")

            def emit_rq(e, tc=tc):
                e.tensor_scalar(hq[:, :], hq[:, :], self.sm[:, 58:59], None, ALU.mult)
                e.tensor_reduce(self.sm[:, 59:60], hq[:, :], AX, ALU.max, apply_absolute_value=True)
                e.tensor_scalar(self.sm[:, 59:60], self.sm[:, 59:60], 1e-5, None, ALU.max)
                e.reciprocal(self.sm[:, 60:61], self.sm[:, 59:60])
                e.tensor_scalar(hq[:, :], hq[:, :], self.sm[:, 60:61], 127.0, ALU.mult, ALU.mult)
                e.tensor_copy(hqi[:, :], hq[:, :])
                e.tensor_copy(hq[:, :], hqi[:, :])
                e.tensor_scalar(hq[:, :], hq[:, :], -128.0, 127.0, ALU.max, ALU.min)
                e.tensor_scalar(stok2, self.sm[:, 59:60], 1.0 / 127.0, None, ALU.mult)
                return e.tensor_tensor(stok2, stok2, self.dqw2[:, 0:1], ALU.mult)
            self.v(emit_rq, reads=hqk + ("sm58", "dq_dqw2"),
                   writes=hqk + hqik + ("sm59", "sm60", "sm61"), tag=f"l0f2_q{tc}")

            self.transpose128(hq, hqk, hfm, hfmk, 32, f"l0f2_t{tc}")

            for eo in range(2):
                bk.dma("sync", [(w2b[:, 0:16, :], w2src[:, 0:16, eo * 512:eo * 512 + 512]),
                                (w2b[:, 16:32, :], w2src[:, 16:32, eo * 512:eo * 512 + 512])],
                       reads=tuple(f"dram_dqw2:{c}" for c in range(32)), writes=w2bk,
                       tag=f"l0f2_w2_{tc}_{eo}")
                pbank = self.pb[eo]
                pk = f"pb{eo}"

                def emit_mm(e, pbank=pbank):
                    return [e.matmul(pbank[:, :], hfm[:, fc, :], w2b[:, fc, :],
                                     start=(fc == 0), stop=(fc == 31)) for fc in range(32)]
                self.mm(emit_mm, reads=w2bk + hfmk, writes=(pk,), tag=f"l0f2_omm{tc}_{eo}")

                def emit_ev(e, eo=eo, pbank=pbank):
                    return e.scalar_tensor_tensor(gst[:, eo * 512:eo * 512 + 512], pbank[:, :],
                                                  stok2, f0b2[:, eo * 512:eo * 512 + 512],
                                                  ALU.mult, ALU.add)
                self.v(emit_ev, reads=(pk, "sm61") + f0b2k, writes=gstk, tag=f"l0f2_oev{tc}_{eo}")
            self.s(lambda e: e.activation(gst[:, :], gst[:, :], AF.Silu),
                   reads=gstk, writes=gstk, tag=f"l0f2_silu{tc}")

            bk.dma("sync", [(sst[:, :], sr[tc, :, :])], reads=(f"sD:{tc}",), writes=sstk,
                   tag=f"l0f2_sld{tc}")

            def emit_comb(e, tc=tc):
                e.tensor_tensor(sst[:, :], sst[:, :], gst[:, :], ALU.mult)
                return e.tensor_tensor(self.xTM[:, tc, :], self.xTM[:, tc, :], sst[:, :], ALU.add)
            self.v(emit_comb, reads=gstk + sstk + (f"xTM:{tc}",), writes=(f"xTM:{tc}",) + sstk,
                   tag=f"l0f2_comb{tc}")



    def layer3(self):
        bk = self.bk
        A = self.arena
        self.ln_stats("l3ln")
        self.build_xnFM("l3fm")
        qD = self.dscr["qD"]; kD = self.dscr["kD"]; vD = self.dscr["vD"]
        sD = self.dscr["sD"]
        sr = sD.rearrange("(tc p) d -> tc p d", p=128)

        latfm, latk = A.view(65664, [128, T], F32)
        wkd, wkdk = A.view(73856, [128, ND, 128], F32)
        bk.dma("sync", [(wkd[:, :, :], self.din["w3kvd"].rearrange("(c p) e -> p c e", p=128))],
               writes=wkdk, tag="l3_wkd")
        for tq in range(4):
            pbank = self.pb[tq % 2]
            pk = f"pb{tq % 2}"

            def emit_mm(e, tq=tq, pbank=pbank):
                return [e.matmul(pbank[:, :], wkd[:, dc, :],
                                 self.xnFM[:, dc, 1 + tq * 512:1 + tq * 512 + 512],
                                 start=(dc == 0), stop=(dc == ND - 1)) for dc in range(ND)]
            self.mm(emit_mm, reads=wkdk + self.xnFM_keys, writes=(pk,), tag=f"l3_lat{tq}")

            def emit_ev(e, tq=tq, pbank=pbank):
                return e.activation(latfm[:, tq * 512:tq * 512 + 512], pbank[:, :], AF.Copy)
            self.s(emit_ev, reads=(pk,), writes=latk, tag=f"l3_latev{tq}")
        wku, wkuk = A.view(78080, [128, 2048], F32)
        stg4 = [A.view(86272, [128, 512], F32), A.view(88320, [128, 512], F32)]
        bk.dma("sync", [(wku[:, :], self.din["w3kvu"])], writes=wkuk, tag="l3_wku")
        for ec in range(8):
            for tq in range(4):
                pbank = self.pb[2 + tq % 2]
                pk = f"pb{2 + tq % 2}"

                def emit_mm(e, ec=ec, tq=tq, pbank=pbank):
                    return e.matmul(pbank[:, :], wku[:, ec * 128:(ec + 1) * 128],
                                    latfm[:, tq * 512:tq * 512 + 512], start=True, stop=True)
                self.mm(emit_mm, reads=wkuk + latk, writes=(pk,), tag=f"l3_kmm{ec}_{tq}")
                sb_, sbk = stg4[tq % 2]

                def emit_ev(e, pbank=pbank, sb_=sb_):
                    return e.activation(sb_[:, :], pbank[:, :], AF.Copy)
                self.s(emit_ev, reads=(pk,), writes=sbk, tag=f"l3_kev{ec}_{tq}")
                bk.dma("gpsimd", [(kD[ec * 128:(ec + 1) * 128, tq * 512:tq * 512 + 512],
                                   sb_[:, :])],
                       reads=sbk, writes=(f"l3kD:{ec}:{tq}",), tag=f"l3_kst{ec}_{tq}")
        for tc in range(NT):
            for eo in range(2):
                pbank = self.pb[4 + eo]
                pk = f"pb{4 + eo}"

                def emit_mm(e, tc=tc, eo=eo, pbank=pbank):
                    return e.matmul(pbank[:, :], latfm[:, tc * 128:tc * 128 + 128],
                                    wku[:, 1024 + eo * 512:1024 + eo * 512 + 512],
                                    start=True, stop=True)
                self.mm(emit_mm, reads=wkuk + latk, writes=(pk,), tag=f"l3_vmm{tc}_{eo}")
                sb_, sbk = stg4[eo]

                def emit_ev(e, pbank=pbank, sb_=sb_):
                    return e.activation(sb_[:, :], pbank[:, :], AF.Copy)
                self.s(emit_ev, reads=(pk,), writes=sbk, tag=f"l3_vev{tc}_{eo}")
                bk.dma("gpsimd", [(vD[tc * 128:(tc + 1) * 128, eo * 512:eo * 512 + 512],
                                   sb_[:, :])],
                       reads=sbk, writes=(f"l3vD:{tc}",), tag=f"l3_vst{tc}_{eo}")
        self.proj_fm_spill_nb("w3q", qD, "l3q")

        wg3, wg3k = A.view(73856, [128, ND, 8], F32)
        bk.dma("sync", [(wg3[:, :, :], self.din["w3g"].rearrange("(c p) e -> p c e", p=128))],
               writes=wg3k, tag="l3r_w")
        self.rweights = bk.sb("rweights", [128, NT, 8], F32)
        for tc in range(NT):
            pbank = self.pb[6]
            pk = "pb6"

            def emit_mm(e, tc=tc, pbank=pbank):
                return [e.matmul(pbank[:, 0:8], self.xnFM[:, dc, 1 + tc * 128:1 + tc * 128 + 128],
                                 wg3[:, dc, :], start=(dc == 0), stop=(dc == ND - 1))
                        for dc in range(ND)]
            self.mm(emit_mm, reads=wg3k + self.xnFM_keys, writes=(pk,), tag=f"l3r_mm{tc}")

            def emit_rt(e, tc=tc, pbank=pbank):
                lg = self.sm[:, 16:24]
                m1 = self.sm[:, 24:25]
                m2 = self.sm[:, 25:26]
                t1 = self.sm[:, 26:34]
                e.tensor_copy(lg, pbank[:, 0:8])
                e.tensor_reduce(m1, lg, AX, ALU.max)
                e.tensor_scalar(t1, lg, m1, None, ALU.is_ge)
                e.tensor_scalar(t1, t1, 1e30, None, ALU.mult)
                e.tensor_tensor(t1, lg, t1, ALU.subtract)
                e.tensor_reduce(m2, t1, AX, ALU.max)
                e.tensor_scalar(self.sm[:, 34:35], m1, -1.0, None, ALU.mult)
                return e.tensor_scalar(self.rweights[:, tc, :], lg, m2, None, ALU.is_ge)
            self.v(emit_rt, reads=(pk,), writes=("rweights", "sm_l3r"), tag=f"l3r_t{tc}")

            def emit_e1(e, tc=tc):
                e.activation(self.sm[:, 16:24], self.sm[:, 16:24], AF.Exp,
                             bias=self.sm[:, 34:35])
                return e.activation(self.sm[:, 35:36], self.sm[:, 25:26], AF.Exp,
                                    bias=self.sm[:, 34:35])
            self.s(emit_e1, reads=("sm_l3r",), writes=("sm_l3r",), tag=f"l3r_e{tc}")

            def emit_w(e, tc=tc):
                e.tensor_scalar(self.sm[:, 35:36], self.sm[:, 35:36], 1.0, None, ALU.add)
                e.reciprocal(self.sm[:, 35:36], self.sm[:, 35:36])
                e.tensor_tensor(self.rweights[:, tc, :], self.rweights[:, tc, :],
                                self.sm[:, 16:24], ALU.mult)
                return e.tensor_scalar(self.rweights[:, tc, :], self.rweights[:, tc, :],
                                       self.sm[:, 35:36], None, ALU.mult)
            self.v(emit_w, reads=("sm_l3r", "rweights"), writes=("rweights", "sm_l3r"),
                   tag=f"l3r_w{tc}")

        attn3, attn3k = A.view(0, [128, ND, 512], F32)
        vpad3, vpad3k = A.view(16384, [128, 16, 65], F32)
        kh3, kh3k = A.view(20608, [128, T], F32)
        qh3, qh3k = A.view(28800, [128, 512], F32)
        expt3 = [A.view(30848, [128, 512], F32), A.view(32896, [128, 512], F32)]
        rrow3, rrow3k = A.view(34944, [128, 512], F32)
        rb3, rb3k = A.view(36992, [128, 512], F32)
        avtmp3, avt3k = A.view(39040, [128, 512], F32)
        cmskT, cmskTk = A.view(41088, [128, 4, 512], F32)
        wo3, wo3k = A.view(49280, [128, ND, D], F32)
        stg5, stg5k = A.view(82048, [128, D], F32)
        bk.dma("sync", [(cmskT[:, :, :], self.din["cmaskT"].rearrange("s p k -> p s k")),
                        (wo3[:, :, 0:512],
                         self.din["w3out"].rearrange("(c p) e -> p c e", p=128)[:, :, 0:512]),
                        (wo3[:, :, 512:1024],
                         self.din["w3out"].rearrange("(c p) e -> p c e", p=128)[:, :, 512:1024])],
               writes=cmskTk + wo3k, tag="l3a_w")
        self.v(lambda e: e.memset(vpad3[:, :, 64:65], 1.0), writes=vpad3k, tag="l3a_vones")
        vsrc3 = vD.rearrange("(kc p) (h dv) -> p kc h dv", p=128, dv=64)
        qrd3 = tuple(f"l3qD:{ec}:{tq}" for ec in range(ND) for tq in range(4))
        krd3 = tuple(f"l3kD:{ec}:{tq}" for ec in range(ND) for tq in range(4))
        vrd3 = tuple(f"l3vD:{tc}" for tc in range(NT))
        for qc in range(4):
            for h in range(16):
                bk.dma("sync", [(kh3[0:64, :], kD[h * 64:(h + 1) * 64, :]),
                                (qh3[0:64, :], qD[h * 64:(h + 1) * 64, qc * 512:qc * 512 + 512]),
                                (vpad3[:, :, 0:64], vsrc3[:, :, h, :])],
                       reads=qrd3 + krd3 + vrd3, writes=kh3k + qh3k + vpad3k,
                       tag=f"l3a_ld{qc}_{h}")
                nkc = 4 * (qc + 1)
                for kc in range(nkc):
                    sb_ = self.pb[2 + kc % 2]
                    spk = f"pb{2 + kc % 2}"

                    def emit_s(e, kc=kc, sb_=sb_):
                        return e.matmul(sb_[:, :], kh3[0:64, kc * 128:kc * 128 + 128],
                                        qh3[0:64, :], start=True, stop=True)
                    self.mm(emit_s, reads=kh3k + qh3k, writes=(spk,), tag=f"l3a_s{qc}_{h}_{kc}")
                    ex, exk = expt3[kc % 2]

                    def emit_e(e, sb_=sb_, ex=ex):
                        return e.activation(ex[:, :], sb_[:, :], AF.Exp, scale=0.125)
                    self.s(emit_e, reads=(spk,), writes=exk, tag=f"l3a_e{qc}_{h}_{kc}")
                    if kc >= 4 * qc:
                        s_ = kc - 4 * qc

                        def emit_m(e, s_=s_, ex=ex):
                            return e.tensor_tensor(ex[:, :], ex[:, :], cmskT[:, s_, :], ALU.mult)
                        self.v(emit_m, reads=exk + cmskTk, writes=exk,
                               tag=f"l3a_m{qc}_{h}_{kc}")

                    def emit_av(e, kc=kc, ex=ex, nkc=nkc):
                        return e.matmul(self.pb[4][0:65, :], vpad3[:, kc, :], ex[:, :],
                                        start=(kc == 0), stop=(kc == nkc - 1))
                    self.mm(emit_av, reads=expt3[kc % 2][1] + vpad3k, writes=("pb4",),
                            tag=f"l3a_av{qc}_{h}_{kc}")

                def emit_cp(e):
                    return e.activation(rrow3[64:65, :], self.pb[4][64:65, :], AF.Copy)
                self.s(emit_cp, reads=("pb4",), writes=rrow3k, tag=f"l3a_cp{qc}_{h}")
                bk.dma("gpsimd", [(rrow3[0:1, :], rrow3[64:65, :])], reads=rrow3k,
                       writes=rrow3k, tag=f"l3a_mvr{qc}_{h}")

                def emit_r(e):
                    return e.reciprocal(rrow3[0:1, :], rrow3[0:1, :])
                self.v(emit_r, reads=rrow3k, writes=rrow3k, tag=f"l3a_r{qc}_{h}")

                def emit_bc(e):
                    return e.matmul(self.pb[5][0:64, :], self.ones1[0:1, 0:64], rrow3[0:1, :],
                                    start=True, stop=True)
                self.mm(emit_bc, reads=rrow3k + ("consts",), writes=("pb5",),
                        tag=f"l3a_bc{qc}_{h}")

                def emit_rbe(e):
                    return e.activation(rb3[0:64, :], self.pb[5][0:64, :], AF.Copy)
                self.s(emit_rbe, reads=("pb5",), writes=rb3k, tag=f"l3a_rbe{qc}_{h}")
                if h % 2 == 0:
                    def emit_at(e, h=h):
                        return e.tensor_tensor(attn3[0:64, h // 2, :], self.pb[4][0:64, :],
                                               rb3[0:64, :], ALU.mult)
                    self.v(emit_at, reads=("pb4",) + rb3k, writes=attn3k, tag=f"l3a_at{qc}_{h}")
                else:
                    def emit_at(e):
                        return e.tensor_tensor(avtmp3[0:64, :], self.pb[4][0:64, :],
                                               rb3[0:64, :], ALU.mult)
                    self.v(emit_at, reads=("pb4",) + rb3k, writes=avt3k, tag=f"l3a_at{qc}_{h}")
                    bk.dma("gpsimd", [(attn3[64:128, h // 2, :], avtmp3[0:64, :])],
                           reads=avt3k, writes=attn3k, tag=f"l3a_mv{qc}_{h}")
            for ts_ in range(4):
                for eo in range(2):
                    pbank = self.pb[eo]
                    pk = f"pb{eo}"

                    def emit_mm(e, ts_=ts_, eo=eo, pbank=pbank):
                        return [e.matmul(pbank[:, :], attn3[:, c, ts_ * 128:ts_ * 128 + 128],
                                         wo3[:, c, eo * 512:eo * 512 + 512],
                                         start=(c == 0), stop=(c == ND - 1)) for c in range(ND)]
                    self.mm(emit_mm, reads=attn3k + wo3k, writes=(pk,),
                            tag=f"l3wo_mm{qc}_{ts_}_{eo}")

                    def emit_ev(e, eo=eo, pbank=pbank):
                        return e.activation(stg5[:, eo * 512:eo * 512 + 512], pbank[:, :],
                                            AF.Copy)
                    self.s(emit_ev, reads=(pk,), writes=stg5k, tag=f"l3wo_ev{qc}_{ts_}_{eo}")
                tc = qc * 4 + ts_
                bk.dma("gpsimd", [(sr[tc, :, :], stg5[:, :])], reads=stg5k,
                       writes=(f"sD:{tc}",), tag=f"l3wo_st{qc}_{ts_}")

        self.l3_moe(sD)

    def proj_fm_spill_nb(self, wnm, outD, tag):
        bk = self.bk
        A = self.arena
        wbuf, wkeys = A.view(90368, [128, 2, ND, 128], F32)
        h1 = len(wkeys) // 2
        wbk = [wkeys[:h1], wkeys[h1:]] if h1 else [wkeys, wkeys]
        stg = [A.view(98560, [128, 512], F32), A.view(100608, [128, 512], F32)]
        wsrc = self.din[wnm]
        for ec in range(ND):
            eb = ec % 2
            bk.dma("sync", [(wbuf[:, eb, :, :],
                             wsrc[:, ec * 128:(ec + 1) * 128].rearrange("(c p) e -> p c e", p=128))],
                   writes=wbk[eb], tag=f"{tag}w{ec}")
            for tq in range(4):
                pbank = self.pb[6 + tq % 2]
                pk = f"pb{6 + tq % 2}"

                def emit_mm(e, tq=tq, eb=eb, pbank=pbank):
                    return [e.matmul(pbank[:, :], wbuf[:, eb, dc, :],
                                     self.xnFM[:, dc, 1 + tq * 512:1 + tq * 512 + 512],
                                     start=(dc == 0), stop=(dc == ND - 1)) for dc in range(ND)]
                self.mm(emit_mm, reads=wbk[eb] + self.xnFM_keys, writes=(pk,),
                        tag=f"{tag}mm{ec}_{tq}")
                sb_, sbk = stg[tq % 2]

                def emit_ev(e, pbank=pbank, sb_=sb_):
                    return e.activation(sb_[:, :], pbank[:, :], AF.Copy)
                self.s(emit_ev, reads=(pk,), writes=sbk, tag=f"{tag}ev{ec}_{tq}")
                bk.dma("gpsimd", [(outD[ec * 128:(ec + 1) * 128, tq * 512:tq * 512 + 512],
                                   sb_[:, :])],
                       reads=sbk, writes=(f"{tag}D:{ec}:{tq}",), tag=f"{tag}st{ec}_{tq}")

    def l3_moe(self, sD):
        bk = self.bk
        A = self.arena
        sr = sD.rearrange("(tc p) d -> tc p d", p=128)
        hfm, hfmk = A.view(0, [128, 16, 512], F32)
        w1b, w1bk = A.view(32768, [128, 2, ND, 128], F32)
        hw1 = len(w1bk) // 2
        w1kk = [w1bk[:hw1], w1bk[hw1:]] if hw1 else [w1bk, w1bk]
        w2b2, _w2blk = A.view(40960, [128, 2, 1024], F32)
        w2kk2 = [("w2s0",), ("w2s1",)]
        b1c, b1ck = A.view(49152, [128, 8, 16], F32)
        b2r, b2rk = A.view(49664, [128, 8, D], F32)
        acc, acck = A.view(82432, [128, 4, D], F32)
        gst6, gst6k = A.view(98816, [128, 512], F32)
        sst6, sst6k = A.view(100864, [128, D], F32)
        self.v(lambda e: e.memset(self.sm[:, 62:63], 0.0),
               reads=_w2blk, writes=("w2s0", "w2s1", "sm62"), tag="l3m_fence")
        bk.dma("sync", [(b1c[:, :, :], self.din["w3b1C"])], writes=b1ck, tag="l3m_b1")
        for e_ in range(8):
            bk.dma("gpsimd", [(b2r[:, e_, :],
                               self.din["w3b2"][e_:e_ + 1, :].partition_broadcast(128).squeeze(1))],
                   writes=b2rk, tag=f"l3m_b2{e_}")
        w1src = self.din["w3W1"]
        w2src = self.din["w3W2"]
        for tq in range(4):
            self.v(lambda e: e.memset(acc[:, :, :], 0.0), writes=acck, tag=f"l3m_z{tq}")
            for e_ in range(8):
                for hc in range(16):
                    eb = hc % 2
                    bk.dma("sync", [(w1b[:, eb, :, :],
                                     w1src[e_, :, hc * 128:(hc + 1) * 128]
                                     .rearrange("(c p) f -> p c f", p=128))],
                           writes=w1kk[eb], tag=f"l3m_w1{tq}_{e_}_{hc}")
                    pbank = self.pb[hc % 2]
                    pk = f"pb{hc % 2}"

                    def emit_mm(e, tq=tq, eb=eb, pbank=pbank):
                        return [e.matmul(pbank[:, :], w1b[:, eb, dc, :],
                                         self.xnFM[:, dc, 1 + tq * 512:1 + tq * 512 + 512],
                                         start=(dc == 0), stop=(dc == ND - 1))
                                for dc in range(ND)]
                    self.mm(emit_mm, reads=w1kk[eb] + self.xnFM_keys, writes=(pk,),
                            tag=f"l3m_m1{tq}_{e_}_{hc}")

                    def emit_ev(e, e_=e_, hc=hc, pbank=pbank):
                        return e.activation(hfm[:, hc, :], pbank[:, :], AF.Relu,
                                            bias=b1c[:, e_, hc:hc + 1])
                    self.s(emit_ev, reads=(pk,) + b1ck, writes=hfmk, tag=f"l3m_e1{tq}_{e_}_{hc}")
                for hc in range(16):
                    eb = hc % 2
                    bk.dma("scalar", [(w2b2[:, eb, :], w2src[e_, hc * 128:(hc + 1) * 128, :])],
                           writes=w2kk2[eb], tag=f"l3m_w2b{tq}_{e_}_{hc}")
                    for ts_ in range(4):
                        for eo in range(2):
                            pbank = self.pb[ts_ * 2 + eo]
                            pk = f"pb{ts_ * 2 + eo}"

                            def emit_mm(e, ts_=ts_, eo=eo, hc=hc, eb=eb, pbank=pbank):
                                return e.matmul(pbank[:, :],
                                                hfm[:, hc, ts_ * 128:ts_ * 128 + 128],
                                                w2b2[:, eb, eo * 512:eo * 512 + 512],
                                                start=(hc == 0), stop=(hc == 15))
                            self.mm(emit_mm, reads=w2kk2[eb] + hfmk, writes=(pk,),
                                    tag=f"l3m_m2{tq}_{e_}_{hc}_{ts_}_{eo}")
                for ts_ in range(4):
                    tc = tq * 4 + ts_
                    for eo in range(2):
                        pbank = self.pb[ts_ * 2 + eo]
                        pk = f"pb{ts_ * 2 + eo}"

                        def emit_acc(e, tc=tc, ts_=ts_, eo=eo, e_=e_, pbank=pbank):
                            e.scalar_tensor_tensor(gst6[:, :], pbank[:, :], 1.0,
                                                   b2r[:, e_, eo * 512:eo * 512 + 512],
                                                   ALU.mult, ALU.add)
                            return e.scalar_tensor_tensor(
                                acc[:, ts_, eo * 512:eo * 512 + 512], gst6[:, :],
                                self.rweights[:, tc, e_:e_ + 1],
                                acc[:, ts_, eo * 512:eo * 512 + 512], ALU.mult, ALU.add)
                        self.v(emit_acc, reads=(pk, "rweights") + b2rk + acck + gst6k,
                               writes=acck + gst6k, tag=f"l3m_ac{tq}_{e_}_{ts_}_{eo}")
            for ts_ in range(4):
                tc = tq * 4 + ts_
                self.s(lambda e, ts_=ts_: e.activation(acc[:, ts_, :], acc[:, ts_, :], AF.Silu),
                       reads=acck, writes=acck, tag=f"l3m_si{tq}_{ts_}")
                bk.dma("sync", [(sst6[:, :], sr[tc, :, :])], reads=(f"sD:{tc}",),
                       writes=sst6k, tag=f"l3m_sld{tc}")

                def emit_cb(e, tc=tc, ts_=ts_):
                    e.tensor_tensor(sst6[:, :], sst6[:, :], acc[:, ts_, :], ALU.mult)
                    return e.tensor_tensor(self.xTM[:, tc, :], self.xTM[:, tc, :],
                                           sst6[:, :], ALU.add)
                self.v(emit_cb, reads=acck + sst6k + (f"xTM:{tc}",),
                       writes=(f"xTM:{tc}",) + sst6k, tag=f"l3m_cb{tc}")

    def quant_fp4(self, wnm, R, Ccols, outD, tag):
        bk = self.bk
        A = self.arena
        PC = 1408 if Ccols % 1408 == 0 else Ccols
        while PC > 1408:
            PC //= 2
        npc = Ccols // PC
        nb = PC // 16
        base = 65664
        wv, wk = A.view(base, [128, PC], F32)
        xn_, xnk = A.view(base + 5632, [128, PC], F32)
        aa, aak = A.view(base + 11264, [128, PC], F32)
        sg, sgk = A.view(base + 16896, [128, PC], F32)
        c1, c1k = A.view(base + 22528, [128, PC], F32)
        c2, c2k = A.view(base + 28160, [128, PC], F32)
        tt_, ttk = A.view(base + 33792, [128, PC], F32)
        bm, bmk = A.view(base + 39424, [128, 1408 // 16], F32)
        rc, rck = A.view(base + 40064, [128, 1408 // 16], F32)
        stg, stgk = A.view(base + 40704, [128, 128], F32)
        wsrc = self.din[wnm].rearrange("(c p) n -> c p n", p=128)
        for rc_i in range(R // 128):
            for pc_i in range(npc):
                bk.dma("sync", [(wv[:, :], wsrc[rc_i, :, pc_i * PC:pc_i * PC + PC])],
                       writes=wk, tag=f"{tag}ld{rc_i}_{pc_i}")

                def emit_q(e, nb=nb):
                    w3 = wv[:, :].rearrange("p (b i) -> p b i", i=16)
                    e.tensor_reduce(bm[:, 0:nb].unsqueeze(-1), w3, AX, ALU.max,
                                    apply_absolute_value=True)
                    e.tensor_scalar(bm[:, 0:nb], bm[:, 0:nb], 1.0 / 6.0, 1e-12, ALU.mult, ALU.max)
                    e.reciprocal(rc[:, 0:nb], bm[:, 0:nb])
                    rc3 = rc[:, 0:nb].unsqueeze(-1).broadcast_to([128, nb, 16])
                    return e.tensor_tensor(xn_[:, :].rearrange("p (b i) -> p b i", i=16), w3,
                                           rc3, ALU.mult)
                self.v(emit_q, reads=wk, writes=xnk + bmk + rck, tag=f"{tag}n{rc_i}_{pc_i}")

                def emit_ab(e):
                    e.activation(aa[:, :], xn_[:, :], AF.Abs)
                    return e.activation(sg[:, :], xn_[:, :], AF.Sign)
                self.s(emit_ab, reads=xnk, writes=aak + sgk, tag=f"{tag}ab{rc_i}_{pc_i}")

                def emit_st(e, nb=nb):
                    e.tensor_single_scalar(c1[:, :], aa[:, :], 0.25, ALU.is_ge)
                    for mid in (0.75, 1.25, 1.75):
                        e.scalar_tensor_tensor(c1[:, :], aa[:, :], mid, c1[:, :],
                                               ALU.is_ge, ALU.add)
                    e.tensor_single_scalar(c2[:, :], aa[:, :], 2.5, ALU.is_ge)
                    e.scalar_tensor_tensor(c2[:, :], aa[:, :], 3.5, c2[:, :], ALU.is_ge, ALU.add)
                    e.tensor_scalar(tt_[:, :], aa[:, :], 5.0, 2.0, ALU.is_ge, ALU.mult)
                    e.tensor_tensor(c2[:, :], c2[:, :], tt_[:, :], ALU.add)
                    e.scalar_tensor_tensor(c1[:, :], c1[:, :], 0.5, c2[:, :], ALU.mult, ALU.add)
                    e.tensor_tensor(c1[:, :], c1[:, :], sg[:, :], ALU.mult)
                    bm3 = bm[:, 0:nb].unsqueeze(-1).broadcast_to([128, nb, 16])
                    return e.tensor_tensor(c1[:, :].rearrange("p (b i) -> p b i", i=16),
                                           c1[:, :].rearrange("p (b i) -> p b i", i=16),
                                           bm3, ALU.mult)
                self.v(emit_st, reads=aak + sgk + bmk, writes=c1k + c2k + ttk,
                       tag=f"{tag}st{rc_i}_{pc_i}")
                for j in range(PC // 128):
                    pbank = self.pb[6 + j % 2]
                    pk = f"pb{6 + j % 2}"

                    def emit_tr(e, j=j, pbank=pbank):
                        return e.transpose(pbank[:, 0:128], c1[:, j * 128:(j + 1) * 128],
                                           self.ident[:, :])
                    self.mm(emit_tr, reads=c1k + ("consts",), writes=(pk,),
                            tag=f"{tag}tr{rc_i}_{pc_i}_{j}")

                    def emit_ev(e, pbank=pbank):
                        return e.activation(stg[:, :], pbank[:, 0:128], AF.Copy)
                    self.s(emit_ev, reads=(pk,), writes=stgk, tag=f"{tag}te{rc_i}_{pc_i}_{j}")
                    bk.dma("gpsimd",
                           [(outD[pc_i * PC + j * 128: pc_i * PC + (j + 1) * 128,
                                  rc_i * 128:(rc_i + 1) * 128], stg[:, :])],
                           reads=stgk, writes=(f"{tag}D:{pc_i * (PC // 128) + j}:{rc_i}",),
                           tag=f"{tag}sto{rc_i}_{pc_i}_{j}")

    def layer2(self):
        bk = self.bk
        A = self.arena
        xsD = self.dram_scratch("xsD", [2048, T])
        gateD = self.dram_scratch("gateD", [2048, T])
        xcD = self.dram_scratch("xcD", [2048, T])
        mD = self.dram_scratch("mD", [2048, T])
        y2D = self.dram_scratch("y2D", [2048, T])
        browsD = self.dram_scratch("browsD", [16, T])
        crowsD = self.dram_scratch("crowsD", [16, T])
        rsrowD = self.dram_scratch("rsrowD", [2, T])
        wgqD = self.dram_scratch("wgqD", [D, 2816])
        wuqD = self.dram_scratch("wuqD", [D, 2816])
        wdqD = self.dram_scratch("wdqD", [2816, D])
        qD = self.dscr["qD"]; kD = self.dscr["kD"]; vD = self.dscr["vD"]
        sD = self.dscr["sD"]; hD = self.dscr["hD"]
        sr = sD.rearrange("(tc p) d -> tc p d", p=128)

        self.quant_fp4("w2gate", 2816, D, wgqD, "l2wg")
        self.quant_fp4("w2up", 2816, D, wuqD, "l2wu")
        self.quant_fp4("w2down", D, 2816, wdqD, "l2wd")

        self.ln_stats("l2ln")
        self.build_xnFM("l2fm")

        wbuf, wkeys = A.view(65664, [128, 2, ND, 128], F32)
        h1 = len(wkeys) // 2
        wbk = [wkeys[:h1], wkeys[h1:]] if h1 else [wkeys, wkeys]
        stg2 = [A.view(73856, [128, 512], F32), A.view(75904, [128, 512], F32)]
        w2in = self.din["W2inT"]

        def fm_region(e0, nchunks, dstD, dst_row0, tag):
            for ci in range(nchunks):
                ec = e0 + ci * 128
                eb = ci % 2
                bk.dma("sync", [(wbuf[:, eb, :, :],
                                 w2in[:, ec:ec + 128].rearrange("(c p) e -> p c e", p=128))],
                       writes=wbk[eb], tag=f"{tag}w{ci}")
                for tq in range(4):
                    pbank = self.pb[tq % 2]
                    pk = f"pb{tq % 2}"

                    def emit_mm(e, tq=tq, eb=eb, pbank=pbank):
                        return [e.matmul(pbank[:, :], wbuf[:, eb, dc, :],
                                         self.xnFM[:, dc, 1 + tq * 512:1 + tq * 512 + 512],
                                         start=(dc == 0), stop=(dc == ND - 1))
                                for dc in range(ND)]
                    self.mm(emit_mm, reads=wbk[eb] + self.xnFM_keys, writes=(pk,),
                            tag=f"{tag}mm{ci}_{tq}")
                    sb_, sbk = stg2[tq % 2]

                    def emit_ev(e, pbank=pbank, sb_=sb_):
                        return e.activation(sb_[:, :], pbank[:, :], AF.Copy)
                    self.s(emit_ev, reads=(pk,), writes=sbk, tag=f"{tag}ev{ci}_{tq}")
                    bk.dma("gpsimd",
                           [(dstD[dst_row0 + ci * 128: dst_row0 + (ci + 1) * 128,
                                  tq * 512:tq * 512 + 512], sb_[:, :])],
                           reads=sbk, writes=(f"{tag}D:{ci}:{tq}",), tag=f"{tag}st{ci}_{tq}")

        fm_region(0, 8, qD, 0, "l2q")
        fm_region(1024, 2, kD, 0, "l2k")
        fm_region(1792, 16, xsD, 0, "l2xs")
        fm_region(3840, 16, gateD, 0, "l2gt")
        wbufv, wvk = A.view(77952, [128, ND, 512], F32)
        bk.dma("sync", [(wbufv[:, :, :],
                         w2in[:, 1280:1792].rearrange("(c p) e -> p c e", p=128))],
               writes=wvk, tag="l2v_w")
        for tc in range(NT):
            pbank = self.pb[2 + tc % 2]
            pk = f"pb{2 + tc % 2}"

            def emit_mm(e, tc=tc, pbank=pbank):
                return [e.matmul(pbank[:, :], self.xnFM[:, dc, 1 + tc * 128:1 + tc * 128 + 128],
                                 wbufv[:, dc, :], start=(dc == 0), stop=(dc == ND - 1))
                        for dc in range(ND)]
            self.mm(emit_mm, reads=wvk + self.xnFM_keys, writes=(pk,), tag=f"l2v_mm{tc}")
            sb_, sbk = stg2[tc % 2]

            def emit_ev(e, pbank=pbank, sb_=sb_):
                return e.activation(sb_[:, :], pbank[:, :], AF.Copy)
            self.s(emit_ev, reads=(pk,), writes=sbk, tag=f"l2v_ev{tc}")
            bk.dma("gpsimd", [(vD[tc * 128:(tc + 1) * 128, 0:512], sb_[:, :])],
                   reads=sbk, writes=(f"l2vD:{tc}",), tag=f"l2v_st{tc}")

        ssp, sspk = A.view(0, [128, NT, 96], F32)
        xs_sl, xsk = A.view(8192, [128, 3 + T], F32)
        xc_sl, xck = A.view(16416, [128, T], F32)
        acc_sl, acck = A.view(24608, [128, T], F32)
        cw, cwk = A.view(32800, [128, 16, 4], F32)
        cb, cbk = A.view(33056, [128, 16], F32)
        wxsl, wxk = A.view(33120, [128, 96], F32)
        bk.dma("sync", [(cw[:, :, :], self.din["convw"]), (cb[:, :], self.din["convb"])],
               writes=cwk + cbk, tag="l2c_w")
        self.v(lambda e: e.memset(ssp[:, :, :], 0.0), writes=sspk, tag="l2c_z")
        self.v(lambda e: e.memset(xs_sl[:, 0:3], 0.0), writes=xsk, tag="l2c_zp")
        for sl in range(16):
            bk.dma("sync", [(xs_sl[:, 3:3 + T], xsD[sl * 128:(sl + 1) * 128, :]),
                            (wxsl[:, :], self.din["W2xT"][sl * 128:(sl + 1) * 128, :])],
                   reads=tuple(f"l2xsD:{sl}:{tq}" for tq in range(4)),
                   writes=xsk + wxk, tag=f"l2c_ld{sl}")

            def emit_conv(e, sl=sl):
                e.tensor_scalar(acc_sl[:, :], xs_sl[:, 0:T], cw[:, sl, 0:1], None, ALU.mult)
                for j in range(1, 4):
                    e.scalar_tensor_tensor(acc_sl[:, :], xs_sl[:, j:j + T], cw[:, sl, j:j + 1],
                                           acc_sl[:, :], ALU.mult, ALU.add)
                return acc_sl and e.tensor_copy(acc_sl[:, 0:1], acc_sl[:, 0:1])
            self.v(emit_conv, reads=xsk + cwk, writes=acck, tag=f"l2c_cv{sl}")

            def emit_silu(e, sl=sl):
                return e.activation(xc_sl[:, :], acc_sl[:, :], AF.Silu, bias=cb[:, sl:sl + 1])
            self.s(emit_silu, reads=acck + cbk, writes=xck, tag=f"l2c_si{sl}")
            bk.dma("gpsimd", [(xcD[sl * 128:(sl + 1) * 128, :], xc_sl[:, :])],
                   reads=xck, writes=(f"xcD:{sl}",), tag=f"l2c_st{sl}")
            for tc in range(NT):
                pbank = self.pb[4 + tc % 2]
                pk = f"pb{4 + tc % 2}"

                def emit_mm(e, tc=tc, pbank=pbank):
                    return e.matmul(pbank[:, 0:96], xc_sl[:, tc * 128:tc * 128 + 128],
                                    wxsl[:, :], start=True, stop=True)
                self.mm(emit_mm, reads=xck + wxk, writes=(pk,), tag=f"l2c_pm{sl}_{tc}")

                def emit_add(e, tc=tc, pbank=pbank):
                    return e.tensor_tensor(ssp[:, tc, :], ssp[:, tc, :], pbank[:, 0:96], ALU.add)
                self.v(emit_add, reads=(pk,) + sspk, writes=sspk, tag=f"l2c_pa{sl}_{tc}")

        dtlw, dtlwk = A.view(39072, [128, 64], F32)
        blw, blwk = A.view(39328, [128, 16], F32)
        clw, clwk = A.view(39392, [128, 16], F32)
        bk.dma("gpsimd",
               [(dtlw[:, :], self.din["dtlnw"].partition_broadcast(128).squeeze(1)),
                (blw[:, :], self.din["Blnw"].partition_broadcast(128).squeeze(1)),
                (clw[:, :], self.din["Clnw"].partition_broadcast(128).squeeze(1))],
               writes=dtlwk + blwk + clwk, tag="l2p3_lw")
        dtn, dtnk = A.view(40096, [128, NT, 64], F32)
        btok, btokk = A.view(44192, [128, NT, 16], F32)
        ctok, ctokk = A.view(45216, [128, NT, 16], F32)
        EPS2 = 1e-6
        for tc in range(NT):
            for (nm, off, w_, rep, repk, dst, dstk) in (
                    ("dt", 0, 64, dtlw, dtlwk, dtn, dtnk),
                    ("B", 64, 16, blw, blwk, btok, btokk),
                    ("C", 80, 16, clw, clwk, ctok, ctokk)):
                def emit_s2(e, off=off, w_=w_, tc=tc):
                    return e.activation(self.sqdump[:, 0:w_], ssp[:, tc, off:off + w_],
                                        AF.Square, accum_out=self.sm[:, 48:49])
                self.s(emit_s2, reads=sspk, writes=("sm48", "sqdump"), tag=f"l2p3_{nm}s{tc}")

                def emit_rs(e, w_=w_):
                    e.tensor_scalar(self.sm[:, 48:49], self.sm[:, 48:49], 1.0 / w_, EPS2,
                                    ALU.mult, ALU.add)
                    return e.reciprocal(self.sm[:, 48:49], self.sm[:, 48:49])
                self.v(emit_rs, reads=("sm48",), writes=("sm48",), tag=f"l2p3_{nm}r{tc}")
                self.s(lambda e: e.activation(self.sm[:, 48:49], self.sm[:, 48:49], AF.Sqrt),
                       reads=("sm48",), writes=("sm48",), tag=f"l2p3_{nm}q{tc}")

                def emit_ap(e, off=off, w_=w_, tc=tc, rep=rep, dst=dst):
                    e.tensor_scalar(self.sqdump[:, 0:w_], ssp[:, tc, off:off + w_],
                                    self.sm[:, 48:49], None, ALU.mult)
                    return e.tensor_tensor(dst[:, tc, :], self.sqdump[:, 0:w_], rep[:, 0:w_],
                                           ALU.mult)
                self.v(emit_ap, reads=sspk + ("sm48",) + repk, writes=dstk + ("sqdump",),
                       tag=f"l2p3_{nm}a{tc}")
        dtfm, dtfmk = A.view(46240, [128, T], F32)
        bct, bctk = A.view(54432, [128, 2, T], F32)
        for tc in range(NT):
            def emit_tr(e, tc=tc):
                return e.transpose(self.pb[6][0:64, 0:128], dtn[:, tc, :], self.ident[:, :])
            self.mm(emit_tr, reads=dtnk + ("consts",), writes=("pb6",), tag=f"l2p3_t{tc}")

            def emit_te(e, tc=tc):
                return e.activation(dtfm[0:64, tc * 128:tc * 128 + 128],
                                    self.pb[6][0:64, 0:128], AF.Copy)
            self.s(emit_te, reads=("pb6",), writes=dtfmk, tag=f"l2p3_te{tc}")

            def emit_tr2(e, tc=tc):
                e.transpose(self.pb[7][0:16, 0:128], btok[:, tc, :], self.ident[:, :])
                return e.transpose(self.pb[7][0:16, 128:256], ctok[:, tc, :], self.ident[:, :])
            self.mm(emit_tr2, reads=btokk + ctokk + ("consts",), writes=("pb7",),
                    tag=f"l2p3_t2{tc}")

            def emit_te2(e, tc=tc):
                e.activation(bct[0:16, 0, tc * 128:tc * 128 + 128], self.pb[7][0:16, 0:128],
                             AF.Copy)
                return e.activation(bct[0:16, 1, tc * 128:tc * 128 + 128],
                                    self.pb[7][0:16, 128:256], AF.Copy)
            self.s(emit_te2, reads=("pb7",), writes=bctk, tag=f"l2p3_te2{tc}")
        self.v(lambda e: e.tensor_scalar(bct[0:16, 0, :], bct[0:16, 0, :], -1.0, None, ALU.mult),
               reads=bctk, writes=bctk, tag="l2p3_negb")
        bk.dma("gpsimd", [(browsD, bct[0:16, 0, :]), (crowsD, bct[0:16, 1, :])],
               reads=bctk, writes=("brows", "crows"), tag="l2p3_spl")

        wdt, wdtk = A.view(70816, [128, T], F32)
        negb, negbk = A.view(79008, [128, 16], F32)
        msl, mslk = A.view(79072, [128, 512], F32)
        bk.dma("sync", [(wdt[0:64, :], self.din["W2dtT"]), (negb[:, :], self.din["bdt"])],
               writes=wdtk + negbk, tag="l2p4_w")
        self.v(lambda e: e.tensor_scalar(negb[:, :], negb[:, :], -1.0, None, ALU.mult),
               reads=negbk, writes=negbk, tag="l2p4_neg")
        for sl in range(16):
            for tq in range(4):
                pbank = self.pb[tq % 2]
                pk = f"pb{tq % 2}"

                def emit_mm(e, sl=sl, tq=tq, pbank=pbank):
                    return e.matmul(pbank[:, :], wdt[0:64, sl * 128:sl * 128 + 128],
                                    dtfm[0:64, tq * 512:tq * 512 + 512], start=True, stop=True)
                self.mm(emit_mm, reads=wdtk + dtfmk, writes=(pk,), tag=f"l2p4_mm{sl}_{tq}")

                def emit_sig(e, sl=sl, pbank=pbank):
                    e.activation(msl[:, :], pbank[:, :], AF.Sigmoid, bias=negb[:, sl:sl + 1],
                                 scale=-1.0)
                    return e.activation(msl[:, :], msl[:, :], AF.Ln)
                self.s(emit_sig, reads=(pk,) + negbk, writes=mslk, tag=f"l2p4_sg{sl}_{tq}")
                bk.dma("gpsimd", [(mD[sl * 128:(sl + 1) * 128, tq * 512:tq * 512 + 512],
                                   msl[:, :])],
                       reads=mslk, writes=(f"mD:{sl}:{tq}",), tag=f"l2p4_st{sl}_{tq}")

        anse, ansek = A.view(0, [128, 16, 16], F32)
        alog, alogk = A.view(1024, [128, 16, 16], F32)
        dpar, dpark = A.view(2048, [128, 16], F32)
        gl2w, gl2wk = A.view(2112, [128, 16], F32)
        bk.dma("sync", [(alog[:, :, :], self.din["AlogC"]), (dpar[:, :], self.din["DparC"]),
                        (gl2w[:, :], self.din["ln2wC"])],
               writes=alogk + dpark + gl2wk, tag="l2p5_w")
        self.s(lambda e: e.activation(anse[:, :, :], alog[:, :, :], AF.Exp),
               reads=alogk, writes=ansek, tag="l2p5_exp")
        m_s, m_sk = A.view(8192, [128, T], F32)
        u_s, u_sk = A.view(16384, [128, T], F32)
        xc2, xc2k = A.view(24576, [128, T], F32)
        Gt, Gtk = A.view(32768, [128, T], F32)
        bt_, btk_ = A.view(40960, [128, T], F32)
        yp, ypk = A.view(49152, [128, T], F32)
        brep = [A.view(57344, [128, T], F32), A.view(65536, [128, T], F32)]
        crep = [A.view(73728, [128, T], F32), A.view(81920, [128, T], F32)]
        at_, atk_ = A.view(90112, [128, T], F32)
        gsl, gslk = A.view(98304, [128, T], F32)
        sq2, sq2k = A.view(106496, [128, 512], F32)
        sums2, sums2k = A.view(108544, [1, T], F32)
        self.v(lambda e: e.memset(sums2[:, :], 0.0), writes=sums2k, tag="l2p5_zs")
        for sl in range(16):
            bk.dma("sync", [(m_s[:, :], mD[sl * 128:(sl + 1) * 128, :]),
                            (xc2[:, :], xcD[sl * 128:(sl + 1) * 128, :]),
                            (gsl[:, :], gateD[sl * 128:(sl + 1) * 128, :])],
                   reads=tuple(f"mD:{sl}:{tq}" for tq in range(4)) + (f"xcD:{sl}",) +
                         tuple(f"l2gtD:{sl}:{tq}" for tq in range(4)),
                   writes=m_sk + xc2k + gslk, tag=f"l2p5_ld{sl}")

            def emit_u(e):
                return e.tensor_tensor(u_s[:, :], m_s[:, :], xc2[:, :], ALU.mult)
            self.v(emit_u, reads=m_sk + xc2k, writes=u_sk, tag=f"l2p5_u{sl}")
            self.v(lambda e: e.memset(yp[:, :], 0.0), writes=ypk, tag=f"l2p5_zy{sl}")
            for n in range(16):
                nb_ = n % 2
                brp, brpk = brep[nb_]
                crp, crpk = crep[nb_]
                bk.dma("sync", [(brp[:, :], browsD[n:n + 1, :].partition_broadcast(128).squeeze(1)),
                                (crp[:, :], crowsD[n:n + 1, :].partition_broadcast(128).squeeze(1))],
                       reads=("brows", "crows"), writes=brpk + crpk, tag=f"l2p5_br{sl}_{n}")

                def emit_a(e, sl=sl, n=n):
                    return e.activation(at_[:, :], m_s[:, :], AF.Exp,
                                        scale=anse[:, sl, n:n + 1])
                self.s(emit_a, reads=m_sk + ansek, writes=atk_, tag=f"l2p5_a{sl}_{n}")

                def emit_b(e, brp=brp):
                    return e.tensor_tensor(bt_[:, :], u_s[:, :], brp[:, :], ALU.mult)
                self.v(emit_b, reads=u_sk + brpk, writes=btk_, tag=f"l2p5_b{sl}_{n}")

                def emit_scan(e):
                    return e.tensor_tensor_scan(Gt[:, :], at_[:, :], bt_[:, :], 0.0,
                                                ALU.mult, ALU.add)
                self.v(emit_scan, reads=atk_ + btk_, writes=Gtk, tag=f"l2p5_s{sl}_{n}")

                def emit_ro(e, crp=crp):
                    return e.tensor_tensor(Gt[:, :], Gt[:, :], crp[:, :], ALU.mult)
                self.v(emit_ro, reads=Gtk + crpk, writes=Gtk, tag=f"l2p5_c{sl}_{n}")

                def emit_acc(e):
                    return e.tensor_tensor(yp[:, :], yp[:, :], Gt[:, :], ALU.add)
                self.bk.t("gpsimd", emit_acc, reads=Gtk + ypk, writes=ypk,
                          tag=f"l2p5_y{sl}_{n}")
            def emit_y2(e, sl=sl):
                return e.scalar_tensor_tensor(yp[:, :], xc2[:, :], dpar[:, sl:sl + 1],
                                              yp[:, :], ALU.mult, ALU.add)
            self.v(emit_y2, reads=xc2k + dpark + ypk, writes=ypk, tag=f"l2p5_y2{sl}")
            self.s(lambda e: e.activation(gsl[:, :], gsl[:, :], AF.Silu),
                   reads=gslk, writes=gslk, tag=f"l2p5_sg{sl}")

            def emit_y3(e):
                return e.tensor_tensor(yp[:, :], yp[:, :], gsl[:, :], ALU.mult)
            self.v(emit_y3, reads=ypk + gslk, writes=ypk, tag=f"l2p5_y3{sl}")
            bk.dma("gpsimd", [(y2D[sl * 128:(sl + 1) * 128, :], yp[:, :])],
                   reads=ypk, writes=(f"y2D:{sl}",), tag=f"l2p5_st{sl}")
            for tq in range(4):
                def emit_sq(e, tq=tq):
                    return e.activation(sq2[:, :], yp[:, tq * 512:tq * 512 + 512], AF.Square)
                self.s(emit_sq, reads=ypk, writes=sq2k, tag=f"l2p5_q{sl}_{tq}")

                def emit_sm(e, pb_=self.pb[2 + tq % 2]):
                    return e.matmul(pb_[0:1, :], self.onescol[:, :], sq2[:, :],
                                    start=True, stop=True)
                self.mm(emit_sm, reads=sq2k + ("consts2",), writes=(f"pb{2 + tq % 2}",),
                        tag=f"l2p5_m{sl}_{tq}")

                def emit_ad(e, tq=tq, pb_=self.pb[2 + tq % 2]):
                    return e.tensor_tensor(sums2[0:1, tq * 512:tq * 512 + 512],
                                           sums2[0:1, tq * 512:tq * 512 + 512],
                                           pb_[0:1, :], ALU.add)
                self.v(emit_ad, reads=(f"pb{2 + tq % 2}",) + sums2k, writes=sums2k,
                       tag=f"l2p5_sa{sl}_{tq}")
        def emit_rs2(e):
            e.tensor_scalar(sums2[:, :], sums2[:, :], 1.0 / 2048.0, EPS2, ALU.mult, ALU.add)
            return e.reciprocal(sums2[:, :], sums2[:, :])
        self.v(emit_rs2, reads=sums2k, writes=sums2k, tag="l2p5_rs2a")
        self.s(lambda e: e.activation(sums2[:, :], sums2[:, :], AF.Sqrt),
               reads=sums2k, writes=sums2k, tag="l2p5_rs2b")
        bk.dma("gpsimd", [(rsrowD[1:2, :], sums2[0:1, :])], reads=sums2k,
               writes=("rsrow2",), tag="l2p5_rsp")

        attn, attnk = A.view(0, [128, 16, 512], F32)
        vpad2, vpad2k = A.view(32768, [128, 16, 128], F32)
        kh2, kh2k = A.view(40960, [128, T], F32)
        qh2, qh2k = A.view(49152, [128, 512], F32)
        prb = [A.view(51200, [128, 512], F32), A.view(53248, [128, 512], F32)]
        prbT, prbTk = A.view(55296, [128, 16, 512], F32)
        cmsk, cmskk = A.view(88064, [128, 4, 512], F32)
        rrow2, rrow2k = A.view(96256, [128, 640], F32)
        rb2, rb2k = A.view(98304, [128, 512], F32)
        sq1, sq1k = A.view(100352, [128, 512], F32)
        sums1, sums1k = A.view(102400, [1, T], F32)
        scol2 = self.sm[:, 16:32]
        r4 = self.sm[:, 48:52]
        bk.dma("sync", [(cmsk[:, :, :], self.din["cmask"].rearrange("s p k -> p s k"))],
               writes=cmskk, tag="l2a_cm")
        self.v(lambda e: e.memset(sums1[:, :], 0.0), writes=sums1k, tag="l2a_zs")
        vsrc2 = vD.rearrange("(kc p) e -> p kc e", p=128)
        qrd = tuple(f"l2qD:{c}:{t_}" for c in range(8) for t_ in range(4))
        krd = tuple(f"l2kD:{c}:{t_}" for c in range(2) for t_ in range(4))
        vrd = tuple(f"l2vD:{tc}" for tc in range(NT))
        for qc in range(4):
            for h in range(16):
                g = h // 4
                bk.dma("sync", [(kh2[0:64, :], kD[g * 64:(g + 1) * 64, :]),
                                (qh2[0:64, :], qD[h * 64:(h + 1) * 64, qc * 512:qc * 512 + 512]),
                                (vpad2[:, :, :], vsrc2[:, :, g * 128:(g + 1) * 128])],
                       reads=qrd + krd + vrd, writes=kh2k + qh2k + vpad2k,
                       tag=f"l2a_ld{qc}_{h}")
                for i in range(4):
                    tq = qc * 4 + i
                    nks = qc + 1
                    for ks in range(nks):
                        pbank = self.pb[ks % 2]
                        pk = f"pb{ks % 2}"

                        def emit_s(e, i=i, ks=ks, pbank=pbank):
                            return e.matmul(pbank[:, :], qh2[0:64, i * 128:(i + 1) * 128],
                                            kh2[0:64, ks * 512:ks * 512 + 512],
                                            start=True, stop=True)
                        self.mm(emit_s, reads=qh2k + kh2k, writes=(pk,),
                                tag=f"l2a_s{qc}_{h}_{i}_{ks}")
                        pr, prk = prb[ks % 2]
                        if ks < qc:
                            def emit_e(e, ks=ks, pbank=pbank, pr=pr):
                                return e.activation(pr[:, :], pbank[:, :], AF.Exp, scale=0.125,
                                                    accum_out=scol2[:, ks:ks + 1])
                            self.s(emit_e, reads=(pk,), writes=prk + (f"sm{16 + ks}",),
                                   tag=f"l2a_e{qc}_{h}_{i}_{ks}")
                        else:
                            def emit_e(e, pbank=pbank, pr=pr):
                                return e.activation(pr[:, :], pbank[:, :], AF.Exp, scale=0.125)
                            self.s(emit_e, reads=(pk,), writes=prk, tag=f"l2a_e{qc}_{h}_{i}_{ks}")

                            def emit_msk(e, i=i, ks=ks, pr=pr):
                                return e.scalar_tensor_tensor(pr[:, :], pr[:, :], 1.0,
                                                              cmsk[:, i, :], ALU.mult, ALU.mult,
                                                              accum_out=scol2[:, ks:ks + 1])
                            self.v(emit_msk, reads=prk + cmskk, writes=prk + (f"sm{16 + ks}",),
                                   tag=f"l2a_m{qc}_{h}_{i}_{ks}")
                        for j in range(4):
                            pt = self.pb[2 + j % 2]
                            ptk = f"pb{2 + j % 2}"

                            def emit_tr(e, j=j, pt=pt, pr=pr):
                                return e.transpose(pt[:, 0:128], pr[:, j * 128:(j + 1) * 128],
                                                   self.ident[:, :])
                            self.mm(emit_tr, reads=prk + ("consts",), writes=(ptk,),
                                    tag=f"l2a_t{qc}_{h}_{i}_{ks}_{j}")

                            def emit_te(e, i=i, ks=ks, j=j, pt=pt):
                                return e.activation(prbT[:, ks * 4 + j, i * 128:(i + 1) * 128],
                                                    pt[:, 0:128], AF.Copy)
                            self.s(emit_te, reads=(ptk,), writes=prbTk,
                                   tag=f"l2a_tev{qc}_{h}_{i}_{ks}_{j}")
                    def emit_rs(e, i=i, nks=nks):
                        e.tensor_reduce(self.sm[:, 52:53], scol2[:, 0:nks], AX, ALU.add)
                        return e.reciprocal(r4[:, i:i + 1], self.sm[:, 52:53])
                    self.v(emit_rs, reads=tuple(f"sm{16 + k_}" for k_ in range(nks)),
                           writes=("sm52", "smr4"), tag=f"l2a_rs{qc}_{h}_{i}")
                def emit_t4(e):
                    return e.transpose(self.pb[6][0:4, 0:128], r4[:, :], self.ident[:, :])
                self.mm(emit_t4, reads=("smr4", "consts"), writes=("pb6",), tag=f"l2a_t4{qc}_{h}")

                def emit_t4e(e):
                    return e.activation(rrow2[0:4, 512:640], self.pb[6][0:4, 0:128], AF.Copy)
                self.s(emit_t4e, reads=("pb6",), writes=rrow2k, tag=f"l2a_t4e{qc}_{h}")
                bk.dma("gpsimd", [(rrow2[0:1, 0:512].rearrange("p (a b) -> p a b", a=4),
                                   rrow2[0:4, 512:640].unsqueeze(1))],
                       reads=rrow2k, writes=rrow2k, tag=f"l2a_mv{qc}_{h}")

                def emit_bc(e):
                    return e.matmul(self.pb[5][:, :], self.ones1[:, :], rrow2[0:1, 0:512],
                                    start=True, stop=True)
                self.mm(emit_bc, reads=rrow2k + ("consts",), writes=("pb5",),
                        tag=f"l2a_bc{qc}_{h}")

                def emit_rbe(e):
                    return e.activation(rb2[:, :], self.pb[5][:, :], AF.Copy)
                self.s(emit_rbe, reads=("pb5",), writes=rb2k, tag=f"l2a_rbe{qc}_{h}")
                for kc in range(4 * (qc + 1)):
                    def emit_av(e, kc=kc):
                        return e.matmul(self.pb[4][:, :], vpad2[:, kc, :], prbT[:, kc, :],
                                        start=(kc == 0), stop=(kc == 4 * (qc + 1) - 1))
                    self.mm(emit_av, reads=prbTk + vpad2k, writes=("pb4",),
                            tag=f"l2a_av{qc}_{h}_{kc}")

                def emit_at(e, h=h):
                    return e.tensor_tensor(attn[:, h, :], self.pb[4][:, :], rb2[:, :], ALU.mult)
                self.v(emit_at, reads=("pb4",) + rb2k, writes=attnk, tag=f"l2a_at{qc}_{h}")

                def emit_sq(e, h=h):
                    return e.activation(sq1[:, :], attn[:, h, :], AF.Square)
                self.s(emit_sq, reads=attnk, writes=sq1k, tag=f"l2a_q{qc}_{h}")

                def emit_sm1(e):
                    return e.matmul(self.pb[7][0:1, :], self.onescol[:, :], sq1[:, :],
                                    start=True, stop=True)
                self.mm(emit_sm1, reads=sq1k + ("consts2",), writes=("pb7",),
                        tag=f"l2a_sm{qc}_{h}")

                def emit_sa(e, qc=qc):
                    return e.tensor_tensor(sums1[0:1, qc * 512:qc * 512 + 512],
                                           sums1[0:1, qc * 512:qc * 512 + 512],
                                           self.pb[7][0:1, :], ALU.add)
                self.v(emit_sa, reads=("pb7",) + sums1k, writes=sums1k, tag=f"l2a_sa{qc}_{h}")

            if qc == 0:
                bk.tap("l2_attn0", attn[:, :, :], reads=attnk)
            if qc == 1:
                bk.tap("l2_attn1", attn[:, :, :], reads=attnk)
            def emit_rs1(e, qc=qc):
                e.tensor_scalar(sums1[0:1, qc * 512:qc * 512 + 512],
                                sums1[0:1, qc * 512:qc * 512 + 512], 1.0 / 2048.0, EPS2,
                                ALU.mult, ALU.add)
                return e.reciprocal(sums1[0:1, qc * 512:qc * 512 + 512],
                                    sums1[0:1, qc * 512:qc * 512 + 512])
            self.v(emit_rs1, reads=sums1k, writes=sums1k, tag=f"l2b_rs{qc}")
            self.s(lambda e, qc=qc: e.activation(sums1[0:1, qc * 512:qc * 512 + 512],
                                                 sums1[0:1, qc * 512:qc * 512 + 512], AF.Sqrt),
                   reads=sums1k, writes=sums1k, tag=f"l2b_rq{qc}")
            bk.dma("gpsimd", [(rsrowD[0:1, qc * 512:qc * 512 + 512],
                               sums1[0:1, qc * 512:qc * 512 + 512])],
                   reads=sums1k, writes=("rsrow1",), tag=f"l2b_sp{qc}")
            rs1b, rs1bk = A.view(65536, [128, 512], F32)
            rs2b, rs2bk = A.view(67584, [128, 512], F32)
            y2q, y2qk = A.view(32768, [128, 16, 512], F32)
            bk.dma("sync", [(rs1b[:, :], rsrowD[0:1, qc * 512:qc * 512 + 512]
                             .partition_broadcast(128).squeeze(1)),
                            (rs2b[:, :], rsrowD[1:2, qc * 512:qc * 512 + 512]
                             .partition_broadcast(128).squeeze(1)),
                            (y2q[:, :, :], y2D.rearrange("(sl p) t -> p sl t", p=128)
                             [:, :, qc * 512:qc * 512 + 512])],
                   reads=("rsrow1", "rsrow2") + tuple(f"y2D:{sl}" for sl in range(16)),
                   writes=rs1bk + rs2bk + y2qk, tag=f"l2b_ld{qc}")
            l1w, l1wk = A.view(69632, [128, 16], F32)
            l2w, l2wk = A.view(69696, [128, 16], F32)
            bk.dma("sync", [(l1w[:, :], self.din["ln1wC"]),
                            (l2w[:, :], self.din["ln2wC"])],
                   writes=l1wk + l2wk, tag=f"l2b_l1w{qc}")

            def emit_half(e):
                e.tensor_scalar(l1w[:, :], l1w[:, :], 0.5, None, ALU.mult)
                return e.tensor_scalar(l2w[:, :], l2w[:, :], 0.5, None, ALU.mult)
            self.v(emit_half, reads=l1wk + l2wk, writes=l1wk + l2wk, tag=f"l2b_hf{qc}")
            for c in range(16):
                def emit_cb(e, c=c):
                    e.scalar_tensor_tensor(sq1[:, :], attn[:, c, :], l1w[:, c:c + 1],
                                           rs1b[:, :], ALU.mult, ALU.mult)
                    e.scalar_tensor_tensor(attn[:, c, :], y2q[:, c, :], l2w[:, c:c + 1],
                                           rs2b[:, :], ALU.mult, ALU.mult)
                    return e.tensor_tensor(attn[:, c, :], attn[:, c, :], sq1[:, :], ALU.add)
                self.v(emit_cb, reads=attnk + y2qk + rs1bk + rs2bk + l1wk + l2wk + sq1k,
                       writes=attnk + sq1k, tag=f"l2b_cb{qc}_{c}")
            if qc == 0:
                bk.tap("l2_comb0", attn[:, :, :], reads=attnk)
            wob, wobk = A.view(70656, [128, 2, 1024], F32)
            w2osrc = self.din["W2outT"].rearrange("(c p) e -> c p e", p=128)
            for c in range(16):
                cb_ = c % 2
                bk.dma("sync", [(wob[:, cb_, :], w2osrc[c, :, :])],
                       writes=(f"wob{cb_}",), tag=f"l2b_w{qc}_{c}")
                for ts_ in range(4):
                    for eo in range(2):
                        pbank = self.pb[ts_ * 2 + eo]
                        pk = f"pb{ts_ * 2 + eo}"

                        def emit_mm(e, c=c, cb_=cb_, ts_=ts_, eo=eo, pbank=pbank):
                            return e.matmul(pbank[:, :], attn[:, c, ts_ * 128:ts_ * 128 + 128],
                                            wob[:, cb_, eo * 512:eo * 512 + 512],
                                            start=(c == 0), stop=(c == 15))
                        self.mm(emit_mm, reads=(f"wob{cb_}",) + attnk, writes=(pk,),
                                tag=f"l2b_mm{qc}_{c}_{ts_}_{eo}")
            stg3, stg3k = A.view(78848, [128, 512], F32)
            for ts_ in range(4):
                for eo in range(2):
                    pbank = self.pb[ts_ * 2 + eo]
                    pk = f"pb{ts_ * 2 + eo}"

                    def emit_ev(e, pbank=pbank):
                        return e.activation(stg3[:, :], pbank[:, :], AF.Copy)
                    self.s(emit_ev, reads=(pk,), writes=stg3k, tag=f"l2b_ev{qc}_{ts_}_{eo}")
                    tc = qc * 4 + ts_
                    bk.dma("gpsimd", [(sr[tc, :, eo * 512:eo * 512 + 512], stg3[:, :])],
                           reads=stg3k, writes=(f"sD:{tc}",), tag=f"l2b_st{qc}_{ts_}_{eo}")

        self.l2_ffn(wgqD, wuqD, wdqD, sD)

    def fp4_act_quant(self, x, xk, W, tmps, tag):
        (sgn, sgnk), (aa, aak), (c2, c2k), (tt_, ttk), (bm, bmk), (rc, rck) = tmps
        nb = W // 16

        def emit_n(e, nb=nb):
            x3 = x.rearrange("p (b i) -> p b i", i=16)
            e.tensor_reduce(bm[:, 0:nb].unsqueeze(-1), x3, AX, ALU.max,
                            apply_absolute_value=True)
            e.tensor_scalar(bm[:, 0:nb], bm[:, 0:nb], 1.0 / 6.0, 1e-12, ALU.mult, ALU.max)
            e.reciprocal(rc[:, 0:nb], bm[:, 0:nb])
            rc3 = rc[:, 0:nb].unsqueeze(-1).broadcast_to([128, nb, 16])
            return e.tensor_tensor(x3, x3, rc3, ALU.mult)
        self.v(emit_n, reads=xk, writes=xk + bmk + rck, tag=f"{tag}n")

        def emit_ab(e):
            e.activation(sgn[:, 0:W], x, AF.Sign)
            return e.activation(aa[:, 0:W], x, AF.Abs)
        self.s(emit_ab, reads=xk, writes=sgnk + aak, tag=f"{tag}ab")

        def emit_st(e, nb=nb):
            a_ = aa[:, 0:W]
            e.tensor_single_scalar(x, a_, 0.25, ALU.is_ge)
            for mid in (0.75, 1.25, 1.75):
                e.scalar_tensor_tensor(x, a_, mid, x, ALU.is_ge, ALU.add)
            e.tensor_single_scalar(c2[:, 0:W], a_, 2.5, ALU.is_ge)
            e.scalar_tensor_tensor(c2[:, 0:W], a_, 3.5, c2[:, 0:W], ALU.is_ge, ALU.add)
            e.tensor_scalar(tt_[:, 0:W], a_, 5.0, 2.0, ALU.is_ge, ALU.mult)
            e.tensor_tensor(c2[:, 0:W], c2[:, 0:W], tt_[:, 0:W], ALU.add)
            e.scalar_tensor_tensor(x, x, 0.5, c2[:, 0:W], ALU.mult, ALU.add)
            e.tensor_tensor(x, x, sgn[:, 0:W], ALU.mult)
            bm3 = bm[:, 0:nb].unsqueeze(-1).broadcast_to([128, nb, 16])
            return e.tensor_tensor(x.rearrange("p (b i) -> p b i", i=16),
                                   x.rearrange("p (b i) -> p b i", i=16), bm3, ALU.mult)
        self.v(emit_st, reads=aak + sgnk + bmk, writes=xk + c2k + ttk, tag=f"{tag}st")

    def l2_ffn(self, wgqD, wuqD, wdqD, sD):
        bk = self.bk
        A = self.arena
        sr = sD.rearrange("(tc p) d -> tc p d", p=128)
        I2 = 2816
        xq, xqk = A.view(0, [128, D], F32)
        xqfm, xqfmk = A.view(4096, [128, ND, 256], F32)
        hbuf, hbk = A.view(12288, [128, 2, I2], F32)
        sgn, sgnk = A.view(34816, [128, I2], F32)
        aa, aak = A.view(46080, [128, I2], F32)
        c2, c2k = A.view(57344, [128, I2], F32)
        tt_, ttk = A.view(68608, [128, I2], F32)
        bm, bmk = A.view(79872, [128, 176], F32)
        rc, rck = A.view(80576 - 64 if False else 80608, [128, 176], F32)
        hqfm, hqfmk = A.view(81344, [128, 22, 256], F32)
        wpc, wpck = A.view(103872, [128, ND, 512], F32)
        wdp, wdpk = A.view(103872, [128, D], F32)
        gst, gstk = A.view(112128, [128, 512], F32)
        sst, sstk = A.view(108032, [128, 1024], F32)
        tmps = ((sgn, sgnk), (aa, aak), (c2, c2k), (tt_, ttk), (bm, bmk), (rc, rck))
        gbuf, gbufk = A.view(81344, [128, 512], F32)

        ipieces = [(0, 512), (512, 512), (1024, 512), (1536, 512), (2048, 512), (2560, 256)]
        for tq8 in range(8):
            for half in range(2):
                tc = tq8 * 2 + half
                self.xn_tile(xq[:, :], xqk, tc, f"l2f_xn{tc}")
                self.fp4_act_quant(xq[:, :], xqk, D, tmps, f"l2f_xq{tc}")
                for grp in range(0, ND, 4):
                    pt = self.pb[2 + (grp // 4) % 2]
                    ptk = f"pb{2 + (grp // 4) % 2}"

                    def emit_tr(e, grp=grp, pt=pt):
                        return [e.transpose(pt[:, j * 128:(j + 1) * 128],
                                            xq[:, (grp + j) * 128:(grp + j + 1) * 128],
                                            self.ident[:, :]) for j in range(4)]
                    self.mm(emit_tr, reads=xqk + ("consts",), writes=(ptk,),
                            tag=f"l2f_tr{tc}_{grp}")

                    def emit_te(e, grp=grp, half=half, pt=pt):
                        return e.activation(xqfm[:, grp:grp + 4, half * 128:half * 128 + 128],
                                            pt[:, :].rearrange("p (j c) -> p j c", c=128),
                                            AF.Copy)
                    self.s(emit_te, reads=(ptk,), writes=xqfmk, tag=f"l2f_te{tc}_{grp}")
            for wD_, wtag, sil in ((wgqD, "g", True), (wuqD, "u", False)):
                for pi, (i0, iw) in enumerate(ipieces):
                    bk.dma("sync", [(wpc[:, :, 0:iw],
                                     wD_.rearrange("(c p) i -> p c i", p=128)[:, :, i0:i0 + iw])],
                           reads=tuple(f"l2w{wtag}D:{c}:{r}" for c in range(D // 128)
                                       for r in range(22)) if False else
                                 tuple(f"l2w{wtag}D:{cc}:{rr}" for cc in range(8) for rr in range(22)),
                           writes=wpck, tag=f"l2f_{wtag}w{tq8}_{pi}")
                    for half in range(2):
                        pbank = self.pb[half]
                        pk = f"pb{half}"

                        def emit_mm(e, half=half, iw=iw, pbank=pbank):
                            return [e.matmul(pbank[:, 0:iw],
                                             xqfm[:, dc, half * 128:half * 128 + 128],
                                             wpc[:, dc, 0:iw], start=(dc == 0),
                                             stop=(dc == ND - 1)) for dc in range(ND)]
                        self.mm(emit_mm, reads=wpck + xqfmk, writes=(pk,),
                                tag=f"l2f_{wtag}mm{tq8}_{pi}_{half}")
                        if sil:
                            def emit_ev(e, half=half, i0=i0, iw=iw, pbank=pbank):
                                return e.activation(hbuf[:, half, i0:i0 + iw],
                                                    pbank[:, 0:iw], AF.Silu)
                            self.s(emit_ev, reads=(pk,), writes=hbk,
                                   tag=f"l2f_{wtag}ev{tq8}_{pi}_{half}")
                        else:
                            def emit_ev(e, half=half, i0=i0, iw=iw, pbank=pbank):
                                return e.tensor_tensor(hbuf[:, half, i0:i0 + iw],
                                                       hbuf[:, half, i0:i0 + iw],
                                                       pbank[:, 0:iw], ALU.mult)
                            self.v(emit_ev, reads=(pk,) + hbk, writes=hbk,
                                   tag=f"l2f_{wtag}ev{tq8}_{pi}_{half}")
            for half in range(2):
                tc = tq8 * 2 + half
                self.fp4_act_quant(hbuf[:, half, :], hbk, I2, tmps, f"l2f_hq{tc}")
                for grp in range(22):
                    pt = self.pb[2 + grp % 2]
                    ptk = f"pb{2 + grp % 2}"

                    def emit_tr(e, grp=grp, half=half, pt=pt):
                        return e.transpose(pt[:, 0:128],
                                           hbuf[:, half, grp * 128:(grp + 1) * 128],
                                           self.ident[:, :])
                    self.mm(emit_tr, reads=hbk + ("consts",), writes=(ptk,),
                            tag=f"l2f_htr{tc}_{grp}")

                    def emit_te(e, grp=grp, half=half, pt=pt):
                        return e.activation(hqfm[:, grp, half * 128:half * 128 + 128],
                                            pt[:, 0:128], AF.Copy)
                    self.s(emit_te, reads=(ptk,), writes=hqfmk, tag=f"l2f_hte{tc}_{grp}")
            for ic in range(22):
                bk.dma("sync", [(wdp[:, :], wdqD[ic * 128:(ic + 1) * 128, :])],
                       reads=tuple(f"l2wdD:{cc}:{rr}" for cc in range(22) for rr in range(8)),
                       writes=wdpk, tag=f"l2f_dw{tq8}_{ic}")
                for half in range(2):
                    for eo in range(2):
                        pbank = self.pb[4 + half * 2 + eo]
                        pk = f"pb{4 + half * 2 + eo}"

                        def emit_mm(e, ic=ic, half=half, eo=eo, pbank=pbank):
                            return e.matmul(pbank[:, :], hqfm[:, ic, half * 128:half * 128 + 128],
                                            wdp[:, eo * 512:eo * 512 + 512],
                                            start=(ic == 0), stop=(ic == 21))
                        self.mm(emit_mm, reads=wdpk + hqfmk, writes=(pk,),
                                tag=f"l2f_dmm{tq8}_{ic}_{half}_{eo}")
            for half in range(2):
                tc = tq8 * 2 + half
                bk.dma("sync", [(sst[:, :], sr[tc, :, :])],
                       reads=(f"sD:{tc}",), writes=sstk, tag=f"l2f_sld{tc}")
                for eo in range(2):
                    pbank = self.pb[4 + half * 2 + eo]
                    pk = f"pb{4 + half * 2 + eo}"

                    def emit_g(e, eo=eo, pbank=pbank):
                        return e.activation(gst[:, :], pbank[:, :], AF.Silu)
                    self.s(emit_g, reads=(pk,), writes=gstk, tag=f"l2f_g{tc}_{eo}")

                    def emit_cb(e, tc=tc, eo=eo):
                        e.tensor_tensor(sst[:, eo * 512:eo * 512 + 512],
                                        sst[:, eo * 512:eo * 512 + 512], gst[:, :], ALU.mult)
                        return e.tensor_tensor(self.xTM[:, tc, eo * 512:eo * 512 + 512],
                                               self.xTM[:, tc, eo * 512:eo * 512 + 512],
                                               sst[:, eo * 512:eo * 512 + 512], ALU.add)
                    self.v(emit_cb, reads=gstk + sstk + (f"xTM:{tc}",),
                           writes=(f"xTM:{tc}",) + sstk, tag=f"l2f_cb{tc}_{eo}")

    def proj_fm_spill(self, wnm, bcol_nm, outD, tag):
        bk = self.bk
        A = self.arena
        wbuf, wkeys = A.view(65664, [128, 2, ND, 128], F32)
        h1 = len(wkeys) // 2
        wbk = [wkeys[:h1], wkeys[h1:]] if h1 else [wkeys, wkeys]
        stg = [A.view(73856, [128, 512], F32), A.view(75904, [128, 512], F32)]
        bcol = bk.sb(f"bcol_{tag}", [128, ND], F32)
        bk.dma("sync", [(bcol[:, :], self.din[bcol_nm])],
               writes=(f"bcol_{tag}",), tag=f"{tag}bc")
        wsrc = self.din[wnm]
        for ec in range(ND):
            eb = ec % 2
            srcw = wsrc[:, ec * 128:(ec + 1) * 128].rearrange("(c p) e -> p c e", p=128)
            bk.dma("sync", [(wbuf[:, eb, :, :], srcw)], writes=wbk[eb], tag=f"{tag}w{ec}")
            for tq in range(4):
                pbank = self.pb[tq % 2]
                pk = f"pb{tq % 2}"

                def emit_mm(e, tq=tq, eb=eb, pbank=pbank):
                    return [e.matmul(pbank[:, :], wbuf[:, eb, dc, :],
                                     self.xnFM[:, dc, 1 + tq * 512:1 + tq * 512 + 512],
                                     start=(dc == 0), stop=(dc == ND - 1)) for dc in range(ND)]
                self.mm(emit_mm, reads=wbk[eb] + self.xnFM_keys, writes=(pk,),
                        tag=f"{tag}mm{ec}_{tq}")
                sb_, sbk = stg[tq % 2]

                def emit_ev(e, ec=ec, pbank=pbank, sb_=sb_):
                    return e.tensor_scalar(sb_[:, :], pbank[:, :], bcol[:, ec:ec + 1], None, ALU.add)
                self.v(emit_ev, reads=(pk, f"bcol_{tag}"), writes=sbk, tag=f"{tag}ev{ec}_{tq}")
                bk.dma("gpsimd", [(outD[ec * 128:(ec + 1) * 128, tq * 512:tq * 512 + 512], sb_[:, :])],
                       reads=sbk, writes=(f"{tag}D:{ec}:{tq}",), tag=f"{tag}st{ec}_{tq}")

    def layer1(self):
        bk = self.bk
        A = self.arena
        self.ln_stats("l1ln")
        self.build_xnFM("l1fm")
        qD = self.dram_scratch("qD", [D, T])
        kD = self.dram_scratch("kD", [D, T])
        vD = self.dram_scratch("vD", [T, D])
        sD = self.dscr["sD"]
        sr = sD.rearrange("(tc p) d -> tc p d", p=128)

        self.proj_fm_spill("Wq1T", "bq1C", qD, "l1q")
        self.proj_fm_spill("Wk1T", "bk1C", kD, "l1k")
        self.proj_spill("Wv1T", D, vD, "bv1R", 110592, None, "l1v")

        s64, s64k = A.view(118784, [128, NT, 64], F32)
        win, wink = A.view(102400, [128, ND, 64], F32)
        binr, binrk = A.view(106496, [128, 64], F32)
        bk.dma("sync", [(win[:, :, :], self.din["f1Win"].rearrange("(c p) m -> p c m", p=128))],
               writes=wink, tag="l1f_win")
        bk.dma("gpsimd", [(binr[:, :], self.din["f1bin"].partition_broadcast(128).squeeze(1))],
               writes=binrk, tag="l1f_bin")
        for tc in range(NT):
            pbank = self.pb[2 + tc % 2]
            pk = f"pb{2 + tc % 2}"

            def emit_mm(e, tc=tc, pbank=pbank):
                return [e.matmul(pbank[:, 0:64], self.xnFM[:, dc, 1 + tc * 128:1 + tc * 128 + 128],
                                 win[:, dc, :], start=(dc == 0), stop=(dc == ND - 1))
                        for dc in range(ND)]
            self.mm(emit_mm, reads=wink + self.xnFM_keys, writes=(pk,), tag=f"l1f_smm{tc}")

            def emit_ev(e, tc=tc, pbank=pbank):
                return e.scalar_tensor_tensor(s64[:, tc, :], pbank[:, 0:64], 1.0, binr[:, :],
                                              ALU.mult, ALU.add)
            self.v(emit_ev, reads=(pk,) + binrk, writes=s64k, tag=f"l1f_sev{tc}")

        attn, attnk = A.view(0, [128, ND, 512], F32)
        vpad, vpadk = A.view(16384, [128, 16, 65], F32)
        khb, khk = A.view(20608, [128, T], F32)
        qhb, qhk = A.view(28800, [128, 512], F32)
        expt = [A.view(30848, [128, 512], F32), A.view(32896, [128, 512], F32)]
        rrow, rrowk = A.view(34944, [128, 512], F32)
        rb, rbk = A.view(36992, [128, 512], F32)
        avtmp, avtmpk = A.view(39040, [128, 512], F32)
        wo, wok = A.view(41088, [128, ND, D], F32)
        stg, stgk = A.view(73856, [128, D], F32)
        bo1r, bo1k = self.rep_at(77952, "bo1", D, tag="repbo1")
        bk.dma("sync", [(wo[:, :, 0:512], self.din["Wo1T"].rearrange("(c p) e -> p c e", p=128)[:, :, 0:512]),
                        (wo[:, :, 512:1024], self.din["Wo1T"].rearrange("(c p) e -> p c e", p=128)[:, :, 512:1024])],
               writes=wok, tag="l1wo_w")
        self.v(lambda e: e.memset(vpad[:, :, 64:65], 1.0), writes=vpadk, tag="l1_vones")

        vre = vD.rearrange("(kc p) (h dv) -> p kc h dv", p=128, dv=64)
        for qc in range(4):
            for h in range(16):
                pairs = [(khb[0:64, :], kD[h * 64:(h + 1) * 64, :]),
                         (qhb[0:64, :], qD[h * 64:(h + 1) * 64, qc * 512:qc * 512 + 512])]
                if qc == 0 or True:
                    pairs.append((vpad[:, :, 0:64], vre[:, :, h, :]))
                rd = tuple(f"l1qD:{ec}:{tq}" for ec in range(ND) for tq in range(4)) + \
                     tuple(f"l1kD:{ec}:{tq}" for ec in range(ND) for tq in range(4)) + \
                     tuple(f"l1vD:{tc}" for tc in range(NT))
                bk.dma("sync", pairs, reads=rd, writes=khk + qhk + vpadk,
                       tag=f"l1a_ld{qc}_{h}")
                for kc in range(16):
                    sb_ = self.pb[2 + kc % 2]
                    spk = f"pb{2 + kc % 2}"

                    def emit_s(e, kc=kc, sb_=sb_):
                        return e.matmul(sb_[:, :], khb[0:64, kc * 128:kc * 128 + 128],
                                        qhb[0:64, :], start=True, stop=True)
                    self.mm(emit_s, reads=khk + qhk, writes=(spk,), tag=f"l1a_s{qc}_{h}_{kc}")
                    ex, exk = expt[kc % 2]

                    def emit_e(e, sb_=sb_, ex=ex):
                        return e.activation(ex[:, :], sb_[:, :], AF.Exp, scale=0.125)
                    self.s(emit_e, reads=(spk,), writes=exk, tag=f"l1a_e{qc}_{h}_{kc}")

                    def emit_av(e, kc=kc, ex=ex):
                        return e.matmul(self.pb[4][0:65, :], vpad[:, kc, :], ex[:, :],
                                        start=(kc == 0), stop=(kc == 15))
                    self.mm(emit_av, reads=expt[kc % 2][1] + vpadk, writes=("pb4",),
                            tag=f"l1a_av{qc}_{h}_{kc}")
                def emit_cp(e):
                    return e.activation(rrow[64:65, :], self.pb[4][64:65, :], AF.Copy)
                self.s(emit_cp, reads=("pb4",), writes=rrowk, tag=f"l1a_cp{qc}_{h}")
                bk.dma("gpsimd", [(rrow[0:1, :], rrow[64:65, :])], reads=rrowk,
                       writes=rrowk, tag=f"l1a_mvr{qc}_{h}")

                def emit_r(e):
                    return e.reciprocal(rrow[0:1, :], rrow[0:1, :])
                self.v(emit_r, reads=rrowk, writes=rrowk, tag=f"l1a_r{qc}_{h}")

                def emit_bc(e):
                    return e.matmul(self.pb[5][0:64, :], self.ones1[0:1, 0:64], rrow[0:1, :],
                                    start=True, stop=True)
                self.mm(emit_bc, reads=rrowk + ("consts",), writes=("pb5",), tag=f"l1a_bc{qc}_{h}")

                def emit_rbe(e):
                    return e.activation(rb[0:64, :], self.pb[5][0:64, :], AF.Copy)
                self.s(emit_rbe, reads=("pb5",), writes=rbk, tag=f"l1a_rbe{qc}_{h}")
                if h % 2 == 0:
                    def emit_at(e, h=h):
                        return e.tensor_tensor(attn[0:64, h // 2, :], self.pb[4][0:64, :],
                                               rb[0:64, :], ALU.mult)
                    self.v(emit_at, reads=("pb4",) + rbk, writes=attnk, tag=f"l1a_at{qc}_{h}")
                else:
                    def emit_at(e):
                        return e.tensor_tensor(avtmp[0:64, :], self.pb[4][0:64, :],
                                               rb[0:64, :], ALU.mult)
                    self.v(emit_at, reads=("pb4",) + rbk, writes=avtmpk, tag=f"l1a_at{qc}_{h}")
                    bk.dma("gpsimd", [(attn[64:128, h // 2, :], avtmp[0:64, :])],
                           reads=avtmpk, writes=attnk, tag=f"l1a_mv{qc}_{h}")
            for ts_ in range(4):
                for eo in range(2):
                    pbank = self.pb[eo]
                    pk = f"pb{eo}"

                    def emit_mm(e, ts_=ts_, eo=eo, pbank=pbank):
                        return [e.matmul(pbank[:, :], attn[:, c, ts_ * 128:ts_ * 128 + 128],
                                         wo[:, c, eo * 512:eo * 512 + 512],
                                         start=(c == 0), stop=(c == ND - 1)) for c in range(ND)]
                    self.mm(emit_mm, reads=attnk + wok, writes=(pk,), tag=f"l1wo_mm{qc}_{ts_}_{eo}")

                    def emit_ev(e, eo=eo, pbank=pbank):
                        return e.scalar_tensor_tensor(stg[:, eo * 512:eo * 512 + 512], pbank[:, :],
                                                      1.0, bo1r[:, eo * 512:eo * 512 + 512],
                                                      ALU.mult, ALU.add)
                    self.v(emit_ev, reads=(pk,) + bo1k, writes=stgk, tag=f"l1wo_ev{qc}_{ts_}_{eo}")
                tc = qc * 4 + ts_
                bk.dma("gpsimd", [(sr[tc, :, :], stg[:, :])], reads=stgk,
                       writes=(f"sD:{tc}",), tag=f"l1wo_st{tc}")

        wout, woutk = A.view(0, [128, D], F32)
        w1r, w1rk = A.view(4096, [128, 1024], F32)
        b1r, b1rk = A.view(8192, [128, 1024], F32)
        w2r, w2rk = A.view(12288, [128, 1024], F32)
        b2r, b2rk = A.view(16384, [128, 64], F32)
        boutr, boutk = A.view(20480, [128, D], F32)
        ht, htk = A.view(24576, [128, 1024], F32)
        tmp, tmpk = A.view(28672, [128, 1024], F32)
        g64, g64k = A.view(32768, [128, 64], F32)
        g64f, g64fk = A.view(33280, [128, 128], F32)
        gst, gstk = A.view(36864, [128, D], F32)
        sst, sstk = A.view(40960, [128, D], F32)
        bk.dma("sync", [(wout[0:64, :], self.din["f1Wout"])], writes=woutk, tag="l1f_wout")
        bk.dma("gpsimd",
               [(w1r[:, :], self.din["f1W1"].partition_broadcast(128).squeeze(1)),
                (b1r[:, :], self.din["f1b1"].partition_broadcast(128).squeeze(1)),
                (w2r[:, :], self.din["f1W2"].partition_broadcast(128).squeeze(1)),
                (b2r[:, :], self.din["f1b2"].partition_broadcast(128).squeeze(1)),
                (boutr[:, :], self.din["f1bout"].partition_broadcast(128).squeeze(1))],
               writes=w1rk + b1rk + w2rk + b2rk + boutk, tag="l1f_reps")
        for tc in range(NT):
            def emit_h(e, tc=tc):
                sb3 = s64[:, tc, :].unsqueeze(-1).broadcast_to([128, 64, 16])
                e.tensor_tensor(ht[:, :].rearrange("p (m i) -> p m i", i=16), sb3,
                                w1r[:, :].rearrange("p (m i) -> p m i", i=16), ALU.mult)
                e.tensor_tensor(ht[:, :], ht[:, :], b1r[:, :], ALU.add)
                e.tensor_relu(tmp[:, :], ht[:, :])
                e.tensor_tensor(tmp[:, :], tmp[:, :], w2r[:, :], ALU.mult)
                e.tensor_reduce(g64[:, :].unsqueeze(-1), tmp[:, :].rearrange("p (m i) -> p m i", i=16),
                                AX, ALU.add)
                return e.tensor_tensor(g64[:, :], g64[:, :], b2r[:, :], ALU.add)
            self.v(emit_h, reads=s64k + w1rk + b1rk + w2rk + b2rk,
                   writes=htk + tmpk + g64k, tag=f"l1f_h{tc}")

            def emit_tr(e):
                return e.transpose(self.pb[6][0:64, 0:128], g64[:, :], self.ident[:, :])
            self.mm(emit_tr, reads=g64k + ("consts",), writes=("pb6",), tag=f"l1f_tr{tc}")

            def emit_tev(e):
                return e.activation(g64f[0:64, :], self.pb[6][0:64, 0:128], AF.Copy)
            self.s(emit_tev, reads=("pb6",), writes=g64fk, tag=f"l1f_tev{tc}")
            for eo in range(2):
                pbank = self.pb[eo]
                pk = f"pb{eo}"

                def emit_om(e, eo=eo, pbank=pbank):
                    return e.matmul(pbank[:, :], g64f[0:64, :], wout[0:64, eo * 512:eo * 512 + 512],
                                    start=True, stop=True)
                self.mm(emit_om, reads=g64fk + woutk, writes=(pk,), tag=f"l1f_om{tc}_{eo}")

                def emit_oev(e, eo=eo, pbank=pbank):
                    return e.scalar_tensor_tensor(gst[:, eo * 512:eo * 512 + 512], pbank[:, :],
                                                  1.0, boutr[:, eo * 512:eo * 512 + 512],
                                                  ALU.mult, ALU.add)
                self.v(emit_oev, reads=(pk,) + boutk, writes=gstk, tag=f"l1f_oev{tc}_{eo}")
            self.s(lambda e: e.activation(gst[:, :], gst[:, :], AF.Silu),
                   reads=gstk, writes=gstk, tag=f"l1f_silu{tc}")
            bk.dma("sync", [(sst[:, :], sr[tc, :, :])], reads=(f"sD:{tc}",), writes=sstk,
                   tag=f"l1f_sld{tc}")

            def emit_comb(e, tc=tc):
                e.tensor_tensor(sst[:, :], sst[:, :], gst[:, :], ALU.mult)
                return e.tensor_tensor(self.xTM[:, tc, :], self.xTM[:, tc, :], sst[:, :], ALU.add)
            self.v(emit_comb, reads=gstk + sstk + (f"xTM:{tc}",), writes=(f"xTM:{tc}",) + sstk,
                   tag=f"l1f_comb{tc}")



def prep_inputs(inputs):
    C = np.ascontiguousarray
    f = np.float32
    d = {}
    d["pos_phases"] = C(np.asarray(inputs["L0_attn_pos_phases"]), dtype=f)
    d["Wpv"] = C(np.asarray(inputs["L0_attn_Wpv"]), dtype=f)
    d["Wk"] = C(np.asarray(inputs["L0_attn_Wk"]), dtype=f)
    d["Wkv"] = C(np.asarray(inputs["L0_attn_Wkv"]), dtype=f)
    d["Wo0"] = C(np.asarray(inputs["L0_attn_Wo"]), dtype=f)
    d["Wg1"] = C(np.asarray(inputs["L0_attn_Wg1"]), dtype=f)
    d["Wb1"] = C(np.asarray(inputs["L0_attn_Wb1"]), dtype=f)
    d["Wg2T"] = C(np.asarray(inputs["L0_attn_Wg2"]).reshape(1, -1), dtype=f)
    d["Wb2T"] = C(np.asarray(inputs["L0_attn_Wb2"]).T, dtype=f)
    for nm, key in (("bpv", "L0_attn_bpv"), ("bk0", "L0_attn_bk"), ("bkv", "L0_attn_bkv"),
                    ("bg1", "L0_attn_bg1"), ("bg2", "L0_attn_bg2"), ("bb1", "L0_attn_bb1"),
                    ("bb2", "L0_attn_bb2"), ("ln_g", "L0_attn_ln_g"), ("ln_b", "L0_attn_ln_b"),
                    ("bo0", "L0_attn_bo"), ("f0b1", "L0_ffn_b1"), ("f0b2", "L0_ffn_b2")):
        d[nm] = C(np.asarray(inputs[key]).reshape(1, -1), dtype=f)
    d["f0w1T"] = C(np.asarray(inputs["L0_ffn_w1"]).T, dtype=f)
    d["f0w2T"] = C(np.asarray(inputs["L0_ffn_w2"]).T, dtype=f)
    d["Wq1T"] = C(np.asarray(inputs["L1_attn_Wq"]).T, dtype=f)
    d["Wk1T"] = C(np.asarray(inputs["L1_attn_Wk"]).T, dtype=f)
    d["Wv1T"] = C(np.asarray(inputs["L1_attn_Wv"]).T, dtype=f)
    d["Wo1T"] = C(np.asarray(inputs["L1_attn_Wo"]).T, dtype=f)
    d["bq1C"] = C(np.asarray(inputs["L1_attn_bq"]).reshape(8, 128).T, dtype=f)
    d["bk1C"] = C(np.asarray(inputs["L1_attn_bk"]).reshape(8, 128).T, dtype=f)
    d["bv1R"] = C(np.asarray(inputs["L1_attn_bv"]).reshape(1, -1), dtype=f)
    d["bo1"] = C(np.asarray(inputs["L1_attn_bo"]).reshape(1, -1), dtype=f)
    d["f1Win"] = C(np.asarray(inputs["L1_ffn_Win"]), dtype=f)
    d["f1Wout"] = C(np.asarray(inputs["L1_ffn_Wout"]), dtype=f)
    d["f1bin"] = C(np.asarray(inputs["L1_ffn_bin_"]).reshape(1, -1), dtype=f)
    d["f1W1"] = C(np.asarray(inputs["L1_ffn_W1"]).reshape(1, -1), dtype=f)
    d["f1b1"] = C(np.asarray(inputs["L1_ffn_b1"]).reshape(1, -1), dtype=f)
    d["f1W2"] = C(np.asarray(inputs["L1_ffn_W2"]).reshape(1, -1), dtype=f)
    d["f1b2"] = C(np.asarray(inputs["L1_ffn_b2"]).reshape(1, -1), dtype=f)
    d["f1bout"] = C(np.asarray(inputs["L1_ffn_bout"]).reshape(1, -1), dtype=f)
    d["W2inT"] = C(np.asarray(inputs["L2_attn_W_in"]).T, dtype=f)
    d["convw"] = C(np.asarray(inputs["L2_attn_conv_w"]).reshape(16, 128, 4).transpose(1, 0, 2), dtype=f)
    d["convb"] = C(np.asarray(inputs["L2_attn_conv_b"]).reshape(16, 128).T, dtype=f)
    d["W2xT"] = C(np.asarray(inputs["L2_attn_W_x"]).T, dtype=f)
    d["dtlnw"] = C(np.asarray(inputs["L2_attn_dt_ln_w"]).reshape(1, -1), dtype=f)
    d["Blnw"] = C(np.asarray(inputs["L2_attn_B_ln_w"]).reshape(1, -1), dtype=f)
    d["Clnw"] = C(np.asarray(inputs["L2_attn_C_ln_w"]).reshape(1, -1), dtype=f)
    d["W2dtT"] = C(np.asarray(inputs["L2_attn_W_dt"]).T, dtype=f)
    d["bdt"] = C(np.asarray(inputs["L2_attn_b_dt"]).reshape(16, 128).T, dtype=f)
    d["AlogC"] = C(np.asarray(inputs["L2_attn_A_log"]).reshape(16, 128, 16).transpose(1, 0, 2), dtype=f)
    d["DparC"] = C(np.asarray(inputs["L2_attn_D_param"]).reshape(16, 128).T, dtype=f)
    d["ln1wC"] = C(np.asarray(inputs["L2_attn_ln1_w"]).reshape(16, 128).T, dtype=f)
    d["ln2wC"] = C(np.asarray(inputs["L2_attn_ln2_w"]).reshape(16, 128).T, dtype=f)
    d["W2outT"] = C(np.asarray(inputs["L2_attn_W_out"]).T, dtype=f)
    d["w2gate"] = C(np.asarray(inputs["L2_ffn_w_gate"]), dtype=f)
    d["w2up"] = C(np.asarray(inputs["L2_ffn_w_up"]), dtype=f)
    d["w2down"] = C(np.asarray(inputs["L2_ffn_w_down"]), dtype=f)
    cm = np.zeros((4, 128, 512), dtype=f)
    for s_ in range(4):
        for pq in range(128):
            cm[s_, pq, :] = (np.arange(512) <= s_ * 128 + pq)
    d["cmask"] = cm
    d["w3kvd"] = C(np.asarray(inputs["L3_attn_w_kv_down"]), dtype=f)
    d["w3kvu"] = C(np.asarray(inputs["L3_attn_w_kv_up"]), dtype=f)
    d["w3q"] = C(np.asarray(inputs["L3_attn_w_q"]), dtype=f)
    d["w3out"] = C(np.asarray(inputs["L3_attn_w_out"]), dtype=f)
    d["w3g"] = C(np.asarray(inputs["L3_ffn_Wg"]), dtype=f)
    d["w3W1"] = C(np.asarray(inputs["L3_ffn_W1"]), dtype=f)
    d["w3W2"] = C(np.asarray(inputs["L3_ffn_W2"]), dtype=f)
    d["w3b1C"] = C(np.asarray(inputs["L3_ffn_b1"]).reshape(8, 16, 128).transpose(2, 0, 1), dtype=f)
    d["w3b2"] = C(np.asarray(inputs["L3_ffn_b2"]), dtype=f)
    cmT = np.zeros((4, 128, 512), dtype=f)
    for s_ in range(4):
        for pk in range(128):
            cmT[s_, pk, :] = (np.arange(512) >= s_ * 128 + pk)
    d["cmaskT"] = cmT
    d["ident"] = np.eye(128, dtype=f)
    d["tri"] = np.triu(np.ones((128, 128), dtype=f))
    d["onesrow"] = np.ones((1, 128), dtype=f)
    return d


def build_program(n_layers=None, taps=()):
    global DEBUG_TAPS
    DEBUG_TAPS = tuple(taps)
    nl = N_LAYERS if n_layers is None else n_layers
    nc = bass.Bass("TRN2", target_bir_lowering=False)
    bk = BK(nc)
    M = Model(bk)
    M.declare_inputs()
    M.build(nl)
    bk.finalize()
    return nc, M, bk


def kernel(**inputs):
    nc, M, bk = build_program()
    shared = prep_inputs(inputs)
    x = np.ascontiguousarray(np.asarray(inputs["L0_attn_x"]), dtype=np.float32)
    in_maps = []
    for c in range(8):
        m = dict(shared)
        m["x"] = np.ascontiguousarray(x[c])
        in_maps.append(m)
    res = run_bass_kernel_spmd(nc, in_maps, core_ids=list(range(8)))
    out = np.stack([res.results[c]["out"] for c in range(8)], axis=0)
    return out
